# version 05
# baseline (speedup 1.0000x reference)
import contextlib
import numpy as np
import ml_dtypes
import concourse.bass as bass
import concourse.mybir as mybir
from concourse.bass_utils import run_bass_kernel_spmd

F32 = mybir.dt.float32
BF16 = mybir.dt.bfloat16
AF = mybir.ActivationFunctionType
ALU = mybir.AluOpType
AX = mybir.AxisListType
BF = ml_dtypes.bfloat16

NCORES = 8
D = 2048
TPC = 1024
SEQ = 4096
NKD = D // 128
BIG = 30000.0



class Prog:

    ENGS = ("sy", "te", "sc", "ve", "gp")

    def __init__(self, name="prog"):
        self.nc = bass.Bass("TRN2", target_bir_lowering=False, name=name)
        self.q = {e: [] for e in self.ENGS}
        self.cnt = {}
        self._sems = {}
        self._stack = contextlib.ExitStack()
        self._done = False
        self._dma_sems = set()

    def sem(self, name):
        if name not in self._sems:
            self._sems[name] = self._stack.enter_context(self.nc.semaphore(name))
            self.cnt[name] = 0
        return self._sems[name]

    def wait(self, e, sname, val=None):
        s = self.sem(sname)
        v = self.cnt[sname] if val is None else val
        self.q[e].append(lambda eng, s=s, v=v: eng.wait_ge(s, v))
        return v

    def dma(self, e, out, in_, sname, **kw):
        s = self.sem(sname)
        self._dma_sems.add(sname)
        self.cnt[sname] += 16
        self.q[e].append(
            lambda eng, s=s, out=out, in_=in_, kw=kw:
            eng.dma_start(out=out, in_=in_, **kw).then_inc(s, 16))
        return self.cnt[sname]

    def op(self, e, fn, inc=None):
        if inc is not None:
            incs = inc if isinstance(inc, list) else [inc]
            hs = []
            tgt = None
            tgt0 = None
            for j, (sname, n) in enumerate(incs):
                s = self.sem(sname)
                self.cnt[sname] += n
                tgt = self.cnt[sname]
                if j == 0:
                    tgt0 = self.cnt[sname]
                hs.append((s, n))

            def run(eng, fn=fn, hs=hs, tgt0=tgt0):
                fn(eng).then_inc(*hs[0])
                if len(hs) > 1:
                    eng.wait_ge(hs[0][0], tgt0)
                    for s, n in hs[1:]:
                        eng.sem_inc(s, n)
            self.q[e].append(run)
            return tgt
        self.q[e].append(lambda eng, fn=fn: fn(eng))
        return None

    def rop(self, e, fn):
        self.q[e].append(lambda eng, fn=fn: fn(eng))

    def chain(self, e):
        s = self.sem("chain_" + e)
        self.cnt["chain_" + e] += 1
        v = self.cnt["chain_" + e]
        prev = self.q[e].pop()

        def wrapped(eng, prev=prev, s=s, v=v):
            prev(eng).then_inc(s, 1)
            eng.wait_ge(s, v)
        self.q[e].append(wrapped)

    def din(self, name, shape, dtype):
        return self.nc.dram_tensor(name, list(shape), dtype, kind="ExternalInput").ap()

    def dout(self, name, shape, dtype):
        return self.nc.dram_tensor(name, list(shape), dtype, kind="ExternalOutput").ap()

    def dtmp(self, name, shape, dtype):
        return self.nc.dram_tensor(name, list(shape), dtype, kind="Internal").ap()

    def sb(self, name, shape, dtype):
        return self.nc.alloc_sbuf_tensor(name, list(shape), dtype)

    def ps(self, name, shape, dtype=F32):
        return self.nc.alloc_psum_tensor(name, list(shape), dtype)

    def inline(self, arr, name):
        return self.nc.inline_tensor(arr, name=name).ap()

    def consts(self, values):
        vals = sorted(set(float(v) for v in values))
        arr = np.tile(np.array(vals, np.float32)[None, :], (128, 1))
        dr = self.inline(arr, "constpool")
        tile = self.sb("constpool_sb", [128, len(vals)], F32)
        self.dma("sy", tile[:], dr[:], "ldc")
        for i, v in enumerate(vals):
            self.nc.const_aps.register(F32, v, tile[:, i:i + 1])

    def instream(self, name, nslots, consumers=1):
        return InStream(self, name, nslots, consumers)

    def outstream(self, name, nslots):
        return OutStream(self, name, nslots)

    def finish(self, e="gp"):
        for sname in sorted(self._dma_sems):
            self.wait(e, sname, self.cnt[sname])

    def finalize(self):
        assert not self._done
        self._done = True
        with self.nc.Block() as block:
            @block.sync
            def _(eng):
                for fn in self.q["sy"]:
                    fn(eng)

            @block.tensor
            def _(eng):
                for fn in self.q["te"]:
                    fn(eng)

            @block.scalar
            def _(eng):
                for fn in self.q["sc"]:
                    fn(eng)

            @block.vector
            def _(eng):
                for fn in self.q["ve"]:
                    fn(eng)

            @block.gpsimd
            def _(eng):
                for fn in self.q["gp"]:
                    fn(eng)
        self._stack.close()
        return self.nc


class InStream:

    def __init__(self, p, name, nslots, consumers=1):
        self.p, self.name, self.n, self.k = p, name, nslots, consumers
        self.i = 0

    def load(self, e, pairs):
        p, s = self.p, self.i % self.n
        if self.i >= self.n:
            p.wait(e, f"{self.name}f{s}", self.k * (self.i // self.n))
        if not isinstance(pairs, list):
            pairs = [pairs]
        for out, in_ in pairs:
            p.dma(e, out, in_, f"{self.name}l{s}")
        tok = (s, p.cnt[f"{self.name}l{s}"])
        self.i += 1
        return tok

    def wait(self, e, tok):
        self.p.wait(e, f"{self.name}l{tok[0]}", tok[1])

    def free(self, tok):
        return (f"{self.name}f{tok[0]}", 1)


class OutStream:

    def __init__(self, p, name, nslots):
        self.p, self.name, self.n = p, name, nslots
        self.i = 0

    def ready(self, e):
        s = self.i % self.n
        if self.i >= self.n:
            self.p.wait(e, f"{self.name}s{s}", 16 * (self.i // self.n))
        return s

    def store(self, e, out, in_, after=None):
        s = self.i % self.n
        if after is not None:
            self.p.wait(e, after[0], after[1])
        self.p.dma(e, out, in_, f"{self.name}s{s}")
        self.i += 1



def lineblock(W):
    K, M = W.shape
    assert K % 128 == 0 and M % 128 == 0
    A = W.reshape(K // 128, 128, M // 128, 128)
    return np.ascontiguousarray(A.transpose(2, 1, 0, 3).reshape(M // 128, 128, K))


def to_bf(x):
    return np.asarray(x, dtype=np.float32).astype(BF)


def f32c(x):
    return np.ascontiguousarray(np.asarray(x, dtype=np.float32))



def emit_ln_stats(p, tag, x_src, NT, nch, bufs, ps_sum, ps_sq, row4, Rb, Mb,
                  oc, orw, eps):
    xbuf, sqt = bufs
    st = p.instream("lx" + tag, 4)
    pex = "lnpe_" + tag
    sq = "lnsq_" + tag
    for kc in range(nch):
        sl = kc % 4
        tok = st.load("sy", (xbuf[:, sl * NT:(sl + 1) * NT], x_src(kc)))
        st.wait("sc", tok)
        if kc >= 4:
            p.wait("sc", pex, 2 * (kc - 4) + 2)
        p.op("sc", lambda eng, sl=sl: eng.activation(
            sqt[:, sl * NT:(sl + 1) * NT], xbuf[:, sl * NT:(sl + 1) * NT],
            AF.Square), inc=(sq, 1))
        p.wait("te", sq, p.cnt[sq])
        p.op("te", lambda eng, sl=sl, kc=kc: eng.matmul(
            ps_sq[0:1, 0:NT], oc[:, 0:1], sqt[:, sl * NT:(sl + 1) * NT],
            start=(kc == 0), stop=(kc == nch - 1)), inc=(pex, 1))
        p.op("te", lambda eng, sl=sl, kc=kc: eng.matmul(
            ps_sum[0:1, 0:NT], oc[:, 0:1], xbuf[:, sl * NT:(sl + 1) * NT],
            start=(kc == 0), stop=(kc == nch - 1)),
            inc=[(pex, 1), st.free(tok)])
    pev = p.cnt[pex]
    s_ = row4[0:1, 0:NT]; q_ = row4[0:1, NT:2 * NT]
    r_ = row4[0:1, 2 * NT:3 * NT]; m_ = row4[0:1, 3 * NT:4 * NT]
    p.wait("ve", pex, pev)
    p.rop("ve", lambda eng: eng.tensor_scalar(s_, ps_sum[0:1, 0:NT], 1.0 / D, None, ALU.mult))
    p.rop("ve", lambda eng: eng.tensor_scalar(q_, ps_sq[0:1, 0:NT], 1.0 / D, None, ALU.mult))
    p.chain("ve")
    p.rop("ve", lambda eng: eng.tensor_tensor(r_, s_, s_, ALU.mult))
    p.chain("ve")
    p.rop("ve", lambda eng: eng.tensor_sub(r_, q_, r_))
    p.chain("ve")
    p.wait("sc", "chain_ve", p.cnt["chain_ve"])
    p.rop("sc", lambda eng: eng.activation(r_, r_, AF.Ln, bias=float(eps)))
    p.chain("sc")
    p.rop("sc", lambda eng: eng.activation(r_, r_, AF.Exp, scale=-0.5))
    p.chain("sc")
    p.wait("ve", "chain_sc", p.cnt["chain_sc"])
    p.rop("ve", lambda eng: eng.tensor_tensor(m_, s_, r_, ALU.mult))
    p.chain("ve")
    p.op("ve", lambda eng: eng.tensor_scalar(m_, m_, -1.0, None, ALU.mult),
         inc=("lnrow_" + tag, 1))
    p.wait("te", "lnrow_" + tag, p.cnt["lnrow_" + tag])
    p.op("te", lambda eng: eng.matmul(ps_sum[:, 0:NT], orw[0:1, :], r_,
                                      start=True, stop=True))
    p.op("te", lambda eng: eng.matmul(ps_sq[:, 0:NT], orw[0:1, :], m_,
                                      start=True, stop=True), inc=(pex, 1))
    p.wait("ve", pex, p.cnt[pex])
    p.rop("ve", lambda eng: eng.tensor_copy(Rb[:, 0:NT], ps_sum[:, 0:NT]))
    p.op("ve", lambda eng: eng.tensor_copy(Mb[:, 0:NT], ps_sq[:, 0:NT]),
         inc=("lnbc_" + tag, 1))
    p.wait("ve", "lnbc_" + tag, p.cnt["lnbc_" + tag])
    return p.cnt["lnbc_" + tag]


def emit_ln_apply(p, tag, x_src, dst_aps, NT, nch, xbuf, Rb, Mb,
                  gain=None, bias=None, app=None):
    st = p.instream("la" + tag, 4)
    for kc in range(nch):
        sl = kc % 4
        tok = st.load("sy", (xbuf[:, sl * NT:(sl + 1) * NT], x_src(kc)))
        st.wait("ve", tok)
        dst = dst_aps[kc]
        x_ = xbuf[:, sl * NT:(sl + 1) * NT]
        incs = [st.free(tok)] + ([(app, 1)] if app else [])
        p.rop("ve", lambda eng, dst=dst, x_=x_: eng.tensor_tensor(
            dst, x_, Rb[:, 0:NT], ALU.mult))
        p.chain("ve")
        if gain is None:
            p.op("ve", lambda eng, dst=dst: eng.tensor_tensor(
                dst, dst, Mb[:, 0:NT], ALU.add), inc=incs)
        else:
            p.rop("ve", lambda eng, dst=dst: eng.tensor_tensor(
                dst, dst, Mb[:, 0:NT], ALU.add))
            p.chain("ve")
            g = gain[kc]; b_ = bias[kc]
            p.op("ve", lambda eng, dst=dst, g=g, b_=b_: eng.tensor_scalar(
                dst, dst, g, b_, ALU.mult, ALU.add), inc=incs)
    return p.cnt[app] if app else None


class ProjCtx:

    def __init__(self, p, wtiles, pb4, tag="pj"):
        self.p = p
        self.wt = wtiles
        self.pb4 = pb4
        self.st = p.instream(tag + "w", len(wtiles))
        self.pe = tag + "pe"
        self.ev = tag + "ev"
        self.gi = 0


def emit_proj_fm(p, ctx, Wlb, nK, nM, rhs_fn, evac_fn, TT=TPC):
    nhalf = TT // 512
    nw = len(ctx.wt)
    for m in range(nM):
        b = ctx.st.i % nw
        tok = ctx.st.load("sy", (ctx.wt[b][:, 0:nK * 128], Wlb[m]))
        ctx.st.wait("te", tok)
        for half in range(nhalf):
            hs = half * 512
            ps_ = ctx.pb4[ctx.gi % 4]
            if ctx.gi >= 4:
                p.wait("te", ctx.ev, ctx.gi - 3)
            for k in range(nK):
                incs = []
                if k == nK - 1:
                    incs.append((ctx.pe, 1))
                    if half == nhalf - 1:
                        incs.append(ctx.st.free(tok))
                p.op("te", lambda eng, k=k, b=b, hs=hs, ps_=ps_: eng.matmul(
                    ps_[:], ctx.wt[b][:, k * 128:(k + 1) * 128], rhs_fn(k, hs),
                    start=(k == 0), stop=(k == nK - 1)),
                    inc=incs if incs else None)
            ctx.gi += 1
            evac_fn(m, half, ps_, p.cnt[ctx.pe])


def load_bias_cols(p, tag, dram_ap, nM):
    t = p.sb("bias_" + tag, [128, nM], F32)
    for m in range(nM):
        p.dma("sy", t[:, m:m + 1], dram_ap[m * 128:(m + 1) * 128, :], "ldc")
    return t



def build_ffn_prog():
    p = Prog("ffn")
    xT = p.din("xT", [D, TPC], F32)
    WG = p.din("WG", [64, 128, D], BF16)
    WU = p.din("WU", [64, 128, D], BF16)
    WD = p.din("WD", [16, 128, 8192], BF16)
    xoT = p.dout("xoT", [D, TPC], F32)

    oc = p.sb("oc", [128, 1], F32)
    orw = p.sb("orw", [1, 128], F32)
    p.dma("sy", oc[:], p.inline(np.ones((128, 1), np.float32), "onesc")[:], "ldc")
    p.dma("sy", orw[:], p.inline(np.ones((1, 128), np.float32), "onesr")[:], "ldc")
    p.consts([1e-5])
    for e in ("te", "sc", "ve"):
        p.wait(e, "ldc", p.cnt["ldc"])

    xbuf = p.sb("xbuf", [128, 4 * 512], F32)
    sqt = p.sb("sqt", [128, 4 * 512], F32)
    abuf = p.sb("abuf", [128, 4 * 512], F32)
    rbuf = p.sb("rbuf", [128, 2 * 512], F32)
    row4 = p.sb("row4", [1, 4 * 512], F32)
    Rb = p.sb("Rb", [128, 512], F32)
    Mb = p.sb("Mb", [128, 512], F32)
    xln = p.sb("xln", [128, 16 * 512], BF16)
    wg_sb = [p.sb(f"wg{i}", [128, D], BF16) for i in range(2)]
    wu_sb = [p.sb(f"wu{i}", [128, D], BF16) for i in range(2)]
    gtmp = p.sb("gtmp", [128, 2 * 512], BF16)
    utmp = p.sb("utmp", [128, 2 * 512], BF16)
    hidh = p.sb("hidh", [128, 64 * 512], BF16)
    wd_sb = [p.sb(f"wd{i}", [128, 8192], BF16) for i in range(2)]
    yb = p.sb("yb", [128, 2 * 512], F32)

    pb = [p.ps(f"pb{i}", [128, 512]) for i in range(8)]

    wgu = p.instream("wgu", 2)
    wdn = p.instream("wdn", 2)
    rsx = p.instream("rsx", 2)
    yo = p.outstream("yo", 2)

    npair = 0
    scev_at = {}
    gev_at = {}

    for half in range(2):
        hs = half * 512
        tag = f"f{half}"
        if half == 1:
            p.wait("sy", "lnpe_f0", p.cnt["lnpe_f0"])
            p.wait("sy", "dn_ev", p.cnt["dn_ev"])
            p.wait("te", "dn_ev", p.cnt["dn_ev"])
            p.wait("sc", "lnpe_f0", p.cnt["lnpe_f0"])
        bcv = emit_ln_stats(p, tag, lambda kc, hs=hs: xT[kc * 128:(kc + 1) * 128, hs:hs + 512],
                            512, NKD, (xbuf, sqt), pb[0], pb[1], row4, Rb, Mb,
                            oc, orw, 1e-5)
        emit_ln_apply(p, tag, lambda kc, hs=hs: xT[kc * 128:(kc + 1) * 128, hs:hs + 512],
                      [xln[:, kc * 512:(kc + 1) * 512] for kc in range(NKD)],
                      512, NKD, abuf, Rb, Mb, app="xlnap")
        p.wait("te", "xlnap", p.cnt["xlnap"])

        for m in range(64):
            tokw = wgu.load("sy", [(wg_sb[m % 2][:], WG[m]), (wu_sb[m % 2][:], WU[m])])
            b = m % 2
            pg = pb[4 + 2 * (m % 2)]
            pu = pb[5 + 2 * (m % 2)]
            wgu.wait("te", tokw)
            if npair >= 2:
                p.wait("te", "scev", scev_at[npair - 2])
            for k in range(NKD):
                p.op("te", lambda eng, k=k, b=b, pg=pg: eng.matmul(
                    pg[:], wg_sb[b][:, k * 128:(k + 1) * 128],
                    xln[:, k * 512:(k + 1) * 512],
                    start=(k == 0), stop=(k == NKD - 1)))
            for k in range(NKD):
                incs = []
                if k == NKD - 1:
                    incs = [("pe_gu", 1), wgu.free(tokw)]
                p.op("te", lambda eng, k=k, b=b, pu=pu: eng.matmul(
                    pu[:], wu_sb[b][:, k * 128:(k + 1) * 128],
                    xln[:, k * 512:(k + 1) * 512],
                    start=(k == 0), stop=(k == NKD - 1)),
                    inc=incs if incs else None)
            sl = npair % 2
            p.wait("sc", "pe_gu", p.cnt["pe_gu"])
            if npair >= 2:
                p.wait("sc", "gevac", gev_at[npair - 2])
            p.rop("sc", lambda eng, pg=pg, sl=sl: eng.activation(
                gtmp[:, sl * 512:(sl + 1) * 512], pg[:], AF.Silu))
            p.op("sc", lambda eng, pu=pu, sl=sl: eng.copy(
                utmp[:, sl * 512:(sl + 1) * 512], pu[:]), inc=("scev", 1))
            scev_at[npair] = p.cnt["scev"]
            p.wait("ve", "scev", p.cnt["scev"])
            p.op("ve", lambda eng, m=m, sl=sl: eng.tensor_tensor(
                hidh[:, m * 512:(m + 1) * 512],
                gtmp[:, sl * 512:(sl + 1) * 512],
                utmp[:, sl * 512:(sl + 1) * 512], ALU.mult), inc=("gevac", 1))
            gev_at[npair] = p.cnt["gevac"]
            npair += 1

        p.wait("te", "gevac", p.cnt["gevac"])
        for dci in range(NKD):
            gd = half * NKD + dci
            tokd = wdn.load("sy", (wd_sb[dci % 2][:], WD[dci]))
            pd = pb[gd % 4]
            wdn.wait("te", tokd)
            if gd >= 4:
                p.wait("te", "dn_ev", gd - 3)
            for k in range(64):
                incs = []
                if k == 63:
                    incs = [("dn_pe", 1), wdn.free(tokd)]
                p.op("te", lambda eng, k=k, dci=dci, pd=pd: eng.matmul(
                    pd[:], wd_sb[dci % 2][:, k * 128:(k + 1) * 128],
                    hidh[:, k * 512:(k + 1) * 512],
                    start=(k == 0), stop=(k == 63)),
                    inc=incs if incs else None)
            dnpev = p.cnt["dn_pe"]
            sl2 = rsx.i % 2
            tokr = rsx.load("sy", (rbuf[:, sl2 * 512:(sl2 + 1) * 512],
                                   xT[dci * 128:(dci + 1) * 128, hs:hs + 512]))
            rsx.wait("ve", tokr)
            p.wait("ve", "dn_pe", dnpev)
            so = yo.ready("ve")
            p.op("ve", lambda eng, pd=pd, sl2=sl2, so=so: eng.scalar_tensor_tensor(
                yb[:, so * 512:(so + 1) * 512], pd[:], 0.5,
                rbuf[:, sl2 * 512:(sl2 + 1) * 512], ALU.mult, ALU.add),
                inc=[("dn_ev", 1), rsx.free(tokr)])
            yo.store("gp", xoT[dci * 128:(dci + 1) * 128, hs:hs + 512],
                     yb[:, so * 512:(so + 1) * 512], after=("dn_ev", p.cnt["dn_ev"]))

    p.finish("gp")
    p.finalize()
    return p


def host_ffn_inputs(w_gate, w_up, w_down):
    return dict(
        WG=lineblock(to_bf(np.asarray(w_gate).T)),
        WU=lineblock(to_bf(np.asarray(w_up).T)),
        WD=lineblock(to_bf(np.asarray(w_down).T)),
    )


def build_mla_pre(stage=9):
    p = Prog("mlapre")
    xT = p.din("xT", [D, TPC], F32)
    WQD = p.din("WQD", [12, 128, D], BF16)
    BQD = p.din("BQD", [1536, 1], F32)
    WQUC = p.din("WQUC", [16, 128, 1536], BF16)
    BQUC = p.din("BQUC", [2048, 1], F32)
    WQUP = p.din("WQUP", [1536, 1024], BF16)
    BQUP = p.din("BQUP", [1, 1024], BF16)
    WKVD = p.din("WKVD", [4, 128, D], BF16)
    BKVD = p.din("BKVD", [512, 1], F32)
    WKUC = p.din("WKUC", [16, 128, 512], BF16)
    BKUC = p.din("BKUC", [2048, 1], F32)
    WKUP = p.din("WKUP", [512, 64], BF16)
    BKUP = p.din("BKUP", [1, 64], BF16)
    WVU = p.din("WVU", [512, 2048], BF16)
    BVU = p.din("BVU", [1, 2048], BF16)
    COS = p.din("COS", [TPC, 32], F32)
    SIN = p.din("SIN", [TPC, 32], F32)
    COSQ = p.din("COSQ", [TPC, 512], F32)
    SINQ = p.din("SINQ", [TPC, 512], F32)

    QC = p.dout("QC", [D, TPC], BF16)
    QR_TM = p.dout("QR_TM", [TPC, 1024], BF16)
    KC = p.dout("KC", [D, TPC], BF16)
    KR_TM = p.dout("KR_TM", [TPC, 64], BF16)
    VT = p.dout("VT", [TPC, D], BF16)
    QN2 = p.dout("QN2", [16, TPC], F32)
    QRN2T = p.dout("QRN2T", [TPC, 16], F32)
    KRN2T = p.dout("KRN2T", [TPC, 1], F32)
    KN2 = p.dout("KN2", [16, TPC], F32)

    oc = p.sb("oc", [128, 1], F32)
    orw = p.sb("orw", [1, 128], F32)
    orb = p.sb("orb", [1, 128], BF16)
    idn = p.sb("idn", [128, 128], BF16)
    p.dma("sy", oc[:], p.inline(np.ones((128, 1), np.float32), "onesc")[:], "ldc")
    p.dma("sy", orw[:], p.inline(np.ones((1, 128), np.float32), "onesr")[:], "ldc")
    p.dma("sy", orb[:], p.inline(np.ones((1, 128), BF), "onesrb")[:], "ldc")
    p.dma("sy", idn[:], p.inline(np.eye(128, dtype=BF), "idnb")[:], "ldc")
    p.consts([1e-5])
    bqd = load_bias_cols(p, "bqd", BQD, 12)
    bquc = load_bias_cols(p, "bquc", BQUC, 16)
    bkvd = load_bias_cols(p, "bkvd", BKVD, 4)
    bkuc = load_bias_cols(p, "bkuc", BKUC, 16)

    xbuf = p.sb("xbuf", [128, 4 * 512], F32)
    sqt = p.sb("sqt", [128, 4 * 512], F32)
    abuf = p.sb("abuf", [128, 4 * 512], F32)
    row4 = p.sb("row4", [1, 4 * 512], F32)
    Rb = p.sb("Rb", [128, 512], F32)
    Mb = p.sb("Mb", [128, 512], F32)
    xln = p.sb("xln", [128, 16 * TPC], BF16)
    qlat = p.sb("qlat", [128, 12 * TPC], BF16)
    kvlat = p.sb("kvlat", [128, 4 * TPC], BF16)
    wt = [p.sb(f"wt{i}", [128, 16 * 128], BF16) for i in range(2)]
    ev_tmp = p.sb("ev_tmp", [128, 2 * 512], BF16)
    sq2 = p.sb("sq2", [128, 2 * 512], F32)
    nrow = p.sb("nrow", [1, 2 * 512], F32)
    wv_sb = p.sb("wv_sb", [128, 4 * 2048], BF16)
    wkup_sb = p.sb("wkup_sb", [128, 4 * 64], BF16)
    wqup_sb = p.sb("wqup_sb", [128, 12 * 1024], BF16)
    bvu_sb = p.sb("bvu_sb", [1, 2048], BF16)
    bqup_sb = p.sb("bqup_sb", [1, 1024], BF16)
    bkup_sb = p.sb("bkup_sb", [1, 64], BF16)
    cos_sb = p.sb("cos_sb", [128, 8 * 32], F32)
    sin_sb = p.sb("sin_sb", [128, 8 * 32], F32)
    cosq_sb = p.sb("cosq_sb", [128, 2 * 512], F32)
    sinq_sb = p.sb("sinq_sb", [128, 2 * 512], F32)
    tmt = p.sb("tmt", [128, 2 * 1024], BF16)
    rpt = p.sb("rpt", [128, 2 * 1024], BF16)

    for k in range(4):
        p.dma("sy", wv_sb[:, k * 2048:(k + 1) * 2048], WVU[k * 128:(k + 1) * 128, :], "ldc")
        p.dma("sy", wkup_sb[:, k * 64:(k + 1) * 64], WKUP[k * 128:(k + 1) * 128, :], "ldc")
    for k in range(12):
        p.dma("sy", wqup_sb[:, k * 1024:(k + 1) * 1024], WQUP[k * 128:(k + 1) * 128, :], "ldc")
    p.dma("sy", bvu_sb[:], BVU[:], "ldc")
    p.dma("sy", bqup_sb[:], BQUP[:], "ldc")
    p.dma("sy", bkup_sb[:], BKUP[:], "ldc")
    for tc in range(8):
        p.dma("sy", cos_sb[:, tc * 32:(tc + 1) * 32], COS[tc * 128:(tc + 1) * 128, :], "ldc")
        p.dma("sy", sin_sb[:, tc * 32:(tc + 1) * 32], SIN[tc * 128:(tc + 1) * 128, :], "ldc")
    for e in ("te", "sc", "ve"):
        p.wait(e, "ldc", p.cnt["ldc"])

    pb = [p.ps(f"pb{i}", [128, 512]) for i in range(6)]
    pt = p.ps("pt", [128, 128], BF16)
    pn = p.ps("pn", [4, 512])

    for half in range(2):
        hs = half * 512
        tag = f"m{half}"
        if half == 1:
            p.wait("sy", "lnpe_m0", p.cnt["lnpe_m0"])
            p.wait("sc", "lnpe_m0", p.cnt["lnpe_m0"])
            p.wait("sy", "xlnap", p.cnt["xlnap"])
        emit_ln_stats(p, tag, lambda kc, hs=hs: xT[kc * 128:(kc + 1) * 128, hs:hs + 512],
                      512, NKD, (xbuf, sqt), pb[0], pb[1], row4, Rb, Mb,
                      oc, orw, 1e-5)
        emit_ln_apply(p, tag, lambda kc, hs=hs: xT[kc * 128:(kc + 1) * 128, hs:hs + 512],
                      [xln[:, kc * TPC + hs:kc * TPC + hs + 512] for kc in range(NKD)],
                      512, NKD, abuf, Rb, Mb, app="xlnap")
    p.wait("te", "xlnap", p.cnt["xlnap"])

    ctx = ProjCtx(p, wt, pb[0:4])

    def mk_lat_evac(dst, bias_sb):
        def evac(m, half, ps_, pev):
            hs = half * 512
            p.wait("sc", ctx.pe, pev)
            p.op("sc", lambda eng, m=m, hs=hs, ps_=ps_: eng.activation(
                dst[:, m * TPC + hs:m * TPC + hs + 512], ps_[:], AF.Identity,
                bias=bias_sb[:, m:m + 1]), inc=(ctx.ev, 1))
        return evac

    emit_proj_fm(p, ctx, WQD, NKD, 12,
                 lambda k, hs: xln[:, k * TPC + hs:k * TPC + hs + 512],
                 mk_lat_evac(qlat, bqd))
    emit_proj_fm(p, ctx, WKVD, NKD, 4,
                 lambda k, hs: xln[:, k * TPC + hs:k * TPC + hs + 512],
                 mk_lat_evac(kvlat, bkvd))
    latev = p.cnt[ctx.ev]
    p.wait("te", ctx.ev, latev)

    if stage < 2:
        p.finish("gp"); p.finalize(); return p
    kro = p.outstream("kro", 2)
    krn = p.outstream("krn", 2)
    krnsq = p.sb("krnsq", [128, 64], F32)
    krnred = p.sb("krnred", [128, 2], F32)
    for tc in range(8):
        ts0 = tc * 128
        pv = pb[4 + tc % 2]
        if tc >= 2:
            p.wait("te", "krrp", tc - 1)
        for k in range(4):
            p.op("te", lambda eng, k=k, ts0=ts0, pv=pv: eng.matmul(
                pv[:, 0:64], kvlat[:, k * TPC + ts0:k * TPC + ts0 + 128],
                wkup_sb[:, k * 64:(k + 1) * 64], start=(k == 0), stop=False))
        p.op("te", lambda eng, pv=pv: eng.matmul(
            pv[:, 0:64], orb[0:1, :], bkup_sb[0:1, :], start=False, stop=True),
            inc=("krpe", 1))
        p.wait("ve", "krpe", p.cnt["krpe"])
        sl = kro.ready("ve")
        x1 = pv[:, 0:32]; x2 = pv[:, 32:64]
        c_ = cos_sb[:, tc * 32:(tc + 1) * 32]
        s_ = sin_sb[:, tc * 32:(tc + 1) * 32]
        o1 = rpt[:, sl * 1024:sl * 1024 + 32]
        o2 = rpt[:, sl * 1024 + 32:sl * 1024 + 64]
        t1 = rpt[:, sl * 1024 + 64:sl * 1024 + 96]
        p.rop("ve", lambda eng, x1=x1, c_=c_, o1=o1: eng.tensor_tensor(o1, x1, c_, ALU.mult))
        p.rop("ve", lambda eng, x2=x2, s_=s_, t1=t1: eng.tensor_tensor(t1, x2, s_, ALU.mult))
        p.chain("ve")
        p.rop("ve", lambda eng, o1=o1, t1=t1: eng.tensor_sub(o1, o1, t1))
        p.chain("ve")
        p.rop("ve", lambda eng, x2=x2, c_=c_, o2=o2: eng.tensor_tensor(o2, x2, c_, ALU.mult))
        p.rop("ve", lambda eng, x1=x1, s_=s_, t1=t1: eng.tensor_tensor(t1, x1, s_, ALU.mult))
        p.chain("ve")
        p.op("ve", lambda eng, o2=o2, t1=t1: eng.tensor_add(o2, o2, t1),
             inc=("krrp", 1))
        kro.store("sy", KR_TM[ts0:ts0 + 128, :], rpt[:, sl * 1024:sl * 1024 + 64],
                  after=("krrp", p.cnt["krrp"]))
        p.wait("sc", "krrp", p.cnt["krrp"])
        if tc >= 2:
            p.wait("sc", "krnv", tc - 1)
        p.op("sc", lambda eng, sl=sl: eng.activation(
            krnsq[:], rpt[:, sl * 1024:sl * 1024 + 64], AF.Square),
            inc=("krns", 1))
        p.wait("ve", "krns", p.cnt["krns"])
        sn = krn.ready("ve")
        p.op("ve", lambda eng, sn=sn: eng.reduce_sum(
            krnred[:, sn:sn + 1], krnsq[:], axis=AX.X), inc=("krnv", 1))
        krn.store("sy", KRN2T[ts0:ts0 + 128, :], krnred[:, sn:sn + 1],
                  after=("krnv", p.cnt["krnv"]))

    if stage < 3:
        p.finish("gp"); p.finalize(); return p
    p.wait("te", "krrp", p.cnt["krrp"])
    do_ = p.outstream("do", 2)
    vo = do_
    for tc in range(8):
        ts0 = tc * 128
        for n in range(4):
            gi = tc * 4 + n
            pv = pb[4 + gi % 2]
            if gi >= 2:
                p.wait("te", "vev", gi - 1)
            for k in range(4):
                p.op("te", lambda eng, k=k, ts0=ts0, pv=pv, n=n: eng.matmul(
                    pv[:], kvlat[:, k * TPC + ts0:k * TPC + ts0 + 128],
                    wv_sb[:, k * 2048 + n * 512:k * 2048 + (n + 1) * 512],
                    start=(k == 0), stop=False))
            p.op("te", lambda eng, pv=pv, n=n: eng.matmul(
                pv[:], orb[0:1, :], bvu_sb[0:1, n * 512:(n + 1) * 512],
                start=False, stop=True), inc=("vpe", 1))
            p.wait("sc", "vpe", p.cnt["vpe"])
            so = vo.ready("sc")
            p.op("sc", lambda eng, pv=pv, so=so: eng.copy(
                ev_tmp[:, so * 512:(so + 1) * 512], pv[:]), inc=("vev", 1))
            vo.store("sy", VT[ts0:ts0 + 128, n * 512:(n + 1) * 512],
                     ev_tmp[:, so * 512:(so + 1) * 512], after=("vev", p.cnt["vev"]))

    if stage < 4:
        p.finish("gp"); p.finalize(); return p
    p.wait("te", "vev", p.cnt["vev"])
    p.wait("ve", "kros0", p.cnt["kros0"])
    p.wait("ve", "kros1", p.cnt["kros1"])
    cq = p.instream("cq", 2)
    qo = p.outstream("qo", 2)
    qn = p.outstream("qn", 2)
    qsq = p.sb("qsq", [128, 512], F32)
    qnred = p.sb("qnred", [128, 2 * 16], F32)
    for tc in range(8):
        ts0 = tc * 128
        csl = cq.i % 2
        tokc = cq.load("sy", [(cosq_sb[:, csl * 512:(csl + 1) * 512], COSQ[ts0:ts0 + 128, :]),
                              (sinq_sb[:, csl * 512:(csl + 1) * 512], SINQ[ts0:ts0 + 128, :])])
        sn = qn.ready("ve")
        for n in range(2):
            gi = tc * 2 + n
            pv = pb[4 + gi % 2]
            if gi >= 2:
                p.wait("te", "qrrp", gi - 1)
            for k in range(12):
                p.op("te", lambda eng, k=k, ts0=ts0, pv=pv, n=n: eng.matmul(
                    pv[:], qlat[:, k * TPC + ts0:k * TPC + ts0 + 128],
                    wqup_sb[:, k * 1024 + n * 512:k * 1024 + (n + 1) * 512],
                    start=(k == 0), stop=False))
            p.op("te", lambda eng, pv=pv, n=n: eng.matmul(
                pv[:], orb[0:1, :], bqup_sb[0:1, n * 512:(n + 1) * 512],
                start=False, stop=True), inc=("qrpe", 1))
            p.wait("ve", "qrpe", p.cnt["qrpe"])
            cq.wait("ve", tokc)
            so = qo.ready("ve")
            dst = tmt[:, so * 1024:so * 1024 + 512]
            pvv = pv[:, 0:512].rearrange("p (h t f) -> p h t f", h=8, t=2, f=32)
            dvv = dst.rearrange("p (h t f) -> p h t f", h=8, t=2, f=32)
            cqv = cosq_sb[:, csl * 512:(csl + 1) * 512].rearrange("p (h f) -> p h f", h=16, f=32)
            sqv = sinq_sb[:, csl * 512:(csl + 1) * 512].rearrange("p (h f) -> p h f", h=16, f=32)
            cqh = cqv[:, n * 8:(n + 1) * 8, :]
            sqh = sqv[:, n * 8:(n + 1) * 8, :]
            x1 = pvv[:, :, 0, :]; x2 = pvv[:, :, 1, :]
            d1 = dvv[:, :, 0, :]; d2 = dvv[:, :, 1, :]
            tscr = rpt[:, 128:384].rearrange("p (h f) -> p h f", h=8, f=32)
            incs = [("qrrp", 1)] + ([cq.free(tokc)] if n == 1 else [])
            p.rop("ve", lambda eng, d1=d1, x1=x1, cqh=cqh: eng.tensor_tensor(d1, x1, cqh, ALU.mult))
            p.rop("ve", lambda eng, tscr=tscr, x2=x2, sqh=sqh: eng.tensor_tensor(tscr, x2, sqh, ALU.mult))
            p.chain("ve")
            p.rop("ve", lambda eng, d1=d1, tscr=tscr: eng.tensor_sub(d1, d1, tscr))
            p.chain("ve")
            p.rop("ve", lambda eng, d2=d2, x2=x2, cqh=cqh: eng.tensor_tensor(d2, x2, cqh, ALU.mult))
            p.rop("ve", lambda eng, tscr=tscr, x1=x1, sqh=sqh: eng.tensor_tensor(tscr, x1, sqh, ALU.mult))
            p.chain("ve")
            p.op("ve", lambda eng, d2=d2, tscr=tscr: eng.tensor_add(d2, d2, tscr),
                 inc=incs)
            p.wait("ve", "qrrp", p.cnt["qrrp"])
            qo.store("sy", QR_TM[ts0:ts0 + 128, n * 512:(n + 1) * 512], dst,
                     after=("qrrp", p.cnt["qrrp"]))
            p.wait("sc", "qrrp", p.cnt["qrrp"])
            if gi >= 2:
                p.wait("sc", "qnv", gi - 1)
            p.op("sc", lambda eng, dst=dst: eng.activation(qsq[:], dst, AF.Square),
                 inc=("qns", 1))
            p.wait("ve", "qns", p.cnt["qns"])
            qv = qsq.rearrange("p (h f) -> p h f", h=8, f=64)
            p.op("ve", lambda eng, qv=qv, sn=sn, n=n: eng.reduce_sum(
                qnred[:, sn * 16 + n * 8:sn * 16 + (n + 1) * 8], qv, axis=AX.X),
                inc=("qnv", 1))
        qn.store("sy", QRN2T[ts0:ts0 + 128, :], qnred[:, sn * 16:(sn + 1) * 16],
                 after=("qnv", p.cnt["qnv"]))

    if stage < 5:
        p.finish("gp"); p.finalize(); return p
    no = p.outstream("no", 2)

    def mk_head_evac(dst_dram, bias_sb, n2_dram, tagn, extra_n2_row=None,
                     extra_fn=None):
        def evac(m, half, ps_, pev):
            hs = half * 512
            gi = m * 2 + half
            sl = gi % 2
            p.wait("sc", ctx.pe, pev)
            if gi >= 2:
                p.wait("sc", "pn_" + tagn, gi - 1)
            so = do_.ready("sc")
            p.op("sc", lambda eng: eng.activation(
                ev_tmp[:, so * 512:so * 512 + 512], ps_[:], AF.Identity,
                bias=bias_sb[:, m:m + 1]), inc=(ctx.ev, 1))
            p.wait("sc", ctx.ev, p.cnt[ctx.ev])
            p.op("sc", lambda eng: eng.activation(
                sq2[:, sl * 512:(sl + 1) * 512],
                ev_tmp[:, so * 512:so * 512 + 512], AF.Square),
                inc=("sq_" + tagn, 1))
            do_.store("sy", dst_dram[m * 128:(m + 1) * 128, hs:hs + 512],
                      ev_tmp[:, so * 512:so * 512 + 512],
                      after=("sq_" + tagn, p.cnt["sq_" + tagn]))
            p.wait("te", "sq_" + tagn, p.cnt["sq_" + tagn])
            if gi >= 1:
                p.wait("te", "nv_" + tagn, gi)
            p.op("te", lambda eng: eng.matmul(
                pn[0:1, 0:512], oc[:, 0:1],
                sq2[:, sl * 512:(sl + 1) * 512], start=True, stop=True),
                inc=("pn_" + tagn, 1))
            p.wait("ve", "pn_" + tagn, p.cnt["pn_" + tagn])
            sn = no.ready("ve")
            if extra_fn is not None:
                ex = extra_fn(m, hs)
                p.op("ve", lambda eng, sn=sn, ex=ex: eng.tensor_tensor(
                    nrow[0:1, sn * 512:(sn + 1) * 512], pn[0:1, 0:512],
                    ex, ALU.add), inc=("nv_" + tagn, 1))
            elif extra_n2_row is None:
                p.op("ve", lambda eng, sn=sn: eng.tensor_copy(
                    nrow[0:1, sn * 512:(sn + 1) * 512], pn[0:1, 0:512]),
                    inc=("nv_" + tagn, 1))
            else:
                p.op("ve", lambda eng, sn=sn, hs=hs: eng.tensor_tensor(
                    nrow[0:1, sn * 512:(sn + 1) * 512], pn[0:1, 0:512],
                    extra_n2_row[0:1, hs:hs + 512], ALU.add),
                    inc=("nv_" + tagn, 1))
            no.store("sy", n2_dram[m:m + 1, hs:hs + 512],
                     nrow[0:1, sn * 512:(sn + 1) * 512],
                     after=("nv_" + tagn, p.cnt["nv_" + tagn]))
        return evac

    emit_proj_fm(p, ctx, WQUC, 12, 16,
                 lambda k, hs: qlat[:, k * TPC + hs:k * TPC + hs + 512],
                 mk_head_evac(QC, bquc, QN2, "qc"))
    emit_proj_fm(p, ctx, WKUC, 4, 16,
                 lambda k, hs: kvlat[:, k * TPC + hs:k * TPC + hs + 512],
                 mk_head_evac(KC, bkuc, KN2, "kc"))

    p.finish("gp")
    p.finalize()
    return p


def build_attn(mla: bool):
    p = Prog("attn_mla" if mla else "attn_mha")
    NH = 16
    scale = 1.0 / np.sqrt(192.0) if mla else 1.0 / np.sqrt(128.0)
    QC = p.din("QC", [D, TPC], BF16)
    KCB = p.din("KCB", [32, D, 128], BF16)
    VTB = p.din("VTB", [32, 128, D], BF16)
    KCD = p.din("KCD", [8, D, 128], BF16)
    VTD = p.din("VTD", [8, 128, D], BF16)
    QN2 = p.din("QN2", [16, TPC], F32)
    KN2F = p.din("KN2F", [16, SEQ], F32)
    VIS = p.din("VIS", [1, 64], F32)
    if mla:
        QRF = p.din("QRF", [1024, TPC], BF16)
        QRN2 = p.din("QRN2", [16, TPC], F32)
        KRN2F = p.din("KRN2F", [1, SEQ], F32)
        KRF = p.din("KRF", [128, SEQ], BF16)
        KRD = p.din("KRD", [128, TPC], BF16)
        xT = p.din("xT", [D, TPC], F32)
        xoT = p.dout("xoT", [D, TPC], F32)
    else:
        OF = p.dout("OF", [D, TPC], BF16)

    orw = p.sb("orw", [1, 128], F32)
    ocb = p.sb("ocb", [128, 1], BF16)
    tri = p.sb("tri", [128, 4 * 512], BF16)
    tri_np = np.zeros((128, 4 * 512), dtype=BF)
    for t_ in range(4):
        kk = np.arange(128)[:, None]
        qq = np.arange(512)[None, :]
        tri_np[:, t_ * 512:(t_ + 1) * 512] = (128 * t_ + kk <= qq).astype(BF)
    p.dma("sy", orw[:], p.inline(np.ones((1, 128), np.float32), "onesr")[:], "ldc")
    p.dma("sy", ocb[:], p.inline(np.ones((128, 1), BF), "onescb")[:], "ldc")
    p.dma("sy", tri[:], p.inline(tri_np, "trie")[:], "ldc")

    qc_sb = p.sb("qc_sb", [128, 16 * TPC], BF16)
    for h in range(16):
        p.dma("sy", qc_sb[:, h * TPC:(h + 1) * TPC], QC[h * 128:(h + 1) * 128, :], "ldq")
    if mla:
        qr_sb = p.sb("qr_sb", [128, 8 * TPC], BF16)
        for cix in range(8):
            p.dma("sy", qr_sb[:, cix * TPC:(cix + 1) * TPC],
                  QRF[cix * 128:(cix + 1) * 128, :], "ldq")
        krf_sb = p.sb("krf_sb2", [128, SEQ], BF16)
        krd_sb = p.sb("krd_sb", [128, TPC], BF16)
        p.dma("sy", krf_sb[:], KRF[:], "ldq")
        p.dma("sy", krd_sb[:], KRD[:], "ldq")
    qn2_sb = p.sb("qn2_sb", [16, TPC], F32)
    kn2_sb = p.sb("kn2_sb", [16, SEQ], F32)
    p.dma("sy", qn2_sb[:], QN2[:], "ldc")
    p.dma("sy", kn2_sb[:], KN2F[:], "ldc")
    if mla:
        qrn2_sb = p.sb("qrn2_sb", [16, TPC], F32)
        p.dma("sy", qrn2_sb[:], QRN2[:], "ldc")
        krn2_sb = p.sb("krn2_sb", [1, SEQ], F32)
        p.dma("sy", krn2_sb[:], KRN2F[:], "ldc")
        o16 = p.sb("o16", [1, 16], F32)
        p.dma("sy", o16[:], p.inline(np.ones((1, 16), np.float32), "ones16")[:], "ldc")
    vis_sb = p.sb("vis_sb", [1, 64], F32)
    p.dma("sy", vis_sb[:], VIS[:], "ldc")

    kbuf = p.sb("kbuf", [128, 3 * 256], BF16)
    vbuf = p.sb("vbuf", [128, 3 * 256], BF16)
    pT = p.sb("pT", [128, 4 * 512], BF16)
    rows = p.sb("rows", [1, 4 * 512], F32)
    rsb = p.sb("rsb", [128, 2 * 512], F32)
    och = p.sb("och", [128, 2 * 512], F32 if mla else BF16)
    xrb = p.sb("xrb", [128, 2 * 512], F32) if mla else None
    small = p.sb("small", [16, 8], F32)
    negrow = p.sb("negrow", [1, 32], F32)
    negBb = p.sb("negBb", [128, 32], F32)
    negBpB = p.sb("negBpB", [128, 32], F32)
    visb = p.sb("visb", [128, 64], F32)
    bias_all = p.sb("bias_all", [128, 1152], F32)
    scratch_d = p.dtmp("scr", [16, 2], F32)

    psS = [p.ps(f"psS{i}", [128, 512]) for i in range(4)]
    psO = [p.ps(f"psO{i}", [128, 512]) for i in range(2)]
    psR = [p.ps(f"psR{i}", [1, 512]) for i in range(2)]

    ldcv = p.cnt["ldc"]
    for e in ("te", "sc", "ve"):
        p.wait(e, "ldc", ldcv)

    qn2t = qn2_sb
    if mla:
        p.rop("ve", lambda eng: eng.tensor_tensor(qn2_sb[:], qn2_sb[:], qrn2_sb[:], ALU.add))
        p.chain("ve")
        for j8 in range(8):
            p.op("te", lambda eng, j8=j8: eng.matmul(
                psS[2][0:16, 0:512], orw[0:1, 0:16], krn2_sb[0:1, j8 * 512:(j8 + 1) * 512],
                start=True, stop=True), inc=("krb", 1))
            p.wait("ve", "krb", p.cnt["krb"])
            p.op("ve", lambda eng, j8=j8: eng.tensor_tensor(
                kn2_sb[:, j8 * 512:(j8 + 1) * 512], kn2_sb[:, j8 * 512:(j8 + 1) * 512],
                psS[2][0:16, 0:512], ALU.add), inc=("krbv", 1))
            p.wait("te", "krbv", p.cnt["krbv"])
        p.wait("ve", "krbv", p.cnt["krbv"])
    p.rop("ve", lambda eng: eng.reduce_max(small[:, 0:1], kn2_sb[:], axis=AX.X))
    p.chain("ve")
    p.rop("ve", lambda eng: eng.reduce_max(small[:, 1:2], qn2_sb[:, 0:512], axis=AX.X))
    p.chain("ve")
    p.rop("ve", lambda eng: eng.reduce_max(small[:, 2:3], qn2_sb[:, 512:1024], axis=AX.X))
    p.chain("ve")
    p.rop("ve", lambda eng: eng.tensor_scalar(small[:, 4:6], small[:, 1:3],
                                              small[:, 0:1], None, ALU.mult))
    p.chain("ve")
    p.op("ve", lambda eng: eng.tensor_copy(small[:, 6:8], small[:, 4:6]),
         inc=("bnd", 1))
    p.wait("sc", "bnd", p.cnt["bnd"])
    p.rop("sc", lambda eng: eng.activation(small[:, 6:8], small[:, 6:8], AF.Ln))
    p.chain("sc")
    p.op("sc", lambda eng: eng.activation(small[:, 6:8], small[:, 6:8],
                                          AF.Exp, scale=0.5), inc=("bnd", 1))
    p.wait("ve", "bnd", p.cnt["bnd"])
    p.op("ve", lambda eng: eng.tensor_scalar(small[:, 6:8], small[:, 6:8],
                                             -float(scale), None, ALU.mult),
         inc=("bnd", 1))
    p.wait("sy", "bnd", p.cnt["bnd"])
    p.dma("sy", scratch_d[:], small[:, 6:8], "bncA")
    p.wait("sy", "bncA", p.cnt["bncA"])
    p.dma("sy", negrow[:], scratch_d.rearrange("h t -> (h t)")[None, :] if False else scratch_d.rearrange("(o h) t -> o (h t)", o=1), "bncB")
    p.wait("te", "bncB", p.cnt["bncB"])
    p.op("te", lambda eng: eng.matmul(psS[0][:, 0:32], orw[0:1, :], negrow[0:1, :],
                                      start=True, stop=True), inc=("bnd", 1))
    p.wait("ve", "bnd", p.cnt["bnd"])
    p.rop("ve", lambda eng: eng.tensor_copy(negBb[:], psS[0][:, 0:32]))
    p.chain("ve")
    p.rop("ve", lambda eng: eng.tensor_scalar(negBpB[:], negBb[:], BIG, None, ALU.add))
    p.chain("ve")
    p.wait("te", "chain_ve", p.cnt["chain_ve"])
    p.op("te", lambda eng: eng.matmul(psS[1][:, 0:64], orw[0:1, :], vis_sb[0:1, :],
                                      start=True, stop=True), inc=("bnd", 1))
    p.wait("ve", "bnd", p.cnt["bnd"])
    p.rop("ve", lambda eng: eng.tensor_copy(visb[:], psS[1][:, 0:64]))
    p.chain("ve")
    for t_ in range(2):
        for H in range(16):
            base = (t_ * 16 + H) * 36
            col = 2 * H + t_
            p.rop("ve", lambda eng, base=base, col=col, t_=t_: eng.tensor_scalar(
                bias_all[:, base:base + 32], visb[:, t_ * 32:(t_ + 1) * 32],
                negBpB[:, col:col + 1], BIG, ALU.mult, ALU.subtract))
            p.rop("ve", lambda eng, base=base, col=col: eng.tensor_scalar(
                bias_all[:, base + 32:base + 36],
                visb[:, 0:4] if False else negBb[:, col:col + 1].rearrange("p one -> p one") if False else visb[:, 0:4],
                negBb[:, col:col + 1], None, ALU.mult) if False else None)
            for dj in range(4):
                p.rop("ve", lambda eng, base=base, col=col, dj=dj: eng.tensor_copy(
                    bias_all[:, base + 32 + dj:base + 33 + dj], negBb[:, col:col + 1]))
    p.op("ve", lambda eng: eng.tensor_copy(small[:, 3:4], small[:, 0:1]),
         inc=("bsv", 1))
    p.wait("sc", "bsv", p.cnt["bsv"])

    kst = p.instream("kst", 3)
    if mla:
        xst = p.instream("xst", 2)
        oo = p.outstream("oo", 2)
    else:
        oo = p.outstream("oo", 2)

    scp = 0
    avp = 0
    oev = 0
    scp_at = {}
    avp_at = {}

    def q_ap(H, t_, mla_part):
        if not mla_part:
            return qc_sb[:, H * TPC + t_ * 512: H * TPC + t_ * 512 + 512]
        base = (H // 2) * TPC + t_ * 512
        r0 = 64 * (H % 2)
        return qr_sb[r0:r0 + 64, base:base + 512]

    vem_at = {}

    def emit_sumav(hb, t_, j, sl, tok):
        nonlocal_avp = None
        for h in range(2):
            pslot = pT[:, (2 * h + j % 2) * 512:(2 * h + j % 2 + 1) * 512]
            if j >= 32:
                p.wait("te", "vem", vem_at[(hb, t_, j, h)])
            else:
                p.wait("te", "scp", scp_at[(hb, t_, j, h)])
            p.op("te", lambda eng, pslot=pslot, h=h, j=j: eng.matmul(
                psR[h][0:1, :], ocb[:, 0:1], pslot,
                start=(j == 0), stop=(j == 35)))
            incs2 = [("avp", 1)]
            if h == 1:
                incs2.append(kst.free(tok))
            p.op("te", lambda eng, pslot=pslot, h=h, j=j, sl=sl: eng.matmul(
                psO[h][:], vbuf[:, sl * 256 + h * 128:sl * 256 + h * 128 + 128],
                pslot, start=(j == 0), stop=(j == 35)), inc=incs2)
            avp_at[(hb, t_, j, h)] = p.cnt["avp"]

    p.wait("te", "ldq", p.cnt["ldq"])
    for hb in range(8):
        for t_ in range(2):
            pend = None
            for j in range(NSLOT := 36):
                sl = kst.i % 3
                if j < 32:
                    ksrc = KCB[j]; vsrc = VTB[j]
                else:
                    jd = 4 * t_ + (j - 32)
                    ksrc = KCD[jd]; vsrc = VTD[jd]
                tok = kst.load("sy", [
                    (kbuf[:, sl * 256:sl * 256 + 128], ksrc[256 * hb:256 * hb + 128, :]),
                    (kbuf[:, sl * 256 + 128:sl * 256 + 256], ksrc[256 * hb + 128:256 * hb + 256, :]),
                    (vbuf[:, sl * 256:(sl + 1) * 256], vsrc[:, 256 * hb:256 * hb + 256]),
                ])
                kst.wait("te", tok)
                for h in range(2):
                    H = 2 * hb + h
                    pS = psS[2 * h + j % 2]
                    if j >= 2:
                        p.wait("te", "scp", scp_at[(hb, t_, j - 2, h)])
                    elif (hb, t_) != (0, 0):
                        prev = (hb, t_ - 1) if t_ == 1 else (hb - 1, 1)
                        p.wait("te", "scp", scp_at[(prev[0], prev[1], 34 + (j % 2), h)])
                    incs = [("spe", 1)]
                    if not mla:
                        p.op("te", lambda eng, pS=pS, sl=sl, h=h, H=H, t_=t_: eng.matmul(
                            pS[:], kbuf[:, sl * 256 + h * 128:sl * 256 + h * 128 + 128],
                            q_ap(H, t_, False), start=True, stop=True), inc=incs)
                    else:
                        p.op("te", lambda eng, pS=pS, sl=sl, h=h, H=H, t_=t_: eng.matmul(
                            pS[:], kbuf[:, sl * 256 + h * 128:sl * 256 + h * 128 + 128],
                            q_ap(H, t_, False), start=True, stop=False))
                        if j < 32:
                            krap = krf_sb[64 * (H % 2):64 * (H % 2) + 64, 128 * j:128 * j + 128]
                        else:
                            jd = 4 * t_ + (j - 32)
                            krap = krd_sb[64 * (H % 2):64 * (H % 2) + 64, 128 * jd:128 * jd + 128]
                        p.op("te", lambda eng, pS=pS, krap=krap, H=H, t_=t_: eng.matmul(
                            pS[:], krap, q_ap(H, t_, True), start=False, stop=True),
                            inc=incs)
                    p.wait("sc", "spe", p.cnt["spe"])
                    if scp >= 4:
                        p.wait("sc", "avp", avp_at.get((hb, t_, j - 2, h), avp))
                    bcol = bias_all[:, (t_ * 16 + H) * 36 + j:(t_ * 16 + H) * 36 + j + 1]
                    pslot = pT[:, (2 * h + j % 2) * 512:(2 * h + j % 2 + 1) * 512]
                    p.op("sc", lambda eng, pslot=pslot, pS=pS, bcol=bcol: eng.activation(
                        pslot, pS[:], AF.Exp, bias=bcol, scale=float(scale)),
                        inc=("scp", 1))
                    scp += 1
                    scp_at[(hb, t_, j, h)] = p.cnt["scp"]
                    if j >= 32:
                        p.wait("ve", "scp", p.cnt["scp"])
                        p.op("ve", lambda eng, pslot=pslot, t_=t_, j=j: eng.tensor_tensor(
                            pslot, pslot, tri[:, (j - 32) * 512:(j - 31) * 512],
                            ALU.mult), inc=("vem", 1))
                        vem_at[(hb, t_, j, h)] = p.cnt["vem"]
                    avp += 1
                if pend is not None:
                    emit_sumav(*pend)
                pend = (hb, t_, j, sl, tok)
            emit_sumav(*pend)
            avv = p.cnt["avp"]
            for h in range(2):
                H = 2 * hb + h
                p.wait("ve", "avp", avv)
                p.rop("ve", lambda eng, h=h: eng.tensor_copy(
                    rows[0:1, h * 512:(h + 1) * 512], psR[h][0:1, :]))
                p.chain("ve")
                p.op("ve", lambda eng, h=h: eng.reciprocal(
                    rows[0:1, 1024 + h * 512:1024 + (h + 1) * 512],
                    rows[0:1, h * 512:(h + 1) * 512]), inc=("rcp", 1))
            p.wait("te", "rcp", p.cnt["rcp"])
            for h in range(2):
                p.op("te", lambda eng, h=h: eng.matmul(
                    psS[h][:, 0:512], orw[0:1, :],
                    rows[0:1, 1024 + h * 512:1024 + (h + 1) * 512],
                    start=True, stop=True), inc=("rbc", 1))
            p.wait("ve", "rbc", p.cnt["rbc"])
            for h in range(2):
                H = 2 * hb + h
                p.rop("ve", lambda eng, h=h: eng.tensor_copy(
                    rsb[:, h * 512:(h + 1) * 512], psS[h][:, 0:512]))
                p.chain("ve")
                so = oo.ready("ve")
                if mla:
                    sx = xst.i % 2
                    tokx = xst.load("sy", (xrb[:, sx * 512:(sx + 1) * 512],
                                           xT[H * 128:H * 128 + 128, t_ * 512:t_ * 512 + 512]))
                    xst.wait("ve", tokx)
                    p.rop("ve", lambda eng, h=h, so=so: eng.tensor_tensor(
                        och[:, so * 512:(so + 1) * 512], psO[h][:],
                        rsb[:, h * 512:(h + 1) * 512], ALU.mult))
                    p.chain("ve")
                    p.op("ve", lambda eng, so=so, sx=sx: eng.tensor_tensor(
                        och[:, so * 512:(so + 1) * 512],
                        och[:, so * 512:(so + 1) * 512],
                        xrb[:, sx * 512:(sx + 1) * 512], ALU.add),
                        inc=[("oev", 1), xst.free(tokx)])
                    oo.store("sy", xoT[H * 128:H * 128 + 128, t_ * 512:t_ * 512 + 512],
                             och[:, so * 512:(so + 1) * 512], after=("oev", p.cnt["oev"]))
                else:
                    p.op("ve", lambda eng, h=h, so=so: eng.tensor_tensor(
                        och[:, so * 512:(so + 1) * 512], psO[h][:],
                        rsb[:, h * 512:(h + 1) * 512], ALU.mult), inc=("oev", 1))
                    oo.store("sy", OF[H * 128:H * 128 + 128, t_ * 512:t_ * 512 + 512],
                             och[:, so * 512:(so + 1) * 512], after=("oev", p.cnt["oev"]))
                oev += 1
            p.wait("te", "oev", p.cnt["oev"])

    p.finish("gp")
    p.finalize()
    return p


def build_mha_pre():
    p = Prog("mhapre")
    xT = p.din("xT", [D, TPC], F32)
    WQ = p.din("WQ", [D, D], BF16)
    WK = p.din("WK", [D, D], BF16)
    WV = p.din("WV", [D, D], BF16)
    BQ = p.din("BQ", [1, D], BF16)
    BK = p.din("BK", [1, D], BF16)
    BV = p.din("BV", [1, D], BF16)
    LN1W = p.din("LN1W", [D, 1], F32)
    LN1B = p.din("LN1B", [D, 1], F32)
    COSQ = p.din("COSQ", [TPC, 1024], F32)
    SINQ = p.din("SINQ", [TPC, 1024], F32)
    H0 = p.dout("H0", [D, TPC], F32)
    Q_TM = p.dout("Q_TM", [TPC, D], BF16)
    K_TM = p.dout("K_TM", [TPC, D], BF16)
    VT = p.dout("VT", [TPC, D], BF16)
    QN2T = p.dout("QN2T", [TPC, 16], F32)
    KN2T = p.dout("KN2T", [TPC, 16], F32)

    oc = p.sb("oc", [128, 1], F32)
    orw = p.sb("orw", [1, 128], F32)
    orb = p.sb("orb", [1, 128], BF16)
    p.dma("sy", oc[:], p.inline(np.ones((128, 1), np.float32), "onesc")[:], "ldc")
    p.dma("sy", orw[:], p.inline(np.ones((1, 128), np.float32), "onesr")[:], "ldc")
    p.dma("sy", orb[:], p.inline(np.ones((1, 128), BF), "onesrb")[:], "ldc")
    p.consts([1e-5])
    ln1w = load_bias_cols(p, "ln1w", LN1W, 16)
    ln1b = load_bias_cols(p, "ln1b", LN1B, 16)
    bq_sb = p.sb("bq_sb", [1, D], BF16)
    bk_sb = p.sb("bk_sb", [1, D], BF16)
    bv_sb = p.sb("bv_sb", [1, D], BF16)
    p.dma("sy", bq_sb[:], BQ[:], "ldc")
    p.dma("sy", bk_sb[:], BK[:], "ldc")
    p.dma("sy", bv_sb[:], BV[:], "ldc")
    for e in ("te", "sc", "ve"):
        p.wait(e, "ldc", p.cnt["ldc"])

    xbuf = p.sb("xbuf", [128, 4 * 512], F32)
    sqt = p.sb("sqt", [128, 4 * 512], F32)
    abuf = p.sb("abuf", [128, 4 * 512], F32)
    row6 = p.sb("row6", [1, 6 * 512], F32)
    Rb = p.sb("Rb", [128, 512], F32)
    Mb = p.sb("Mb", [128, 512], F32)
    R2b = p.sb("R2b", [128, 512], F32)
    h0t = p.sb("h0t", [128, 2 * 512], F32)
    xln = p.sb("xln", [128, 16 * TPC], BF16)
    wsl = [p.sb(f"wsl{i}", [128, 16 * 512], BF16) for i in range(2)]
    qtm = p.sb("qtm", [128, 8 * D], BF16)
    cosq_sb = p.sb("cosq_sb", [128, 2 * 1024], F32)
    sinq_sb = p.sb("sinq_sb", [128, 2 * 1024], F32)
    rsc = p.sb("rsc", [128, 2048], BF16)
    qsq = p.sb("qsq", [128, 2048], F32)
    qnred = p.sb("qnred", [128, 2 * 16], F32)
    vtt = p.sb("vtt", [128, 2 * 512], BF16)

    pb = [p.ps(f"pb{i}", [128, 512]) for i in range(8)]

    ho = p.outstream("ho", 2)

    for half in range(2):
        hs = half * 512
        tag = f"d{half}"
        if half == 1:
            p.wait("sy", "lnpe_d0", p.cnt["lnpe_d0"])
            p.wait("sc", "lnpe_d0", p.cnt["lnpe_d0"])
            p.wait("sy", "xlnap", p.cnt["xlnap"])
        emit_ln_stats(p, tag, lambda kc, hs=hs: xT[kc * 128:(kc + 1) * 128, hs:hs + 512],
                      512, NKD, (xbuf, sqt), pb[0], pb[1], row6, Rb, Mb,
                      oc, orw, 1e-5)
        s_ = row6[0:1, 0:512]; q_ = row6[0:1, 512:1024]
        r_ = row6[0:1, 1024:1536]
        v_ = row6[0:1, 2048:2560]; r2_ = row6[0:1, 2560:3072]
        p.rop("ve", lambda eng: eng.tensor_tensor(v_, s_, s_, ALU.mult))
        p.chain("ve")
        p.rop("ve", lambda eng: eng.tensor_sub(v_, q_, v_))
        p.chain("ve")
        p.rop("ve", lambda eng: eng.tensor_tensor(r2_, r_, r_, ALU.mult))
        p.chain("ve")
        p.op("ve", lambda eng: eng.tensor_tensor(r2_, r2_, v_, ALU.mult),
             inc=("r2v_" + tag, 1))
        p.wait("sc", "r2v_" + tag, p.cnt["r2v_" + tag])
        p.rop("sc", lambda eng: eng.activation(r2_, r2_, AF.Ln, bias=1e-5))
        p.chain("sc")
        p.op("sc", lambda eng: eng.activation(r2_, r2_, AF.Exp, scale=-0.5),
             inc=("r2s_" + tag, 1))
        p.wait("te", "r2s_" + tag, p.cnt["r2s_" + tag])
        p.op("te", lambda eng: eng.matmul(pb[2][:, 0:512], orw[0:1, :], r2_,
                                          start=True, stop=True), inc=("r2p_" + tag, 1))
        p.wait("ve", "r2p_" + tag, p.cnt["r2p_" + tag])
        p.op("ve", lambda eng: eng.tensor_copy(R2b[:], pb[2][:, 0:512]),
             inc=("r2b_" + tag, 1))
        p.wait("ve", "r2b_" + tag, p.cnt["r2b_" + tag])
        st = p.instream("hx" + tag, 4)
        for kc in range(NKD):
            sl = kc % 4
            tok = st.load("sy", (abuf[:, sl * 512:(sl + 1) * 512],
                                 xT[kc * 128:(kc + 1) * 128, hs:hs + 512]))
            st.wait("ve", tok)
            hsl = ho.ready("ve")
            h0c = h0t[:, hsl * 512:(hsl + 1) * 512]
            x_ = abuf[:, sl * 512:(sl + 1) * 512]
            dst = xln[:, kc * TPC + hs:kc * TPC + hs + 512]
            p.rop("ve", lambda eng, h0c=h0c, x_=x_: eng.tensor_tensor(h0c, x_, Rb[:], ALU.mult))
            p.chain("ve")
            p.op("ve", lambda eng, h0c=h0c: eng.tensor_tensor(h0c, h0c, Mb[:], ALU.add),
                 inc=[("h0v", 1), st.free(tok)])
            p.wait("ve", "h0v", p.cnt["h0v"])
            ho.store("sy", H0[kc * 128:(kc + 1) * 128, hs:hs + 512], h0c,
                     after=("h0v", p.cnt["h0v"]))
            p.rop("ve", lambda eng, dst=dst, h0c=h0c: eng.tensor_tensor(
                dst, h0c, R2b[:], ALU.mult))
            p.chain("ve")
            p.op("ve", lambda eng, dst=dst, kc=kc: eng.tensor_scalar(
                dst, dst, ln1w[:, kc:kc + 1], ln1b[:, kc:kc + 1],
                ALU.mult, ALU.add), inc=("xlnap", 1))
    p.wait("te", "xlnap", p.cnt["xlnap"])

    wst = p.instream("wst", 2)
    cq = p.instream("cqd", 2)
    qno = p.outstream("qno", 2)
    tmo = p.outstream("tmo", 2)

    def proj_tm(which, Wap, Bap, OUTap, N2ap, do_rope, call_idx=0, prev=None):
        for n in range(4):
            b = wst.i % 2
            tokw = wst.load("sy", [(wsl[b][:, k * 512:(k + 1) * 512],
                                    Wap[k * 128:(k + 1) * 128, n * 512:(n + 1) * 512])
                                   for k in range(NKD)])
            wst.wait("te", tokw)
            for tc in range(8):
                gi = n * 8 + tc
                pv = pb[4 + gi % 2]
                if gi >= 2:
                    p.wait("te", "tmev_" + which, gi - 1)
                for k in range(NKD):
                    p.op("te", lambda eng, k=k, b=b, tc=tc, pv=pv: eng.matmul(
                        pv[:], xln[:, k * TPC + tc * 128:k * TPC + tc * 128 + 128],
                        wsl[b][:, k * 512:(k + 1) * 512],
                        start=(k == 0), stop=False))
                incs = [("tmpe_" + which, 1)]
                if tc == 7:
                    incs.append(wst.free(tokw))
                p.op("te", lambda eng, pv=pv, n=n: eng.matmul(
                    pv[:], orb[0:1, :], Bap[0:1, n * 512:(n + 1) * 512],
                    start=False, stop=True), inc=incs)
                p.wait("sc", "tmpe_" + which, p.cnt["tmpe_" + which])
                if call_idx > 0 and n == 0:
                    p.wait("sc", f"tmos{tc % 2}", 64 * call_idx)
                    if prev is not None:
                        p.wait("sc", "n2s_" + prev, 8)
                p.op("sc", lambda eng, pv=pv, tc=tc, n=n: eng.copy(
                    qtm[:, tc * D + n * 512:tc * D + (n + 1) * 512], pv[:]),
                    inc=("tmev_" + which, 1))
        p.wait("ve", "tmev_" + which, p.cnt["tmev_" + which])
        p.wait("sy", "tmev_" + which, p.cnt["tmev_" + which])
        for tc in range(8):
            ts0 = tc * 128
            src_ = qtm[:, tc * D:(tc + 1) * D]
            if do_rope:
                csl = cq.i % 2
                tokc = cq.load("sy", [(cosq_sb[:, csl * 1024:(csl + 1) * 1024], COSQ[ts0:ts0 + 128, :]),
                                      (sinq_sb[:, csl * 1024:(csl + 1) * 1024], SINQ[ts0:ts0 + 128, :])])
                cq.wait("ve", tokc)
                vv_ = src_.rearrange("p (h t f) -> p h t f", h=16, t=2, f=64)
                x1 = vv_[:, :, 0, :]; x2 = vv_[:, :, 1, :]
                cqv = cosq_sb[:, csl * 1024:(csl + 1) * 1024].rearrange("p (h f) -> p h f", h=16, f=64)
                sqv = sinq_sb[:, csl * 1024:(csl + 1) * 1024].rearrange("p (h f) -> p h f", h=16, f=64)
                A = rsc[:, 0:1024].rearrange("p (h f) -> p h f", h=16, f=64)
                B = rsc[:, 1024:2048].rearrange("p (h f) -> p h f", h=16, f=64)
                p.rop("ve", lambda eng, A=A, x1=x1, cqv=cqv: eng.tensor_tensor(A, x1, cqv, ALU.mult))
                p.rop("ve", lambda eng, B=B, x1=x1, sqv=sqv: eng.tensor_tensor(B, x1, sqv, ALU.mult))
                p.chain("ve")
                p.rop("ve", lambda eng, x1=x1, x2=x2, sqv=sqv: eng.tensor_tensor(x1, x2, sqv, ALU.mult))
                p.chain("ve")
                p.rop("ve", lambda eng, x1=x1, A=A: eng.tensor_sub(x1, A, x1))
                p.rop("ve", lambda eng, x2=x2, cqv=cqv: eng.tensor_tensor(x2, x2, cqv, ALU.mult))
                p.chain("ve")
                incs = [("rpd_" + which, 1)] + [cq.free(tokc)]
                p.op("ve", lambda eng, x2=x2, B=B: eng.tensor_add(x2, x2, B), inc=incs)
                p.wait("ve", "rpd_" + which, p.cnt["rpd_" + which])
            else:
                p.op("ve", lambda eng: eng.tensor_copy(rsc[:, 0:4], rsc[:, 4:8]),
                     inc=("rpd_" + which, 1))
                p.wait("ve", "rpd_" + which, p.cnt["rpd_" + which])
            tmo.store("sy", OUTap[ts0:ts0 + 128, :], src_,
                      after=("rpd_" + which, p.cnt["rpd_" + which]))
            if N2ap is not None:
                p.wait("sc", "rpd_" + which, p.cnt["rpd_" + which])
                if tc >= 1:
                    p.wait("sc", "n2v_" + which, tc)
                p.op("sc", lambda eng, src_=src_: eng.activation(qsq[:], src_, AF.Square),
                     inc=("n2s_" + which, 1))
                p.wait("ve", "n2s_" + which, p.cnt["n2s_" + which])
                sn = qno.ready("ve")
                qv = qsq.rearrange("p (h f) -> p h f", h=16, f=128)
                p.op("ve", lambda eng, qv=qv, sn=sn: eng.reduce_sum(
                    qnred[:, sn * 16:(sn + 1) * 16], qv, axis=AX.X),
                    inc=("n2v_" + which, 1))
                qno.store("sy", N2ap[ts0:ts0 + 128, :], qnred[:, sn * 16:(sn + 1) * 16],
                          after=("n2v_" + which, p.cnt["n2v_" + which]))

    proj_tm("q", WQ, bq_sb, Q_TM, QN2T, True, 0)
    proj_tm("k", WK, bk_sb, K_TM, KN2T, True, 1, "q")
    proj_tm("v", WV, bv_sb, VT, None, False, 2, "k")
    p.finish("gp")
    p.finalize()
    return p


def build_mha_post():
    p = Prog("mhapost")
    OF = p.din("OF", [D, TPC], BF16)
    WO = p.din("WO", [16, 128, D], BF16)
    BO = p.din("BO", [D, 1], F32)
    H0 = p.din("H0", [D, TPC], F32)
    xT = p.din("xT", [D, TPC], F32)
    LN2W = p.din("LN2W", [D, 1], F32)
    LN2B = p.din("LN2B", [D, 1], F32)
    WG4 = p.din("WG4", [D, 4], BF16)
    WE1 = p.din("WE1", [4, 16, 128, D], BF16)
    BE1 = p.din("BE1", [4 * D, 1], F32)
    WE2 = p.din("WE2", [4, 16, 128, D], BF16)
    BE2 = p.din("BE2", [4, D], BF16)
    xoT = p.dout("xoT", [D, TPC], F32)
    hdram = p.dtmp("hdram", [D, TPC], F32)
    xhdram = p.dtmp("xhdram", [D, TPC], F32)

    oc = p.sb("oc", [128, 1], F32)
    orw = p.sb("orw", [1, 128], F32)
    idf = p.sb("idf", [128, 128], F32)
    p.dma("sy", oc[:], p.inline(np.ones((128, 1), np.float32), "onesc")[:], "ldc")
    p.dma("sy", orw[:], p.inline(np.ones((1, 128), np.float32), "onesr")[:], "ldc")
    p.dma("sy", idf[:], p.inline(np.eye(128, dtype=np.float32), "idfi")[:], "ldc")
    p.consts([1e-5])
    bo_c = load_bias_cols(p, "bo", BO, 16)
    ln2w = load_bias_cols(p, "ln2w", LN2W, 16)
    ln2b = load_bias_cols(p, "ln2b", LN2B, 16)
    be1_c = load_bias_cols(p, "be1", BE1, 64)
    wg4_sb = p.sb("wg4_sb", [128, 16 * 4], BF16)
    for k in range(NKD):
        p.dma("sy", wg4_sb[:, 4 * k:4 * k + 4], WG4[k * 128:(k + 1) * 128, :], "ldc")
    be2rr = [p.sb(f"be2r{e}", [1, D], BF16) for e in range(4)]
    for e in range(4):
        p.dma("sy", be2rr[e][:], BE2[e:e + 1, :], "ldc")
    big = p.sb("big", [128, 32768], BF16)
    of_sb = big
    for h in range(16):
        p.dma("sy", of_sb[:, h * TPC:(h + 1) * TPC], OF[h * 128:(h + 1) * 128, :], "ldc")
    for e in ("te", "sc", "ve"):
        p.wait(e, "ldc", p.cnt["ldc"])

    xbuf = p.sb("xbuf", [128, 4 * 512], F32)
    sqt = p.sb("sqt", [128, 4 * 512], F32)
    abuf = p.sb("abuf", [128, 4 * 512], F32)
    bbuf = p.sb("bbuf", [128, 4 * 512], F32)
    row4 = p.sb("row4", [1, 4 * 512], F32)
    Rb = p.sb("Rb", [128, 512], F32)
    Mb = p.sb("Mb", [128, 512], F32)
    hbuf = p.sb("hbuf", [128, 2 * 512], F32)
    xhb = p.sb("xhb", [128, 2 * 512], F32)
    hln2 = p.sb("hln2", [128, 16 * TPC], BF16)
    wt = [p.sb(f"wt{i}", [128, 16 * 128], BF16) for i in range(2)]
    pb = [p.ps(f"pb{i}", [128, 512]) for i in range(8)]

    ho_ = p.outstream("hoo", 2)
    xho = p.outstream("xho", 2)
    h0st = p.instream("h0st", 2)
    xst = p.instream("xst2", 2)

    ctx = ProjCtx(p, wt, pb[0:4])

    def evac_f1(m, half, ps_, pev):
        hs = half * 512
        p.wait("sc", ctx.pe, pev)
        hsl = ho_.ready("sc")
        p.op("sc", lambda eng, m=m, ps_=ps_, hsl=hsl: eng.activation(
            hbuf[:, hsl * 512:(hsl + 1) * 512], ps_[:], AF.Identity,
            bias=bo_c[:, m:m + 1]), inc=(ctx.ev, 1))
        sl0 = h0st.i % 2
        tok0 = h0st.load("sy", (abuf[:, 2048 + sl0 * 512:2048 + (sl0 + 1) * 512]
                                if False else bbuf[:, sl0 * 512:(sl0 + 1) * 512],
                                H0[m * 128:(m + 1) * 128, hs:hs + 512]))
        h0st.wait("ve", tok0)
        p.wait("ve", ctx.ev, p.cnt[ctx.ev])
        p.op("ve", lambda eng, hsl=hsl, sl0=sl0: eng.tensor_tensor(
            hbuf[:, hsl * 512:(hsl + 1) * 512], hbuf[:, hsl * 512:(hsl + 1) * 512],
            bbuf[:, sl0 * 512:(sl0 + 1) * 512], ALU.add),
            inc=[("hv", 1), h0st.free(tok0)])
        ho_.store("sy", hdram[m * 128:(m + 1) * 128, hs:hs + 512],
                  hbuf[:, hsl * 512:(hsl + 1) * 512], after=("hv", p.cnt["hv"]))
        slx = xst.i % 2
        tokx = xst.load("sy", (bbuf[:, 1024 + slx * 512:1024 + (slx + 1) * 512],
                               xT[m * 128:(m + 1) * 128, hs:hs + 512]))
        xst.wait("ve", tokx)
        xsl = xho.ready("ve")
        p.op("ve", lambda eng, hsl=hsl, slx=slx, xsl=xsl: eng.scalar_tensor_tensor(
            xhb[:, xsl * 512:(xsl + 1) * 512], hbuf[:, hsl * 512:(hsl + 1) * 512],
            0.5, bbuf[:, 1024 + slx * 512:1024 + (slx + 1) * 512],
            ALU.mult, ALU.add), inc=[("xhv", 1), xst.free(tokx)])
        xho.store("sy", xhdram[m * 128:(m + 1) * 128, hs:hs + 512],
                  xhb[:, xsl * 512:(xsl + 1) * 512], after=("xhv", p.cnt["xhv"]))

    emit_proj_fm(p, ctx, WO, NKD, NKD,
                 lambda k, hs: of_sb[:, k * TPC + hs:k * TPC + hs + 512], evac_f1)
    p.wait("te", "xhv", p.cnt["xhv"])
    p.wait("te", "hv", p.cnt["hv"])

    for s_ in range(2):
        p.wait("sy", f"hoos{s_}", p.cnt.get(f"hoos{s_}", 0))
    for half in range(2):
        hs = half * 512
        tag = f"p{half}"
        if half == 1:
            p.wait("sy", "lnpe_p0", p.cnt["lnpe_p0"])
            p.wait("sc", "lnpe_p0", p.cnt["lnpe_p0"])
            p.wait("sy", "xlnap", p.cnt["xlnap"])
        emit_ln_stats(p, tag, lambda kc, hs=hs: hdram[kc * 128:(kc + 1) * 128, hs:hs + 512],
                      512, NKD, (xbuf, sqt), pb[0], pb[1], row4, Rb, Mb,
                      oc, orw, 1e-5)
        emit_ln_apply(p, tag, lambda kc, hs=hs: hdram[kc * 128:(kc + 1) * 128, hs:hs + 512],
                      [hln2[:, kc * TPC + hs:kc * TPC + hs + 512] for kc in range(NKD)],
                      512, NKD, abuf, Rb, Mb,
                      gain=[ln2w[:, kc:kc + 1] for kc in range(NKD)],
                      bias=[ln2b[:, kc:kc + 1] for kc in range(NKD)], app="xlnap")
    p.wait("te", "xlnap", p.cnt["xlnap"])

    lgt_all = p.sb("lgt_all", [128, 32], F32)
    raw_all = p.sb("raw_all", [128, 32], F32)
    m1w = p.sb("m1w", [128, 32], F32)
    m2w = p.sb("m2w", [128, 32], F32)
    comb_all = p.sb("comb_all", [128, 32], F32)
    gax = p.sb("gax", [128, 64], F32)
    crowb = p.sb("crowb", [1, 4 * TPC], BF16)
    cbc = p.sb("cbc", [128, 4 * TPC], BF16)
    mx = gax[:, 0:8]; nmx = gax[:, 8:16]; sm = gax[:, 16:24]; rc = gax[:, 24:32]
    sec = gax[:, 32:40]; dif = gax[:, 40:48]; w1 = gax[:, 48:56]; w2 = gax[:, 56:64]
    for tt in range(8):
        for k in range(NKD):
            p.op("te", lambda eng, k=k, tt=tt: eng.matmul(
                pb[4][:, 4 * tt:4 * tt + 4],
                hln2[:, k * TPC + tt * 128:k * TPC + tt * 128 + 128],
                wg4_sb[:, 4 * k:4 * k + 4], start=(k == 0), stop=(k == NKD - 1)),
                inc=("lgp", 1) if (tt == 7 and k == NKD - 1) else None)
    p.wait("ve", "lgp", p.cnt["lgp"])
    p.op("ve", lambda eng: eng.tensor_copy(lgt_all[:], pb[4][:, 0:32]), inc=("lgv", 1))
    p.wait("ve", "lgv", p.cnt["lgv"])
    lg_g = lgt_all.rearrange("p (t e) -> p t e", t=8)
    raw_g = raw_all.rearrange("p (t e) -> p t e", t=8)
    m2_g = m2w.rearrange("p (t e) -> p t e", t=8)
    p.rop("ve", lambda eng: eng.reduce_max(mx, lg_g, axis=AX.X))
    p.chain("ve")
    p.op("ve", lambda eng: eng.tensor_scalar(nmx, mx, -1.0, None, ALU.mult),
         inc=("g1", 1))
    p.wait("sc", "g1", p.cnt["g1"])
    for tt in range(8):
        p.op("sc", lambda eng, tt=tt: eng.activation(
            raw_all[:, 4 * tt:4 * tt + 4], lgt_all[:, 4 * tt:4 * tt + 4], AF.Exp,
            bias=nmx[:, tt:tt + 1]), inc=("g2", 1) if tt == 7 else None)
    p.wait("ve", "g2", p.cnt["g2"])
    p.rop("ve", lambda eng: eng.reduce_sum(sm, raw_g, axis=AX.X))
    p.chain("ve")
    p.rop("ve", lambda eng: eng.reciprocal(rc, sm))
    p.chain("ve")
    for tt in range(8):
        p.rop("ve", lambda eng, tt=tt: eng.tensor_scalar(
            raw_all[:, 4 * tt:4 * tt + 4], raw_all[:, 4 * tt:4 * tt + 4],
            rc[:, tt:tt + 1], None, ALU.mult))
    p.chain("ve")
    p.rop("ve", lambda eng: eng.reduce_max(mx, raw_g, axis=AX.X))
    p.chain("ve")
    for tt in range(8):
        p.rop("ve", lambda eng, tt=tt: eng.tensor_scalar(
            m1w[:, 4 * tt:4 * tt + 4], raw_all[:, 4 * tt:4 * tt + 4],
            mx[:, tt:tt + 1], None, ALU.is_ge))
    p.chain("ve")
    p.rop("ve", lambda eng: eng.tensor_tensor(m2w[:], raw_all[:], m1w[:], ALU.mult))
    p.chain("ve")
    p.rop("ve", lambda eng: eng.tensor_sub(m2w[:], raw_all[:], m2w[:]))
    p.chain("ve")
    p.rop("ve", lambda eng: eng.reduce_max(sec, m2_g, axis=AX.X))
    p.chain("ve")
    for tt in range(8):
        p.rop("ve", lambda eng, tt=tt: eng.tensor_scalar(
            m2w[:, 4 * tt:4 * tt + 4], m2w[:, 4 * tt:4 * tt + 4],
            sec[:, tt:tt + 1], None, ALU.is_ge))
    p.chain("ve")
    p.op("ve", lambda eng: eng.tensor_sub(dif, mx, sec), inc=("g3", 1))
    p.wait("sc", "g3", p.cnt["g3"])
    p.op("sc", lambda eng: eng.activation(w1, dif, AF.Sigmoid), inc=("g4", 1))
    p.wait("ve", "g4", p.cnt["g4"])
    p.rop("ve", lambda eng: eng.tensor_scalar(w2, w1, -1.0, 1.0, ALU.mult, ALU.add))
    p.chain("ve")
    for tt in range(8):
        p.rop("ve", lambda eng, tt=tt: eng.tensor_scalar(
            m1w[:, 4 * tt:4 * tt + 4], m1w[:, 4 * tt:4 * tt + 4],
            w1[:, tt:tt + 1], None, ALU.mult))
        p.rop("ve", lambda eng, tt=tt: eng.tensor_scalar(
            m2w[:, 4 * tt:4 * tt + 4], m2w[:, 4 * tt:4 * tt + 4],
            w2[:, tt:tt + 1], None, ALU.mult))
    p.chain("ve")
    p.op("ve", lambda eng: eng.tensor_add(comb_all[:], m1w[:], m2w[:]),
         inc=("g5", 1))
    p.wait("te", "g5", p.cnt["g5"])
    crow_g = crowb.rearrange("o (e t) -> o e t", e=4)
    for tt in range(8):
        b = 5 + tt % 2
        if tt >= 2:
            p.wait("te", "ctv", p.cnt["ctv"] - 1)
        for e in range(4):
            p.op("te", lambda eng, e=e, tt=tt, b=b: eng.matmul(
                pb[b][0:1, 128 * e:128 * e + 128], comb_all[:, 4 * tt + e:4 * tt + e + 1],
                idf[:], start=True, stop=True),
                inc=("ctp", 1) if e == 3 else None)
        p.wait("ve", "ctp", p.cnt["ctp"])
        p.op("ve", lambda eng, tt=tt, b=b: eng.tensor_copy(
            crow_g[0:1, :, tt * 128:(tt + 1) * 128],
            pb[b][0:1, 0:512].rearrange("p (e c) -> p e c", e=4)), inc=("ctv", 1))
    p.wait("te", "ctv", p.cnt["ctv"])
    orb2 = p.sb("orb2", [1, 128], BF16)
    p.dma("sy", orb2[:], p.inline(np.ones((1, 128), BF), "onesrb2")[:], "cb0")
    p.wait("te", "cb0", p.cnt["cb0"])
    for e in range(4):
        for half in range(2):
            p.op("te", lambda eng, e=e, half=half: eng.matmul(
                pb[4][:, 0:512], orb2[0:1, :],
                crowb[0:1, e * TPC + half * 512:e * TPC + (half + 1) * 512],
                start=True, stop=True), inc=("cbp", 1))
            p.wait("ve", "cbp", p.cnt["cbp"])
            p.op("ve", lambda eng, e=e, half=half: eng.tensor_copy(
                cbc[:, e * TPC + half * 512:e * TPC + (half + 1) * 512],
                pb[4][:, 0:512]), inc=("cbv", 1))
            p.wait("te", "cbv", p.cnt["cbv"])
    p.wait("ve", "cbv", p.cnt["cbv"])

    hid = big
    w1st = p.instream("w1st", 2)
    w2st = p.instream("w2st", 3)
    wt2 = [p.sb(f"wt2{i}", [128, 2048], BF16) for i in range(3)]
    xhst = p.instream("xhst", 2)
    yo = p.outstream("yo2", 2)
    yb2 = xhb
    for s_ in range(2):
        p.wait("sy", f"xhos{s_}", p.cnt.get(f"xhos{s_}", 0))
    p.wait("ve", "xhv", p.cnt["xhv"])
    for s_ in range(2):
        p.wait("ve", f"xhos{s_}", p.cnt.get(f"xhos{s_}", 0))
    for half in range(2):
        hs = half * 512
        for e in range(4):
            for mi in range(NKD):
                b = w1st.i % 2
                tokw = w1st.load("sy", (wt[b][:], WE1[e, mi]))
                w1st.wait("te", tokw)
                gi = (half * 4 + e) * NKD + mi
                ps_ = pb[gi % 4]
                if gi >= 4:
                    p.wait("te", "hev", gi - 3)
                for k in range(NKD):
                    incs = []
                    if k == NKD - 1:
                        incs = [("hpe", 1), w1st.free(tokw)]
                    p.op("te", lambda eng, k=k, b=b, hs=hs, ps_=ps_: eng.matmul(
                        ps_[:], wt[b][:, k * 128:(k + 1) * 128],
                        hln2[:, k * TPC + hs:k * TPC + hs + 512],
                        start=(k == 0), stop=(k == NKD - 1)),
                        inc=incs if incs else None)
                p.wait("sc", "hpe", p.cnt["hpe"])
                p.op("sc", lambda eng, ps_=ps_, e=e, mi=mi: eng.activation(
                    hid[:, (e * 16 + mi) * 512:(e * 16 + mi + 1) * 512], ps_[:],
                    AF.Gelu, bias=be1_c[:, e * 16 + mi:e * 16 + mi + 1]),
                    inc=("hsc", 1))
                p.wait("ve", "hsc", p.cnt["hsc"])
                p.op("ve", lambda eng, e=e, mi=mi, hs=hs: eng.tensor_tensor(
                    hid[:, (e * 16 + mi) * 512:(e * 16 + mi + 1) * 512],
                    hid[:, (e * 16 + mi) * 512:(e * 16 + mi + 1) * 512],
                    cbc[:, e * TPC + hs:e * TPC + hs + 512], ALU.mult),
                    inc=("hev", 1))
        p.wait("te", "hev", p.cnt["hev"])
        for dci in range(NKD):
            gd = half * NKD + dci
            pd = pb[4 + gd % 2]
            toks = []
            for e in range(4):
                b = w2st.i % 3
                toks.append((e, b, w2st.load("sy", (wt2[b][:], WE2[e, dci]))))
            if gd >= 2:
                p.wait("te", "mev", gd - 1)
            for e, b, tokw in toks:
                w2st.wait("te", tokw)
                for k in range(NKD):
                    incs = []
                    if k == NKD - 1:
                        incs = [w2st.free(tokw)]
                    p.op("te", lambda eng, k=k, b=b, e=e, hs=hs, pd=pd, gd=gd: eng.matmul(
                        pd[:], wt2[b][:, k * 128:(k + 1) * 128],
                        hid[:, (e * 16 + k) * 512:(e * 16 + k + 1) * 512],
                        start=(e == 0 and k == 0), stop=False),
                        inc=incs if incs else None)
            for e in range(4):
                p.op("te", lambda eng, e=e, pd=pd, hs=hs, dci=dci: eng.matmul(
                    pd[:], be2rr[e][0:1, dci * 128:(dci + 1) * 128],
                    crowb[0:1, e * TPC + hs:e * TPC + hs + 512],
                    start=False, stop=(e == 3)), inc=("mpe", 1) if e == 3 else None)
            slx = xhst.i % 2
            tokx = xhst.load("sy", (bbuf[:, 1024 + slx * 512:1024 + (slx + 1) * 512],
                                    xhdram[dci * 128:(dci + 1) * 128, hs:hs + 512]))
            xhst.wait("ve", tokx)
            p.wait("ve", "mpe", p.cnt["mpe"])
            ysl = yo.ready("ve")
            p.op("ve", lambda eng, pd=pd, slx=slx, ysl=ysl: eng.scalar_tensor_tensor(
                yb2[:, ysl * 512:(ysl + 1) * 512], pd[:], 0.5,
                bbuf[:, 1024 + slx * 512:1024 + (slx + 1) * 512],
                ALU.mult, ALU.add), inc=[("mev", 1), xhst.free(tokx)])
            yo.store("sy", xoT[dci * 128:(dci + 1) * 128, hs:hs + 512],
                     yb2[:, ysl * 512:(ysl + 1) * 512], after=("mev", p.cnt["mev"]))

    p.finish("gp")
    p.finalize()
    return p


def build_la_pre():
    p = Prog("lapre")
    xT = p.din("xT", [D, TPC], F32)
    WQl = p.din("WQl", [16, 128, D], BF16)
    WKl = p.din("WKl", [16, 128, D], BF16)
    WVl = p.din("WVl", [16, 128, D], BF16)
    WF = p.din("WF", [D, 16], BF16)
    OPART = p.dout("OPART", [TPC, D], BF16)
    QS = p.dout("QS", [16, 128, 1024], BF16)
    PD = p.dout("PD", [16, 16], F32)
    SLOC = p.dout("SLOC", [16, 128, 128], F32)
    TOTD = p.dout("TOTD", [1, 16], F32)
    cumd = p.dtmp("cumd", [16, 16, 64], F32)

    oc = p.sb("oc", [128, 1], F32)
    orw = p.sb("orw", [1, 128], F32)
    idb = p.sb("idb", [128, 128], BF16)
    id64 = p.sb("id64", [64, 64], F32)
    triu = p.sb("triu", [64, 64], F32)
    triub = p.sb("triub", [64, 64], BF16)
    p.dma("sy", oc[:], p.inline(np.ones((128, 1), np.float32), "onesc")[:], "ldc")
    p.dma("sy", orw[:], p.inline(np.ones((1, 128), np.float32), "onesr")[:], "ldc")
    p.dma("sy", idb[:], p.inline(np.eye(128, dtype=BF), "idb_")[:], "ldc")
    p.dma("sy", id64[:], p.inline(np.eye(64, dtype=np.float32), "id64_")[:], "ldc")
    tri_np = (np.arange(64)[:, None] <= np.arange(64)[None, :])
    p.dma("sy", triu[:], p.inline(tri_np.astype(np.float32), "triu_")[:], "ldc")
    p.dma("sy", triub[:], p.inline(tri_np.astype(BF), "triub_")[:], "ldc")
    p.consts([1e-5, 1.0])
    wf_sb = p.sb("wf_sb", [128, 16 * 16], BF16)
    for k in range(NKD):
        p.dma("sy", wf_sb[:, 16 * k:16 * k + 16], WF[k * 128:(k + 1) * 128, :], "ldc")
    for e in ("te", "sc", "ve"):
        p.wait(e, "ldc", p.cnt["ldc"])

    xbuf = p.sb("xbuf", [128, 4 * 512], F32)
    sqt = p.sb("sqt", [128, 4 * 512], F32)
    abuf = p.sb("abuf", [128, 4 * 512], F32)
    row4 = p.sb("row4", [1, 4 * 512], F32)
    Rb = p.sb("Rb", [128, 512], F32)
    Mb = p.sb("Mb", [128, 512], F32)
    xln = p.sb("xln", [128, 16 * TPC], BF16)
    q_sb = p.sb("q_sb", [128, 16 * TPC], BF16)
    k_sb = p.sb("k_sb", [128, 16 * TPC], BF16)
    v_sb = p.sb("v_sb", [128, 16 * TPC], BF16)
    wt = [p.sb(f"wt{i}", [128, 16 * 128], BF16) for i in range(2)]
    pb = [p.ps(f"pb{i}", [128, 512]) for i in range(8)]

    for half in range(2):
        hs = half * 512
        tag = f"g{half}"
        if half == 1:
            p.wait("sy", "lnpe_g0", p.cnt["lnpe_g0"])
            p.wait("sc", "lnpe_g0", p.cnt["lnpe_g0"])
            p.wait("sy", "xlnap", p.cnt["xlnap"])
        emit_ln_stats(p, tag, lambda kc, hs=hs: xT[kc * 128:(kc + 1) * 128, hs:hs + 512],
                      512, NKD, (xbuf, sqt), pb[0], pb[1], row4, Rb, Mb,
                      oc, orw, 1e-5)
        emit_ln_apply(p, tag, lambda kc, hs=hs: xT[kc * 128:(kc + 1) * 128, hs:hs + 512],
                      [xln[:, kc * TPC + hs:kc * TPC + hs + 512] for kc in range(NKD)],
                      512, NKD, abuf, Rb, Mb, app="xlnap")
    p.wait("te", "xlnap", p.cnt["xlnap"])

    ctx = ProjCtx(p, wt, pb[0:4])

    def mk_evac(dst, act):
        def evac(m, half, ps_, pev):
            hs = half * 512
            p.wait("sc", ctx.pe, pev)
            p.op("sc", lambda eng, m=m, hs=hs, ps_=ps_: eng.activation(
                dst[:, m * TPC + hs:m * TPC + hs + 512], ps_[:], act),
                inc=(ctx.ev, 1))
        return evac

    emit_proj_fm(p, ctx, WQl, NKD, NKD,
                 lambda k, hs: xln[:, k * TPC + hs:k * TPC + hs + 512],
                 mk_evac(q_sb, AF.Silu))
    emit_proj_fm(p, ctx, WKl, NKD, NKD,
                 lambda k, hs: xln[:, k * TPC + hs:k * TPC + hs + 512],
                 mk_evac(k_sb, AF.Silu))
    emit_proj_fm(p, ctx, WVl, NKD, NKD,
                 lambda k, hs: xln[:, k * TPC + hs:k * TPC + hs + 512],
                 mk_evac(v_sb, AF.Identity))
    p.wait("te", ctx.ev, p.cnt[ctx.ev])

    ldc_ = p.sb("ldc_", [64, 16], F32)
    cumt = p.sb("cumt", [64, 16], F32)
    negcum = p.sb("negcum", [64, 16], F32)
    cumTs = p.sb("cumTs", [16, 64], F32)
    cumrow = p.sb("cumrow", [1, 1024], F32)
    erow = p.sb("erow", [1, 1024], F32)
    totrow = p.sb("totrow", [1, 16], F32)
    ecb_all = sqt[:, 0:1024]
    xb16 = xbuf[:].bitcast(BF16)
    ab16 = abuf[:].bitcast(BF16)
    sq16 = sqt[:].bitcast(BF16)
    Dm_all = xb16[0:64, 0:1024]
    W_all = xb16[0:64, 1024:2048]
    kdt_all = ab16[0:64, 0:2048]
    vtt_all = ab16[0:64, 2048:4096]
    triu16 = sq16[0:64, 2048:3072]
    qs_t = p.sb("qs_t", [128, 16 * 64], BF16)
    ecol_all = p.sb("ecol_all", [64, 16], F32)
    etb = p.sb("etb", [128, 16], F32)
    S_sb = p.sb("S_sb", [128, 16 * 128], F32)
    S_bf = p.sb("S_bf", [128, 16 * 128], BF16)
    ptot = p.sb("ptot", [1, 3 * 16], F32)
    oc_m = p.sb("oc_m", [64, 2 * 2048], BF16)
    ldsc = p.sb("ldsc", [64, 3 * 16], F32)
    p.wait("sy", "lnpe_g0", p.cnt.get("lnpe_g0", 0))
    p.wait("sy", "lnpe_g1", p.cnt.get("lnpe_g1", 0))
    p.wait("sy", "lnsq_g0", p.cnt.get("lnsq_g0", 0))
    p.wait("sy", "lnsq_g1", p.cnt.get("lnsq_g1", 0))
    p.wait("sy", "xlnap", p.cnt.get("xlnap", 0))
    p.dma("sy", triu16, p.inline(
        np.tile(tri_np.astype(BF), (1, 16)), "triu16_")[0:64, :], "ld2")
    p.wait("ve", "ld2", p.cnt.get("ld2", 0))
    p.wait("sc", "ld2", p.cnt.get("ld2", 0))

    p.rop("ve", lambda eng: eng.memset(S_sb[:], 0.0))
    p.rop("ve", lambda eng: eng.memset(S_bf[:], 0.0))
    p.op("ve", lambda eng: eng.memset(ptot[:], 0.0), inc=("sz", 1))
    p.wait("ve", "sz", p.cnt.get("sz", 0))
    p.wait("te", "sz", p.cnt.get("sz", 0))
    p.wait("sy", "sz", p.cnt.get("sz", 0))

    om = p.outstream("om", 2)
    qso = p.outstream("qso", 2)

    for m in range(16):
        t0_ = m * 64
        if m >= 1:
            p.wait("te", "sbv", p.cnt.get("sbv", 0))
        for k in range(NKD):
            p.op("te", lambda eng, k=k, t0_=t0_: eng.matmul(
                pb[2][0:64, 0:16], xln[:, k * TPC + t0_:k * TPC + t0_ + 64],
                wf_sb[:, 16 * k:16 * k + 16], start=(k == 0), stop=(k == NKD - 1)),
                inc=("ldp", 1) if k == NKD - 1 else None)
        p.wait("sc", "ldp", p.cnt.get("ldp", 0))
        if m >= 1:
            p.wait("sc", "cumv", p.cnt.get("cumv", 0))
        p.op("sc", lambda eng: eng.activation(ldsc[:, 0:16], pb[2][0:64, 0:16], AF.Abs),
             inc=("lsg", 1))
        p.wait("sc", "lsg", p.cnt.get("lsg", 0))
        p.rop("sc", lambda eng: eng.activation(ldsc[:, 16:32], ldsc[:, 0:16], AF.Exp, scale=-1.0))
        p.chain("sc")
        p.op("sc", lambda eng: eng.activation(ldsc[:, 16:32], ldsc[:, 16:32], AF.Ln, bias=1.0),
             inc=("lsg2", 1))
        p.wait("ve", "lsg2", p.cnt.get("lsg2", 0))
        p.rop("ve", lambda eng: eng.tensor_scalar(ldsc[:, 32:48], pb[2][0:64, 0:16], 0.0, None, ALU.min))
        p.chain("ve")
        p.op("ve", lambda eng: eng.tensor_sub(ldc_[:], ldsc[:, 32:48], ldsc[:, 16:32]),
             inc=("ldv", 1))
        p.wait("te", "ldv", p.cnt.get("ldv", 0))
        p.op("te", lambda eng: eng.matmul(pb[2][0:64, 32:48], triu[:], ldc_[:],
                                          start=True, stop=True), inc=("cump", 1))
        p.wait("ve", "cump", p.cnt.get("cump", 0))
        p.op("ve", lambda eng: eng.tensor_copy(cumt[:], pb[2][0:64, 32:48]), inc=("cumv", 1))
        p.wait("te", "cumv", p.cnt.get("cumv", 0))
        p.op("te", lambda eng: eng.matmul(pb[2][0:16, 64:128], cumt[:], id64[:],
                                          start=True, stop=True), inc=("cum2", 1))
        p.wait("ve", "dmv", p.cnt.get("dmv", 0))
        p.op("ve", lambda eng: eng.tensor_scalar(negcum[:], cumt[:], -1.0, None, ALU.mult),
             inc=("negv", 1))
        p.wait("ve", "cum2", p.cnt.get("cum2", 0))
        p.op("ve", lambda eng: eng.tensor_copy(cumTs[:], pb[2][0:16, 64:128]), inc=("cum3", 1))
        p.wait("sy", "cum3", p.cnt.get("cum3", 0))
        p.dma("sy", cumd[m], cumTs[:], "cbA")
        p.wait("sy", "cbA", p.cnt["cbA"])
        if m >= 1:
            p.wait("sy", "bcp", p.cnt.get("bcp", 0))
            p.wait("sy", "ptv", p.cnt.get("ptv", 0))
            p.wait("sy", "erw", p.cnt.get("erw", 0))
            p.wait("sy", "totv", p.cnt.get("totv", 0))
        p.dma("sy", cumrow[0:1, :], cumd.rearrange("m h i -> m (h i)")[m:m + 1, :], "cbB")
        p.wait("sc", "cbB", p.cnt["cbB"])
        if m >= 1:
            p.wait("sc", "ecp", p.cnt.get("ecp", 0))
        p.op("sc", lambda eng: eng.activation(erow[0:1, :], cumrow[0:1, :], AF.Exp),
             inc=("erw", 1))
        p.wait("ve", "cbB", p.cnt["cbB"])
        if m >= 1:
            p.wait("ve", "bcp", p.cnt.get("bcp", 0))
        p.op("ve", lambda eng: eng.tensor_copy(
            totrow[:], cumrow[0:1, :].rearrange("o (h i) -> o h i", h=16)[0:1, :, 63]),
            inc=("totv", 1))
        p.wait("ve", "pdS", 16 * m)
        p.rop("ve", lambda eng: eng.tensor_copy(ptot[0:1, 16:32], ptot[0:1, 0:16]))
        p.chain("ve")
        p.wait("ve", "totv", p.cnt.get("totv", 0))
        p.op("ve", lambda eng: eng.tensor_tensor(
            ptot[0:1, 0:16], ptot[0:1, 0:16], totrow[:], ALU.add), inc=("ptv", 1))
        p.wait("sy", "ptv", p.cnt.get("ptv", 0))
        p.dma("sy", PD[m:m + 1, :], ptot[0:1, 16:32], "pdS")
        p.wait("te", "erw", p.cnt.get("erw", 0))
        p.op("te", lambda eng: eng.matmul(pb[0][:, 0:512], orw[0:1, :],
                                          erow[0:1, 0:512], start=True, stop=True))
        p.op("te", lambda eng: eng.matmul(pb[1][:, 0:512], orw[0:1, :],
                                          erow[0:1, 512:1024], start=True, stop=True),
             inc=("ecp", 1))
        p.wait("ve", "ecp", p.cnt.get("ecp", 0))
        p.rop("ve", lambda eng: eng.tensor_copy(ecb_all[:, 0:512], pb[0][:, 0:512]))
        p.op("ve", lambda eng: eng.tensor_copy(ecb_all[:, 512:1024], pb[1][:, 0:512]),
             inc=("ecv", 1))
        p.wait("ve", "ecv", p.cnt.get("ecv", 0))
        p.wait("ve", "opb", p.cnt.get("opb", 0))
        p.wait("ve", "qsos0", p.cnt.get("qsos0", 0))
        p.wait("ve", "qsos1", p.cnt.get("qsos1", 0))
        for h in range(16):
            p.op("ve", lambda eng, h=h, t0_=t0_: eng.tensor_tensor(
                qs_t[:, h * 64:(h + 1) * 64], q_sb[:, h * TPC + t0_:h * TPC + t0_ + 64],
                ecb_all[:, h * 64:(h + 1) * 64], ALU.mult),
                inc=("qsv", 1) if h == 15 else None)
        if m >= 1:
            p.wait("te", "ocv", p.cnt.get("ocv", 0))
        for h in range(16):
            dst = pb[4 + h // 8][0:64, 64 * (h % 8):64 * (h % 8) + 64]
            p.op("te", lambda eng, dst=dst, h=h, t0_=t0_: eng.matmul(
                dst, k_sb[:, h * TPC + t0_:h * TPC + t0_ + 64],
                q_sb[:, h * TPC + t0_:h * TPC + t0_ + 64], start=True, stop=True),
                inc=("scp", 1) if h == 15 else None)
        p.wait("te", "totv", p.cnt.get("totv", 0))
        p.op("te", lambda eng: eng.matmul(pb[6][0:64, 0:512], orw[0:1, 0:64],
                                          cumrow[0:1, 0:512], start=True, stop=True))
        p.op("te", lambda eng: eng.matmul(pb[7][0:64, 0:512], orw[0:1, 0:64],
                                          cumrow[0:1, 512:1024], start=True, stop=True))
        p.op("te", lambda eng: eng.matmul(pb[3][0:64, 0:16], orw[0:1, 0:64],
                                          totrow[0:1, :], start=True, stop=True))
        p.op("te", lambda eng: eng.matmul(pb[3][0:128, 16:32], orw[0:1, :],
                                          totrow[0:1, :], start=True, stop=True),
             inc=("bcp", 1))
        p.wait("sc", "bcp", p.cnt.get("bcp", 0))
        p.wait("sc", "negv", p.cnt.get("negv", 0))
        if m >= 1:
            p.wait("sc", "wtv", p.cnt.get("wtv", 0))
        for h in range(16):
            src = pb[6 + h // 8][0:64, 64 * (h % 8):64 * (h % 8) + 64]
            p.op("sc", lambda eng, src=src, h=h: eng.activation(
                Dm_all[:, h * 64:(h + 1) * 64], src, AF.Exp,
                bias=negcum[:, h:h + 1]), inc=("dmv", 1) if h == 15 else None)
        p.wait("ve", "bcp", p.cnt.get("bcp", 0))
        if m >= 1:
            p.wait("ve", "kdv", p.cnt.get("kdv", 0))
        p.op("ve", lambda eng: eng.tensor_sub(ecol_all[:], pb[3][0:64, 0:16], cumt[:]),
             inc=("eclv", 1))
        p.wait("sc", "eclv", p.cnt.get("eclv", 0))
        p.op("sc", lambda eng: eng.activation(ecol_all[:], ecol_all[:], AF.Exp),
             inc=("ecls", 1))
        if m >= 1:
            p.wait("sc", "supv", p.cnt.get("supv", 0))
        p.op("sc", lambda eng: eng.activation(etb[:], pb[3][0:128, 16:32], AF.Exp),
             inc=("etbs", 1))
        p.wait("ve", "scp", p.cnt.get("scp", 0))
        p.wait("ve", "dmv", p.cnt.get("dmv", 0))
        p.rop("ve", lambda eng: eng.tensor_tensor(
            W_all[:, 0:512], pb[4][0:64, 0:512], Dm_all[:, 0:512], ALU.mult))
        p.rop("ve", lambda eng: eng.tensor_tensor(
            W_all[:, 512:1024], pb[5][0:64, 0:512], Dm_all[:, 512:1024], ALU.mult))
        p.chain("ve")
        p.rop("ve", lambda eng: eng.tensor_tensor(
            W_all[:, 0:512], W_all[:, 0:512], triu16[:, 0:512], ALU.mult))
        p.op("ve", lambda eng: eng.tensor_tensor(
            W_all[:, 512:1024], W_all[:, 512:1024], triu16[:, 512:1024], ALU.mult),
            inc=("wtv", 1))
        p.wait("te", "ecv", p.cnt.get("ecv", 0))
        p.wait("te", "cum3", p.cnt.get("cum3", 0))
        p.wait("te", "lsg", p.cnt.get("lsg", 0))
        p.wait("te", "eclv", p.cnt.get("eclv", 0))
        p.wait("te", "etbs", p.cnt.get("etbs", 0))
        for h in range(16):
            dst = pb[h // 4][0:64, 128 * (h % 4):128 * (h % 4) + 128]
            p.op("te", lambda eng, dst=dst, h=h, t0_=t0_: eng.matmul(
                dst, k_sb[:, h * TPC + t0_:h * TPC + t0_ + 64], idb[:],
                start=True, stop=True), inc=("ktp", 1) if h == 15 else None)
        p.wait("ve", "ktp", p.cnt.get("ktp", 0))
        p.wait("ve", "ecls", p.cnt.get("ecls", 0))
        for h in range(16):
            src = pb[h // 4][0:64, 128 * (h % 4):128 * (h % 4) + 128]
            p.op("ve", lambda eng, src=src, h=h: eng.tensor_scalar(
                kdt_all[:, h * 128:(h + 1) * 128], src, ecol_all[:, h:h + 1],
                None, ALU.mult), inc=("kdv", 1) if h == 15 else None)
        p.wait("te", "kdv", p.cnt.get("kdv", 0))
        for h in range(16):
            dst = pb[h // 4][0:64, 128 * (h % 4):128 * (h % 4) + 128]
            p.op("te", lambda eng, dst=dst, h=h, t0_=t0_: eng.matmul(
                dst, v_sb[:, h * TPC + t0_:h * TPC + t0_ + 64], idb[:],
                start=True, stop=True), inc=("vtp", 1) if h == 15 else None)
        p.wait("ve", "vtp", p.cnt.get("vtp", 0))
        for q in range(4):
            p.op("ve", lambda eng, q=q: eng.tensor_copy(
                vtt_all[:, q * 512:(q + 1) * 512], pb[q][0:64, 0:512]),
                inc=("vtv", 1) if q == 3 else None)
        osl = om.ready("ve")
        p.wait("te", "wtv", p.cnt.get("wtv", 0))
        p.wait("te", "qsv", p.cnt.get("qsv", 0))
        p.wait("te", "vtv", p.cnt.get("vtv", 0))
        for h in range(16):
            dstb = pb[4 + h // 4][0:64, 128 * (h % 4):128 * (h % 4) + 128]
            p.rop("te", lambda eng, dstb=dstb, h=h: eng.matmul(
                dstb, W_all[:, h * 64:(h + 1) * 64], vtt_all[:, h * 128:(h + 1) * 128],
                start=True, stop=False))
            p.op("te", lambda eng, dstb=dstb, h=h: eng.matmul(
                dstb, qs_t[:, h * 64:(h + 1) * 64], S_bf[:, h * 128:(h + 1) * 128],
                start=False, stop=True), inc=("opb", 1) if h == 15 else None)
        p.wait("ve", "opb", p.cnt.get("opb", 0))
        for q in range(4):
            p.op("ve", lambda eng, q=q, osl=osl: eng.tensor_copy(
                oc_m[:, osl * 2048 + q * 512:osl * 2048 + (q + 1) * 512],
                pb[4 + q][0:64, 0:512]), inc=("ocv", 1) if q == 3 else None)
        om.store("sy", OPART[t0_:t0_ + 64, :], oc_m[:, osl * 2048:(osl + 1) * 2048],
                 after=("ocv", p.cnt.get("ocv", 0)))
        qso.ready("sy")
        qso.store("sy", QS[m], qs_t[:], after=("qsv", p.cnt.get("qsv", 0)))
        for h in range(16):
            dstb = pb[h // 4][:, 128 * (h % 4):128 * (h % 4) + 128]
            p.op("te", lambda eng, dstb=dstb, h=h: eng.matmul(
                dstb, kdt_all[:, h * 128:(h + 1) * 128], vtt_all[:, h * 128:(h + 1) * 128],
                start=True, stop=True), inc=("sup", 1) if h == 15 else None)
        p.wait("ve", "sup", p.cnt.get("sup", 0))
        p.wait("ve", "etbs", p.cnt.get("etbs", 0))
        for h in range(16):
            src = pb[h // 4][:, 128 * (h % 4):128 * (h % 4) + 128]
            p.op("ve", lambda eng, src=src, h=h: eng.scalar_tensor_tensor(
                S_sb[:, h * 128:(h + 1) * 128], S_sb[:, h * 128:(h + 1) * 128],
                etb[:, h:h + 1], src, ALU.mult, ALU.add),
                inc=("supv", 1) if h == 15 else None)
        p.wait("ve", "supv", p.cnt.get("supv", 0))
        p.op("ve", lambda eng: eng.tensor_copy(S_bf[:], S_sb[:]), inc=("sbv", 1))
    p.wait("sy", "sbv", p.cnt["sbv"])
    for h in range(16):
        p.dma("sy", SLOC[h], S_sb[:, h * 128:(h + 1) * 128], "slst")
    p.wait("sy", "ptv", p.cnt["ptv"])
    p.dma("sy", TOTD[:], ptot[0:1, 0:16], "slst")
    p.finish("gp")
    p.finalize()
    return p


def build_la_post():
    p = Prog("lapost")
    OPART = p.din("OPART", [TPC, D], BF16)
    QS = p.din("QS", [16, 128, 1024], BF16)
    PD = p.din("PD", [16, 16], F32)
    SLA = p.din("SLA", [4, 16, 128, 128], F32)
    TOTA = p.din("TOTA", [4, 16], F32)
    MM = p.din("MM", [4, 4], F32)
    G4 = p.din("G4", [1, 4], F32)
    NORMW = p.din("NORMW", [1, D], F32)
    WOl = p.din("WOl", [16, 128, D], BF16)
    xT = p.din("xT", [D, TPC], F32)
    xoT = p.dout("xoT", [D, TPC], F32)
    wgtd = p.dtmp("wgtd", [16, 4], F32)

    oc = p.sb("oc", [128, 1], F32)
    orw = p.sb("orw", [1, 128], F32)
    idb = p.sb("idb", [128, 128], BF16)
    p.dma("sy", oc[:], p.inline(np.ones((128, 1), np.float32), "onesc")[:], "ldc")
    p.dma("sy", orw[:], p.inline(np.ones((1, 128), np.float32), "onesr")[:], "ldc")
    p.dma("sy", idb[:], p.inline(np.eye(128, dtype=BF), "idb_")[:], "ldc")
    p.consts([1e-6])
    tot_sb = p.sb("tot_sb", [4, 16], F32)
    mm_sb = p.sb("mm_sb", [4, 4], F32)
    g4_sb = p.sb("g4_sb", [1, 4], F32)
    pdrow = p.sb("pdrow", [1, 256], F32)
    p.dma("sy", tot_sb[:], TOTA[:], "ldc")
    p.dma("sy", mm_sb[:], MM[:], "ldc")
    p.dma("sy", g4_sb[:], G4[:], "ldc")
    p.dma("sy", pdrow[:], PD.rearrange("(o m) h -> o (m h)", o=1), "ldc")
    nww = p.sb("nww", [1, D], F32)
    p.dma("sy", nww[:], NORMW[:], "ldc")
    for e in ("te", "sc", "ve"):
        p.wait(e, "ldc", p.cnt["ldc"])

    lwt = p.sb("lwt", [16, 4], F32)
    gb16 = p.sb("gb16", [16, 4], F32)
    wrow = p.sb("wrow", [1, 64], F32)
    wb = p.sb("wb", [128, 64], F32)
    epd = p.sb("epd", [64, 256], F32)
    Sin = p.sb("Sin", [128, 16 * 128], F32)
    Sin_bf = p.sb("Sin_bf", [128, 16 * 128], BF16)
    slb = p.sb("slb", [128, 2 * 2048], F32)
    qsb = p.sb("qsb", [128, 2 * 1024], BF16)
    opb = p.sb("opb", [64, 2 * 2048], BF16)
    o_m = p.sb("o_m", [64, 2048], F32)
    on_bf = p.sb("on_bf", [64, 2048], BF16)
    sqf32 = p.sb("sqf32", [64, 2048], F32)
    rms = p.sb("rms", [64, 4], F32)
    normwb = p.sb("normwb", [128, D], F32)
    of_full = p.sb("of_full", [128, 16 * TPC], BF16)
    wt = [p.sb(f"wt{i}", [128, 16 * 128], BF16) for i in range(2)]
    xrb = p.sb("xrb", [128, 2 * 512], F32)
    yb = p.sb("yb", [128, 2 * 512], F32)

    pb = [p.ps(f"pb{i}", [128, 512]) for i in range(6)]
    pT2 = p.ps("pT2", [128, 128])
    pW = p.ps("pW", [16, 64])

    p.op("te", lambda eng: eng.matmul(pW[0:16, 0:4], tot_sb[:], mm_sb[:],
                                      start=True, stop=True), inc=("w1", 1))
    p.op("te", lambda eng: eng.matmul(pW[0:16, 4:8], orw[0:1, 0:16], g4_sb[0:1, :],
                                      start=True, stop=True), inc=("w1", 1))
    p.wait("ve", "w1", p.cnt["w1"])
    p.rop("ve", lambda eng: eng.tensor_copy(lwt[:], pW[0:16, 0:4]))
    p.op("ve", lambda eng: eng.tensor_copy(gb16[:], pW[0:16, 4:8]), inc=("w2", 1))
    p.wait("ve", "w2", p.cnt["w2"])
    p.wait("sc", "w2", p.cnt["w2"])
    p.op("sc", lambda eng: eng.activation(lwt[:], lwt[:], AF.Exp), inc=("w3", 1))
    p.wait("ve", "w3", p.cnt["w3"])
    p.op("ve", lambda eng: eng.tensor_tensor(lwt[:], lwt[:], gb16[:], ALU.mult),
         inc=("w4", 1))
    p.wait("sy", "w4", p.cnt["w4"])
    p.dma("sy", wgtd[:], lwt[:], "wb1")
    p.wait("sy", "wb1", p.cnt["wb1"])
    p.dma("sy", wrow[:], wgtd.rearrange("(o h) r -> o (h r)", o=1), "wb2")
    p.wait("te", "wb2", p.cnt["wb2"])
    p.op("te", lambda eng: eng.matmul(pT2[:, 0:64], orw[0:1, :], wrow[0:1, :],
                                      start=True, stop=True), inc=("w5", 1))
    p.wait("ve", "w5", p.cnt["w5"])
    p.op("ve", lambda eng: eng.tensor_copy(wb[:], pT2[:, 0:64]), inc=("w6", 1))
    p.wait("ve", "w6", p.cnt["w6"])
    p.op("te", lambda eng: eng.matmul(pb[0][0:64, 0:256], orw[0:1, 0:64], pdrow[0:1, :],
                                      start=True, stop=True), inc=("w7", 1))
    p.wait("sc", "w7", p.cnt["w7"])
    p.op("sc", lambda eng: eng.activation(epd[:], pb[0][0:64, 0:256], AF.Exp),
         inc=("w8", 1))
    p.wait("ve", "w8", p.cnt["w8"])
    for half in range(4):
        p.op("te", lambda eng, half=half: eng.matmul(
            pb[1][:, 0:512], orw[0:1, :], nww[0:1, half * 512:(half + 1) * 512],
            start=True, stop=True), inc=("w9", 1))
        p.wait("ve", "w9", p.cnt["w9"])
        p.op("ve", lambda eng, half=half: eng.tensor_copy(
            normwb[:, half * 512:(half + 1) * 512], pb[1][:, 0:512]), inc=("w10", 1))
        p.wait("te", "w10", p.cnt["w10"])

    sls = p.instream("sls", 2)
    p.rop("ve", lambda eng: eng.memset(Sin[:], 0.0))
    p.op("ve", lambda eng: eng.memset(Sin_bf[:], 0.0), inc=("sz", 1))
    p.wait("ve", "sz", p.cnt["sz"])
    for r in range(4):
        sl = sls.i % 2
        tok = sls.load("sy", [(slb[:, sl * 2048 + h * 128:sl * 2048 + (h + 1) * 128],
                               SLA[r, h]) for h in range(16)])
        sls.wait("ve", tok)
        for h in range(16):
            incs = [("siv", 1), sls.free(tok)] if h == 15 else None
            p.op("ve", lambda eng, h=h, r=r, sl=sl: eng.scalar_tensor_tensor(
                Sin[:, h * 128:(h + 1) * 128],
                slb[:, sl * 2048 + h * 128:sl * 2048 + (h + 1) * 128],
                wb[:, 4 * h + r:4 * h + r + 1],
                Sin[:, h * 128:(h + 1) * 128], ALU.mult, ALU.add), inc=incs)
        p.wait("ve", "siv", p.cnt["siv"])
    p.op("ve", lambda eng: eng.tensor_copy(Sin_bf[:], Sin[:]), inc=("sbv", 1))
    p.wait("te", "sbv", p.cnt["sbv"])

    qst = p.instream("qst", 2)
    ops = p.instream("ops", 2)
    p.wait("te", "w8", p.cnt["w8"])
    p.wait("te", "w10", p.cnt["w10"])
    for m in range(16):
        t0_ = m * 64
        sq = qst.i % 2
        tokq = qst.load("sy", (qsb[:, sq * 1024:(sq + 1) * 1024], QS[m]))
        so = ops.i % 2
        toko = ops.load("sy", (opb[:, so * 2048:(so + 1) * 2048], OPART[t0_:t0_ + 64, :]))
        qst.wait("te", tokq)
        if m >= 1:
            p.wait("te", "oev", p.cnt["oev"])
        for h in range(16):
            dst = pb[h // 4][0:64, 128 * (h % 4):128 * (h % 4) + 128]
            incs = [("oep", 1)] + ([qst.free(tokq)] if True else []) if h == 15 else None
            p.op("te", lambda eng, dst=dst, h=h, sq=sq: eng.matmul(
                dst, qsb[:, sq * 1024 + h * 64:sq * 1024 + (h + 1) * 64],
                Sin_bf[:, h * 128:(h + 1) * 128], start=True, stop=True), inc=incs)
        ops.wait("ve", toko)
        p.wait("ve", "oep", p.cnt["oep"])
        for h in range(16):
            src_ = pb[h // 4][0:64, 128 * (h % 4):128 * (h % 4) + 128]
            incs2 = [("oev", 1)] + ([ops.free(toko)] if True else []) if h == 15 else None
            p.op("ve", lambda eng, src_=src_, h=h, m=m, so=so: eng.scalar_tensor_tensor(
                o_m[:, h * 128:(h + 1) * 128], src_,
                epd[:, (16 * m + h):(16 * m + h) + 1],
                opb[:, so * 2048 + h * 128:so * 2048 + (h + 1) * 128],
                ALU.mult, ALU.add), inc=incs2)
        p.wait("sc", "oev", p.cnt["oev"])
        p.op("sc", lambda eng: eng.activation(sqf32[:], o_m[:], AF.Square),
             inc=("rq", 1))
        p.wait("ve", "rq", p.cnt["rq"])
        p.rop("ve", lambda eng: eng.reduce_sum(rms[:, 0:1], sqf32[:], axis=AX.X))
        p.chain("ve")
        p.op("ve", lambda eng: eng.tensor_scalar(rms[:, 1:2], rms[:, 0:1],
                                                 1.0 / D, None, ALU.mult), inc=("rr", 1))
        p.wait("sc", "rr", p.cnt["rr"])
        p.rop("sc", lambda eng: eng.activation(rms[:, 1:2], rms[:, 1:2], AF.Ln, bias=1e-6))
        p.chain("sc")
        p.op("sc", lambda eng: eng.activation(rms[:, 1:2], rms[:, 1:2], AF.Exp,
                                              scale=-0.5), inc=("rs", 1))
        p.wait("ve", "rs", p.cnt["rs"])
        p.rop("ve", lambda eng: eng.tensor_scalar(o_m[:], o_m[:], rms[:, 1:2],
                                                  None, ALU.mult))
        p.chain("ve")
        p.op("ve", lambda eng: eng.tensor_tensor(on_bf[:], o_m[:], normwb[0:64, :],
                                                 ALU.mult), inc=("onv", 1))
        p.wait("te", "onv", p.cnt["onv"])
        if m >= 1:
            p.wait("te", "otv", p.cnt["otv"])
        for cix in range(16):
            dst = pb[4 + cix // 8][:, 64 * (cix % 8):64 * (cix % 8) + 64]
            p.op("te", lambda eng, dst=dst, cix=cix: eng.matmul(
                dst, on_bf[:, cix * 128:(cix + 1) * 128], idb[0:64, 0:64],
                start=True, stop=True), inc=("otp", 1) if cix == 15 else None)
        p.wait("ve", "otp", p.cnt["otp"])
        for cix in range(16):
            src_ = pb[4 + cix // 8][:, 64 * (cix % 8):64 * (cix % 8) + 64]
            p.op("ve", lambda eng, src_=src_, cix=cix, t0_=t0_: eng.tensor_copy(
                of_full[:, cix * TPC + t0_:cix * TPC + t0_ + 64], src_),
                inc=("otv", 1) if cix == 15 else None)
    p.wait("te", "otv", p.cnt["otv"])
    p.wait("te", "oev", p.cnt["oev"])
    ctx = ProjCtx(p, wt, pb[0:4])
    xst = p.instream("xst3", 2)
    yo = p.outstream("yo3", 2)

    def evac_wo(mm_, half, ps_, pev):
        hs = half * 512
        slx = xst.i % 2
        tokx = xst.load("sy", (xrb[:, slx * 512:(slx + 1) * 512],
                               xT[mm_ * 128:(mm_ + 1) * 128, hs:hs + 512]))
        xst.wait("ve", tokx)
        p.wait("ve", ctx.pe, pev)
        ysl = yo.ready("ve")
        p.op("ve", lambda eng, ps_=ps_, slx=slx, ysl=ysl: eng.tensor_tensor(
            yb[:, ysl * 512:(ysl + 1) * 512], ps_[:],
            xrb[:, slx * 512:(slx + 1) * 512], ALU.add),
            inc=[(ctx.ev, 1), xst.free(tokx)])
        yo.store("sy", xoT[mm_ * 128:(mm_ + 1) * 128, hs:hs + 512],
                 yb[:, ysl * 512:(ysl + 1) * 512], after=(ctx.ev, p.cnt[ctx.ev]))

    emit_proj_fm(p, ctx, WOl, NKD, NKD,
                 lambda k, hs: of_full[:, k * TPC + hs:k * TPC + hs + 512], evac_wo)
    p.finish("gp")
    p.finalize()
    return p


_PROGS = {}


def _prog(name, builder):
    if name not in _PROGS:
        _PROGS[name] = builder()
    return _PROGS[name].nc


def _run(nc, ims):
    return run_bass_kernel_spmd(nc, ims, core_ids=list(range(NCORES))).results


def _rope_tabs(c):
    pp = np.arange(TPC, dtype=np.float64) + (c % 4) * TPC
    inv32 = 1.0 / 10000.0 ** (np.arange(32) / 32.0)
    ang = pp[:, None] * inv32[None, :]
    cs, sn = np.cos(ang).astype(np.float32), np.sin(ang).astype(np.float32)
    inv64 = 1.0 / 10000.0 ** (np.arange(0, 128, 2) / 128.0)
    ang64 = pp[:, None] * inv64[None, :]
    c64, s64 = np.cos(ang64).astype(np.float32), np.sin(ang64).astype(np.float32)
    return cs, sn, c64, s64


def _vis(c):
    v = np.zeros((1, 64), np.float32)
    c4 = c % 4
    for t in range(2):
        v[0, t * 32: t * 32 + 8 * c4 + 4 * t] = 1.0
    return v


def _attn_reshard(r1, kc_key, vt_key, kn_key, qn_key, mla):
    per_core = []
    seqdat = {}
    for sq in range(2):
        cores = range(sq * 4, sq * 4 + 4)
        if mla:
            KCF = np.concatenate([np.asarray(r1[cc][kc_key]) for cc in cores], axis=1)
            KN2F = np.ascontiguousarray(
                np.concatenate([np.asarray(r1[cc][kn_key]) for cc in cores], axis=1))
        else:
            KCF = np.concatenate([np.asarray(r1[cc][kc_key]) for cc in cores], axis=0).T
            KN2F = np.ascontiguousarray(
                np.concatenate([np.asarray(r1[cc][kn_key]) for cc in cores], axis=0).T)
        KCB = np.ascontiguousarray(KCF.reshape(D, 32, 128).transpose(1, 0, 2))
        VTB = np.ascontiguousarray(
            np.concatenate([np.asarray(r1[cc][vt_key]) for cc in cores], axis=0)
            .reshape(32, 128, D))
        seqdat[sq] = (KCB, VTB, KN2F)
    for c in range(NCORES):
        KCB, VTB, KN2F = seqdat[c // 4]
        c4 = c % 4
        d = dict(KCB=KCB, VTB=VTB,
                 KCD=np.ascontiguousarray(KCB[8 * c4:8 * c4 + 8]),
                 VTD=np.ascontiguousarray(VTB[8 * c4:8 * c4 + 8]),
                 KN2F=KN2F, VIS=_vis(c))
        if mla:
            d["QC"] = np.asarray(r1[c]["QC"])
            d["QN2"] = np.asarray(r1[c][qn_key])
        else:
            d["QC"] = np.ascontiguousarray(np.asarray(r1[c]["Q_TM"]).T)
            d["QN2"] = np.ascontiguousarray(np.asarray(r1[c][qn_key]).T)
        per_core.append(d)
    return per_core


def kernel(**inputs):
    nf = lambda k: np.asarray(inputs[k], np.float32)
    bf, f32c, lb = to_bf, f32c_, lineblock

    x = nf("L0_attn_x").reshape(8192, D)
    xTs = [np.ascontiguousarray(x[c * TPC:(c + 1) * TPC].T) for c in range(NCORES)]

    nc_ffn = _prog("ffn", build_ffn_prog)
    nc_mlap = _prog("mlapre", build_mla_pre)
    nc_amla = _prog("amla", lambda: build_attn(True))
    nc_mhap = _prog("mhapre", build_mha_pre)
    nc_amha = _prog("amha", lambda: build_attn(False))
    nc_post = _prog("mhapost", build_mha_post)
    nc_lap = _prog("lapre", build_la_pre)
    nc_lapo = _prog("lapost", build_la_post)

    ffn_sh = host_ffn_inputs(nf("L0_ffn_w_gate"), nf("L0_ffn_w_up"), nf("L0_ffn_w_down"))
    mla_sh = dict(
        WQD=lb(bf(nf("L0_attn_Wq_d"))), BQD=f32c(nf("L0_attn_bq_d"))[:, None],
        WQUC=lb(bf(nf("L0_attn_Wq_uc"))), BQUC=f32c(nf("L0_attn_bq_uc"))[:, None],
        WQUP=bf(nf("L0_attn_Wq_up")), BQUP=bf(nf("L0_attn_bq_up"))[None, :],
        WKVD=lb(bf(nf("L0_attn_Wkv_d"))), BKVD=f32c(nf("L0_attn_bkv_d"))[:, None],
        WKUC=lb(bf(nf("L0_attn_Wk_uc"))), BKUC=f32c(nf("L0_attn_bk_uc"))[:, None],
        WKUP=bf(nf("L0_attn_Wk_up")), BKUP=bf(nf("L0_attn_bk_up"))[None, :],
        WVU=bf(nf("L0_attn_Wv_u")), BVU=bf(nf("L0_attn_bv_u"))[None, :],
    )
    mha_sh = dict(
        WQ=bf(nf("L1_ffn_wq")), WK=bf(nf("L1_ffn_wk")), WV=bf(nf("L1_ffn_wv")),
        BQ=bf(nf("L1_ffn_bq"))[None, :], BK=bf(nf("L1_ffn_bk"))[None, :],
        BV=bf(nf("L1_ffn_bv"))[None, :],
        LN1W=f32c(nf("L1_ffn_ln1_w"))[:, None], LN1B=f32c(nf("L1_ffn_ln1_b"))[:, None],
    )
    post_sh = dict(
        WO=lb(bf(nf("L1_ffn_wo"))), BO=f32c(nf("L1_ffn_bo"))[:, None],
        LN2W=f32c(nf("L1_ffn_ln2_w"))[:, None], LN2B=f32c(nf("L1_ffn_ln2_b"))[:, None],
        WG4=bf(nf("L1_ffn_w_gate")),
        WE1=np.stack([lb(bf(nf("L1_ffn_we1")[e])) for e in range(4)]),
        BE1=f32c(nf("L1_ffn_be1")).reshape(-1, 1),
        WE2=np.stack([lb(bf(nf("L1_ffn_we2")[e])) for e in range(4)]),
        BE2=bf(nf("L1_ffn_be2")),
    )
    la_sh = dict(
        WQl=lb(bf(nf("L1_attn_Wq"))), WKl=lb(bf(nf("L1_attn_Wk"))),
        WVl=lb(bf(nf("L1_attn_Wv"))), WF=bf(nf("L1_attn_Wf")),
    )
    lapo_sh = dict(
        NORMW=f32c(nf("L1_attn_norm_w"))[None, :],
        WOl=lb(bf(nf("L1_attn_Wo"))),
    )

    tabs = [_rope_tabs(c) for c in range(NCORES)]

    def run_ffn(xTs_):
        r = _run(nc_ffn, [dict(xT=xTs_[c], **ffn_sh) for c in range(NCORES)])
        return [np.asarray(r[c]["xoT"]) for c in range(NCORES)]

    def run_s2(xTs_):
        ims = []
        for c in range(NCORES):
            cs, sn, _, _ = tabs[c]
            ims.append(dict(xT=xTs_[c], COS=cs, SIN=sn,
                            COSQ=np.ascontiguousarray(np.tile(cs, (1, 16))),
                            SINQ=np.ascontiguousarray(np.tile(sn, (1, 16))),
                            **mla_sh))
        r1 = _run(nc_mlap, ims)
        parts = _attn_reshard(r1, "KC", "VT", "KN2", "QN2", mla=True)
        ims2 = []
        for c in range(NCORES):
            sq = c // 4
            cores = range(sq * 4, sq * 4 + 4)
            KRFc = np.concatenate([np.asarray(r1[cc]["KR_TM"]) for cc in cores], axis=0).T
            KRFd = np.ascontiguousarray(np.tile(KRFc, (2, 1)).astype(BF))
            KRN2F = np.ascontiguousarray(
                np.concatenate([np.asarray(r1[cc]["KRN2T"]) for cc in cores], axis=0).T)
            c4 = c % 4
            d = parts[c]
            d.update(
                QRF=np.ascontiguousarray(np.asarray(r1[c]["QR_TM"]).T),
                QRN2=np.ascontiguousarray(np.asarray(r1[c]["QRN2T"]).T),
                KRF=KRFd, KRD=np.ascontiguousarray(KRFd[:, c4 * TPC:(c4 + 1) * TPC]),
                KRN2F=KRN2F, xT=xTs_[c],
            )
            ims2.append(d)
        r2 = _run(nc_amla, ims2)
        return [np.asarray(r2[c]["xoT"]) for c in range(NCORES)]

    def run_s4(xTs_):
        ims = []
        for c in range(NCORES):
            _, _, c64, s64 = tabs[c]
            ims.append(dict(xT=xTs_[c],
                            COSQ=np.ascontiguousarray(np.tile(c64, (1, 16))),
                            SINQ=np.ascontiguousarray(np.tile(s64, (1, 16))),
                            **mha_sh))
        r1 = _run(nc_mhap, ims)
        parts = _attn_reshard(r1, "K_TM", "VT", "KN2T", "QN2T", mla=False)
        ims2 = [dict(parts[c]) for c in range(NCORES)]
        r2 = _run(nc_amha, ims2)
        ims3 = []
        for c in range(NCORES):
            ims3.append(dict(
                OF=np.asarray(r2[c]["OF"]), H0=np.asarray(r1[c]["H0"]),
                xT=xTs_[c], **post_sh))
        r3 = _run(nc_post, ims3)
        return [np.asarray(r3[c]["xoT"]) for c in range(NCORES)]

    def run_s5(xTs_):
        r1 = _run(nc_lap, [dict(xT=xTs_[c], **la_sh) for c in range(NCORES)])
        ims2 = []
        for c in range(NCORES):
            sq = c // 4
            cores = list(range(sq * 4, sq * 4 + 4))
            SLA = np.ascontiguousarray(
                np.stack([np.asarray(r1[cc]["SLOC"]) for cc in cores]))
            TOTA = np.ascontiguousarray(
                np.concatenate([np.asarray(r1[cc]["TOTD"]) for cc in cores], axis=0))
            me = c % 4
            mm = np.zeros((4, 4), np.float32)
            g4 = np.zeros((1, 4), np.float32)
            for r_ in range(4):
                if r_ < me:
                    g4[0, r_] = 1
                    for rr in range(4):
                        if r_ < rr < me:
                            mm[r_, rr] = 1
            ims2.append(dict(
                OPART=np.asarray(r1[c]["OPART"]), QS=np.asarray(r1[c]["QS"]),
                PD=np.asarray(r1[c]["PD"]), SLA=SLA, TOTA=TOTA,
                MM=np.ascontiguousarray(mm.T), G4=g4, xT=xTs_[c], **lapo_sh))
        r2 = _run(nc_lapo, ims2)
        return [np.asarray(r2[c]["xoT"]) for c in range(NCORES)]

    xTs = run_ffn(xTs)
    xTs = run_s2(xTs)
    xTs = run_ffn(xTs)
    xTs = run_s4(xTs)
    xTs = run_s5(xTs)
    xTs = run_s4(xTs)

    out = np.concatenate([xTs[c].T for c in range(NCORES)], axis=0)
    return out.reshape(2, 4096, D).astype(np.float32)


f32c_ = f32c


# version 06
# speedup vs baseline: 1.1705x; 1.1705x over previous
import contextlib
import numpy as np
import ml_dtypes
import concourse.bass as bass
import concourse.mybir as mybir
from concourse.bass_utils import run_bass_kernel_spmd

F32 = mybir.dt.float32
BF16 = mybir.dt.bfloat16
AF = mybir.ActivationFunctionType
ALU = mybir.AluOpType
AX = mybir.AxisListType
BF = ml_dtypes.bfloat16

NCORES = 8
D = 2048
TPC = 1024
SEQ = 4096
NKD = D // 128
BIG = 30000.0



class Prog:

    ENGS = ("sy", "te", "sc", "ve", "gp")

    def __init__(self, name="prog"):
        self.nc = bass.Bass("TRN2", target_bir_lowering=False, name=name)
        self.q = {e: [] for e in self.ENGS}
        self.cnt = {}
        self._sems = {}
        self._stack = contextlib.ExitStack()
        self._done = False
        self._dma_sems = set()

    def sem(self, name):
        if name not in self._sems:
            self._sems[name] = self._stack.enter_context(self.nc.semaphore(name))
            self.cnt[name] = 0
        return self._sems[name]

    def wait(self, e, sname, val=None):
        s = self.sem(sname)
        v = self.cnt[sname] if val is None else val
        self.q[e].append(lambda eng, s=s, v=v: eng.wait_ge(s, v))
        return v

    def dma(self, e, out, in_, sname, **kw):
        s = self.sem(sname)
        self._dma_sems.add(sname)
        self.cnt[sname] += 16
        self.q[e].append(
            lambda eng, s=s, out=out, in_=in_, kw=kw:
            eng.dma_start(out=out, in_=in_, **kw).then_inc(s, 16))
        return self.cnt[sname]

    def op(self, e, fn, inc=None):
        if inc is not None:
            incs = inc if isinstance(inc, list) else [inc]
            hs = []
            tgt = None
            tgt0 = None
            for j, (sname, n) in enumerate(incs):
                s = self.sem(sname)
                self.cnt[sname] += n
                tgt = self.cnt[sname]
                if j == 0:
                    tgt0 = self.cnt[sname]
                hs.append((s, n))

            def run(eng, fn=fn, hs=hs, tgt0=tgt0):
                fn(eng).then_inc(*hs[0])
                if len(hs) > 1:
                    eng.wait_ge(hs[0][0], tgt0)
                    for s, n in hs[1:]:
                        eng.sem_inc(s, n)
            self.q[e].append(run)
            return tgt
        self.q[e].append(lambda eng, fn=fn: fn(eng))
        return None

    def rop(self, e, fn):
        self.q[e].append(lambda eng, fn=fn: fn(eng))

    def chain(self, e):
        s = self.sem("chain_" + e)
        self.cnt["chain_" + e] += 1
        v = self.cnt["chain_" + e]
        prev = self.q[e].pop()

        def wrapped(eng, prev=prev, s=s, v=v):
            prev(eng).then_inc(s, 1)
            eng.wait_ge(s, v)
        self.q[e].append(wrapped)

    def din(self, name, shape, dtype):
        return self.nc.dram_tensor(name, list(shape), dtype, kind="ExternalInput").ap()

    def dout(self, name, shape, dtype):
        return self.nc.dram_tensor(name, list(shape), dtype, kind="ExternalOutput").ap()

    def dtmp(self, name, shape, dtype):
        return self.nc.dram_tensor(name, list(shape), dtype, kind="Internal").ap()

    def sb(self, name, shape, dtype):
        return self.nc.alloc_sbuf_tensor(name, list(shape), dtype)

    def ps(self, name, shape, dtype=F32):
        return self.nc.alloc_psum_tensor(name, list(shape), dtype)

    def inline(self, arr, name):
        return self.nc.inline_tensor(arr, name=name).ap()

    def consts(self, values):
        vals = sorted(set(float(v) for v in values))
        arr = np.tile(np.array(vals, np.float32)[None, :], (128, 1))
        dr = self.inline(arr, "constpool")
        tile = self.sb("constpool_sb", [128, len(vals)], F32)
        self.dma("sy", tile[:], dr[:], "ldc")
        for i, v in enumerate(vals):
            self.nc.const_aps.register(F32, v, tile[:, i:i + 1])

    def instream(self, name, nslots, consumers=1):
        return InStream(self, name, nslots, consumers)

    def outstream(self, name, nslots):
        return OutStream(self, name, nslots)

    def finish(self, e="gp"):
        for sname in sorted(self._dma_sems):
            self.wait(e, sname, self.cnt[sname])

    def finalize(self):
        assert not self._done
        self._done = True
        with self.nc.Block() as block:
            @block.sync
            def _(eng):
                for fn in self.q["sy"]:
                    fn(eng)

            @block.tensor
            def _(eng):
                for fn in self.q["te"]:
                    fn(eng)

            @block.scalar
            def _(eng):
                for fn in self.q["sc"]:
                    fn(eng)

            @block.vector
            def _(eng):
                for fn in self.q["ve"]:
                    fn(eng)

            @block.gpsimd
            def _(eng):
                for fn in self.q["gp"]:
                    fn(eng)
        self._stack.close()
        return self.nc


class InStream:

    def __init__(self, p, name, nslots, consumers=1):
        self.p, self.name, self.n, self.k = p, name, nslots, consumers
        self.i = 0

    def load(self, e, pairs):
        p, s = self.p, self.i % self.n
        if self.i >= self.n:
            p.wait(e, f"{self.name}f{s}", self.k * (self.i // self.n))
        if not isinstance(pairs, list):
            pairs = [pairs]
        for out, in_ in pairs:
            p.dma(e, out, in_, f"{self.name}l{s}")
        tok = (s, p.cnt[f"{self.name}l{s}"])
        self.i += 1
        return tok

    def wait(self, e, tok):
        self.p.wait(e, f"{self.name}l{tok[0]}", tok[1])

    def free(self, tok):
        return (f"{self.name}f{tok[0]}", 1)


class OutStream:

    def __init__(self, p, name, nslots):
        self.p, self.name, self.n = p, name, nslots
        self.i = 0

    def ready(self, e):
        s = self.i % self.n
        if self.i >= self.n:
            self.p.wait(e, f"{self.name}s{s}", 16 * (self.i // self.n))
        return s

    def store(self, e, out, in_, after=None):
        s = self.i % self.n
        if after is not None:
            self.p.wait(e, after[0], after[1])
        self.p.dma(e, out, in_, f"{self.name}s{s}")
        self.i += 1



def lineblock(W):
    K, M = W.shape
    assert K % 128 == 0 and M % 128 == 0
    A = W.reshape(K // 128, 128, M // 128, 128)
    return np.ascontiguousarray(A.transpose(2, 1, 0, 3).reshape(M // 128, 128, K))


def to_bf(x):
    return np.asarray(x, dtype=np.float32).astype(BF)


def f32c(x):
    return np.ascontiguousarray(np.asarray(x, dtype=np.float32))



def emit_ln_stats(p, tag, x_src, NT, nch, bufs, ps_sum, ps_sq, row4, Rb, Mb,
                  oc, orw, eps):
    xbuf, sqt = bufs
    st = p.instream("lx" + tag, 4)
    pex = "lnpe_" + tag
    sq = "lnsq_" + tag
    for kc in range(nch):
        sl = kc % 4
        tok = st.load("sy", (xbuf[:, sl * NT:(sl + 1) * NT], x_src(kc)))
        st.wait("sc", tok)
        if kc >= 4:
            p.wait("sc", pex, 2 * (kc - 4) + 2)
        p.op("sc", lambda eng, sl=sl: eng.activation(
            sqt[:, sl * NT:(sl + 1) * NT], xbuf[:, sl * NT:(sl + 1) * NT],
            AF.Square), inc=(sq, 1))
        p.wait("te", sq, p.cnt[sq])
        p.op("te", lambda eng, sl=sl, kc=kc: eng.matmul(
            ps_sq[0:1, 0:NT], oc[:, 0:1], sqt[:, sl * NT:(sl + 1) * NT],
            start=(kc == 0), stop=(kc == nch - 1)), inc=(pex, 1))
        p.op("te", lambda eng, sl=sl, kc=kc: eng.matmul(
            ps_sum[0:1, 0:NT], oc[:, 0:1], xbuf[:, sl * NT:(sl + 1) * NT],
            start=(kc == 0), stop=(kc == nch - 1)),
            inc=[(pex, 1), st.free(tok)])
    pev = p.cnt[pex]
    s_ = row4[0:1, 0:NT]; q_ = row4[0:1, NT:2 * NT]
    r_ = row4[0:1, 2 * NT:3 * NT]; m_ = row4[0:1, 3 * NT:4 * NT]
    p.wait("ve", pex, pev)
    p.rop("ve", lambda eng: eng.tensor_scalar(s_, ps_sum[0:1, 0:NT], 1.0 / D, None, ALU.mult))
    p.rop("ve", lambda eng: eng.tensor_scalar(q_, ps_sq[0:1, 0:NT], 1.0 / D, None, ALU.mult))
    p.chain("ve")
    p.rop("ve", lambda eng: eng.tensor_tensor(r_, s_, s_, ALU.mult))
    p.chain("ve")
    p.rop("ve", lambda eng: eng.tensor_sub(r_, q_, r_))
    p.chain("ve")
    p.wait("sc", "chain_ve", p.cnt["chain_ve"])
    p.rop("sc", lambda eng: eng.activation(r_, r_, AF.Ln, bias=float(eps)))
    p.chain("sc")
    p.rop("sc", lambda eng: eng.activation(r_, r_, AF.Exp, scale=-0.5))
    p.chain("sc")
    p.wait("ve", "chain_sc", p.cnt["chain_sc"])
    p.rop("ve", lambda eng: eng.tensor_tensor(m_, s_, r_, ALU.mult))
    p.chain("ve")
    p.op("ve", lambda eng: eng.tensor_scalar(m_, m_, -1.0, None, ALU.mult),
         inc=("lnrow_" + tag, 1))
    p.wait("te", "lnrow_" + tag, p.cnt["lnrow_" + tag])
    p.op("te", lambda eng: eng.matmul(ps_sum[:, 0:NT], orw[0:1, :], r_,
                                      start=True, stop=True))
    p.op("te", lambda eng: eng.matmul(ps_sq[:, 0:NT], orw[0:1, :], m_,
                                      start=True, stop=True), inc=(pex, 1))
    p.wait("ve", pex, p.cnt[pex])
    p.rop("ve", lambda eng: eng.tensor_copy(Rb[:, 0:NT], ps_sum[:, 0:NT]))
    p.op("ve", lambda eng: eng.tensor_copy(Mb[:, 0:NT], ps_sq[:, 0:NT]),
         inc=("lnbc_" + tag, 1))
    p.wait("ve", "lnbc_" + tag, p.cnt["lnbc_" + tag])
    return p.cnt["lnbc_" + tag]


def emit_ln_apply(p, tag, x_src, dst_aps, NT, nch, xbuf, Rb, Mb,
                  gain=None, bias=None, app=None):
    st = p.instream("la" + tag, 4)
    for kc in range(nch):
        sl = kc % 4
        tok = st.load("sy", (xbuf[:, sl * NT:(sl + 1) * NT], x_src(kc)))
        st.wait("ve", tok)
        dst = dst_aps[kc]
        x_ = xbuf[:, sl * NT:(sl + 1) * NT]
        incs = [st.free(tok)] + ([(app, 1)] if app else [])
        p.rop("ve", lambda eng, dst=dst, x_=x_: eng.tensor_tensor(
            dst, x_, Rb[:, 0:NT], ALU.mult))
        p.chain("ve")
        if gain is None:
            p.op("ve", lambda eng, dst=dst: eng.tensor_tensor(
                dst, dst, Mb[:, 0:NT], ALU.add), inc=incs)
        else:
            p.rop("ve", lambda eng, dst=dst: eng.tensor_tensor(
                dst, dst, Mb[:, 0:NT], ALU.add))
            p.chain("ve")
            g = gain[kc]; b_ = bias[kc]
            p.op("ve", lambda eng, dst=dst, g=g, b_=b_: eng.tensor_scalar(
                dst, dst, g, b_, ALU.mult, ALU.add), inc=incs)
    return p.cnt[app] if app else None


class ProjCtx:

    def __init__(self, p, wtiles, pb4, tag="pj"):
        self.p = p
        self.wt = wtiles
        self.pb4 = pb4
        self.st = p.instream(tag + "w", len(wtiles))
        self.pe = tag + "pe"
        self.ev = tag + "ev"
        self.gi = 0


def emit_proj_fm(p, ctx, Wlb, nK, nM, rhs_fn, evac_fn, TT=TPC):
    nhalf = TT // 512
    nw = len(ctx.wt)
    for m in range(nM):
        b = ctx.st.i % nw
        tok = ctx.st.load("sy", (ctx.wt[b][:, 0:nK * 128], Wlb[m]))
        ctx.st.wait("te", tok)
        for half in range(nhalf):
            hs = half * 512
            ps_ = ctx.pb4[ctx.gi % 4]
            if ctx.gi >= 4:
                p.wait("te", ctx.ev, ctx.gi - 3)
            for k in range(nK):
                incs = []
                if k == nK - 1:
                    incs.append((ctx.pe, 1))
                    if half == nhalf - 1:
                        incs.append(ctx.st.free(tok))
                p.op("te", lambda eng, k=k, b=b, hs=hs, ps_=ps_: eng.matmul(
                    ps_[:], ctx.wt[b][:, k * 128:(k + 1) * 128], rhs_fn(k, hs),
                    start=(k == 0), stop=(k == nK - 1)),
                    inc=incs if incs else None)
            ctx.gi += 1
            evac_fn(m, half, ps_, p.cnt[ctx.pe])


def load_bias_cols(p, tag, dram_ap, nM):
    t = p.sb("bias_" + tag, [128, nM], F32)
    for m in range(nM):
        p.dma("sy", t[:, m:m + 1], dram_ap[m * 128:(m + 1) * 128, :], "ldc")
    return t



def build_ffn_prog():
    p = Prog("ffn")
    xT = p.din("xT", [D, TPC], F32)
    WG = p.din("WG", [64, 128, D], BF16)
    WU = p.din("WU", [64, 128, D], BF16)
    WD = p.din("WD", [16, 128, 8192], BF16)
    xoT = p.dout("xoT", [D, TPC], F32)

    oc = p.sb("oc", [128, 1], F32)
    orw = p.sb("orw", [1, 128], F32)
    p.dma("sy", oc[:], p.inline(np.ones((128, 1), np.float32), "onesc")[:], "ldc")
    p.dma("sy", orw[:], p.inline(np.ones((1, 128), np.float32), "onesr")[:], "ldc")
    p.consts([1e-5])
    for e in ("te", "sc", "ve"):
        p.wait(e, "ldc", p.cnt["ldc"])

    xbuf = p.sb("xbuf", [128, 4 * 512], F32)
    sqt = p.sb("sqt", [128, 4 * 512], F32)
    abuf = p.sb("abuf", [128, 4 * 512], F32)
    rbuf = p.sb("rbuf", [128, 2 * 512], F32)
    row4 = p.sb("row4", [1, 4 * 512], F32)
    Rb = p.sb("Rb", [128, 512], F32)
    Mb = p.sb("Mb", [128, 512], F32)
    xln = p.sb("xln", [128, 16 * 512], BF16)
    wg_sb = [p.sb(f"wg{i}", [128, D], BF16) for i in range(2)]
    wu_sb = [p.sb(f"wu{i}", [128, D], BF16) for i in range(2)]
    gtmp = p.sb("gtmp", [128, 2 * 512], BF16)
    utmp = p.sb("utmp", [128, 2 * 512], BF16)
    hidh = p.sb("hidh", [128, 64 * 512], BF16)
    wd_sb = [p.sb(f"wd{i}", [128, 8192], BF16) for i in range(2)]
    yb = p.sb("yb", [128, 2 * 512], F32)

    pb = [p.ps(f"pb{i}", [128, 512]) for i in range(8)]

    wgu = p.instream("wgu", 2)
    wdn = p.instream("wdn", 2)
    rsx = p.instream("rsx", 2)
    yo = p.outstream("yo", 2)

    npair = 0
    scev_at = {}
    gev_at = {}

    for half in range(2):
        hs = half * 512
        tag = f"f{half}"
        if half == 1:
            p.wait("sy", "lnpe_f0", p.cnt["lnpe_f0"])
            p.wait("sy", "dn_ev", p.cnt["dn_ev"])
            p.wait("te", "dn_ev", p.cnt["dn_ev"])
            p.wait("sc", "lnpe_f0", p.cnt["lnpe_f0"])
        bcv = emit_ln_stats(p, tag, lambda kc, hs=hs: xT[kc * 128:(kc + 1) * 128, hs:hs + 512],
                            512, NKD, (xbuf, sqt), pb[0], pb[1], row4, Rb, Mb,
                            oc, orw, 1e-5)
        emit_ln_apply(p, tag, lambda kc, hs=hs: xT[kc * 128:(kc + 1) * 128, hs:hs + 512],
                      [xln[:, kc * 512:(kc + 1) * 512] for kc in range(NKD)],
                      512, NKD, abuf, Rb, Mb, app="xlnap")
        p.wait("te", "xlnap", p.cnt["xlnap"])

        for m in range(64):
            tokw = wgu.load("sy", [(wg_sb[m % 2][:], WG[m]), (wu_sb[m % 2][:], WU[m])])
            b = m % 2
            pg = pb[4 + 2 * (m % 2)]
            pu = pb[5 + 2 * (m % 2)]
            wgu.wait("te", tokw)
            if npair >= 2:
                p.wait("te", "scev", scev_at[npair - 2])
            for k in range(NKD):
                p.op("te", lambda eng, k=k, b=b, pg=pg: eng.matmul(
                    pg[:], wg_sb[b][:, k * 128:(k + 1) * 128],
                    xln[:, k * 512:(k + 1) * 512],
                    start=(k == 0), stop=(k == NKD - 1)))
            for k in range(NKD):
                incs = []
                if k == NKD - 1:
                    incs = [("pe_gu", 1), wgu.free(tokw)]
                p.op("te", lambda eng, k=k, b=b, pu=pu: eng.matmul(
                    pu[:], wu_sb[b][:, k * 128:(k + 1) * 128],
                    xln[:, k * 512:(k + 1) * 512],
                    start=(k == 0), stop=(k == NKD - 1)),
                    inc=incs if incs else None)
            sl = npair % 2
            p.wait("sc", "pe_gu", p.cnt["pe_gu"])
            if npair >= 2:
                p.wait("sc", "gevac", gev_at[npair - 2])
            p.rop("sc", lambda eng, pg=pg, sl=sl: eng.activation(
                gtmp[:, sl * 512:(sl + 1) * 512], pg[:], AF.Silu))
            p.op("sc", lambda eng, pu=pu, sl=sl: eng.copy(
                utmp[:, sl * 512:(sl + 1) * 512], pu[:]), inc=("scev", 1))
            scev_at[npair] = p.cnt["scev"]
            p.wait("ve", "scev", p.cnt["scev"])
            p.op("ve", lambda eng, m=m, sl=sl: eng.tensor_tensor(
                hidh[:, m * 512:(m + 1) * 512],
                gtmp[:, sl * 512:(sl + 1) * 512],
                utmp[:, sl * 512:(sl + 1) * 512], ALU.mult), inc=("gevac", 1))
            gev_at[npair] = p.cnt["gevac"]
            npair += 1

        p.wait("te", "gevac", p.cnt["gevac"])
        for dci in range(NKD):
            gd = half * NKD + dci
            tokd = wdn.load("sy", (wd_sb[dci % 2][:], WD[dci]))
            pd = pb[gd % 4]
            wdn.wait("te", tokd)
            if gd >= 4:
                p.wait("te", "dn_ev", gd - 3)
            for k in range(64):
                incs = []
                if k == 63:
                    incs = [("dn_pe", 1), wdn.free(tokd)]
                p.op("te", lambda eng, k=k, dci=dci, pd=pd: eng.matmul(
                    pd[:], wd_sb[dci % 2][:, k * 128:(k + 1) * 128],
                    hidh[:, k * 512:(k + 1) * 512],
                    start=(k == 0), stop=(k == 63)),
                    inc=incs if incs else None)
            dnpev = p.cnt["dn_pe"]
            sl2 = rsx.i % 2
            tokr = rsx.load("sy", (rbuf[:, sl2 * 512:(sl2 + 1) * 512],
                                   xT[dci * 128:(dci + 1) * 128, hs:hs + 512]))
            rsx.wait("ve", tokr)
            p.wait("ve", "dn_pe", dnpev)
            so = yo.ready("ve")
            p.op("ve", lambda eng, pd=pd, sl2=sl2, so=so: eng.scalar_tensor_tensor(
                yb[:, so * 512:(so + 1) * 512], pd[:], 0.5,
                rbuf[:, sl2 * 512:(sl2 + 1) * 512], ALU.mult, ALU.add),
                inc=[("dn_ev", 1), rsx.free(tokr)])
            yo.store("gp", xoT[dci * 128:(dci + 1) * 128, hs:hs + 512],
                     yb[:, so * 512:(so + 1) * 512], after=("dn_ev", p.cnt["dn_ev"]))

    p.finish("gp")
    p.finalize()
    return p


def host_ffn_inputs(w_gate, w_up, w_down):
    return dict(
        WG=lineblock(to_bf(np.asarray(w_gate).T)),
        WU=lineblock(to_bf(np.asarray(w_up).T)),
        WD=lineblock(to_bf(np.asarray(w_down).T)),
    )


def build_mla_pre(stage=9):
    p = Prog("mlapre")
    xT = p.din("xT", [D, TPC], F32)
    WQD = p.din("WQD", [12, 128, D], BF16)
    BQD = p.din("BQD", [1536, 1], F32)
    WQUC = p.din("WQUC", [16, 128, 1536], BF16)
    BQUC = p.din("BQUC", [2048, 1], F32)
    WQUP = p.din("WQUP", [1536, 1024], BF16)
    BQUP = p.din("BQUP", [1, 1024], BF16)
    WKVD = p.din("WKVD", [4, 128, D], BF16)
    BKVD = p.din("BKVD", [512, 1], F32)
    WKUC = p.din("WKUC", [16, 128, 512], BF16)
    BKUC = p.din("BKUC", [2048, 1], F32)
    WKUP = p.din("WKUP", [512, 64], BF16)
    BKUP = p.din("BKUP", [1, 64], BF16)
    WVU = p.din("WVU", [512, 2048], BF16)
    BVU = p.din("BVU", [1, 2048], BF16)
    COS = p.din("COS", [TPC, 32], F32)
    SIN = p.din("SIN", [TPC, 32], F32)
    COSQ = p.din("COSQ", [TPC, 512], F32)
    SINQ = p.din("SINQ", [TPC, 512], F32)

    QC = p.dout("QC", [D, TPC], BF16)
    QR_TM = p.dout("QR_TM", [TPC, 1024], BF16)
    KC = p.dout("KC", [D, TPC], BF16)
    KR_TM = p.dout("KR_TM", [TPC, 64], BF16)
    VT = p.dout("VT", [TPC, D], BF16)
    QN2 = p.dout("QN2", [16, TPC], F32)
    QRN2T = p.dout("QRN2T", [TPC, 16], F32)
    KRN2T = p.dout("KRN2T", [TPC, 1], F32)
    KN2 = p.dout("KN2", [16, TPC], F32)

    oc = p.sb("oc", [128, 1], F32)
    orw = p.sb("orw", [1, 128], F32)
    orb = p.sb("orb", [1, 128], BF16)
    idn = p.sb("idn", [128, 128], BF16)
    p.dma("sy", oc[:], p.inline(np.ones((128, 1), np.float32), "onesc")[:], "ldc")
    p.dma("sy", orw[:], p.inline(np.ones((1, 128), np.float32), "onesr")[:], "ldc")
    p.dma("sy", orb[:], p.inline(np.ones((1, 128), BF), "onesrb")[:], "ldc")
    p.dma("sy", idn[:], p.inline(np.eye(128, dtype=BF), "idnb")[:], "ldc")
    p.consts([1e-5])
    bqd = load_bias_cols(p, "bqd", BQD, 12)
    bquc = load_bias_cols(p, "bquc", BQUC, 16)
    bkvd = load_bias_cols(p, "bkvd", BKVD, 4)
    bkuc = load_bias_cols(p, "bkuc", BKUC, 16)

    xbuf = p.sb("xbuf", [128, 4 * 512], F32)
    sqt = p.sb("sqt", [128, 4 * 512], F32)
    abuf = p.sb("abuf", [128, 4 * 512], F32)
    row4 = p.sb("row4", [1, 4 * 512], F32)
    Rb = p.sb("Rb", [128, 512], F32)
    Mb = p.sb("Mb", [128, 512], F32)
    xln = p.sb("xln", [128, 16 * TPC], BF16)
    qlat = p.sb("qlat", [128, 12 * TPC], BF16)
    kvlat = p.sb("kvlat", [128, 4 * TPC], BF16)
    wt = [p.sb(f"wt{i}", [128, 16 * 128], BF16) for i in range(2)]
    ev_tmp = p.sb("ev_tmp", [128, 2 * 512], BF16)
    sq2 = p.sb("sq2", [128, 2 * 512], F32)
    nrow = p.sb("nrow", [1, 2 * 512], F32)
    wv_sb = p.sb("wv_sb", [128, 4 * 2048], BF16)
    wkup_sb = p.sb("wkup_sb", [128, 4 * 64], BF16)
    wqup_sb = p.sb("wqup_sb", [128, 12 * 1024], BF16)
    bvu_sb = p.sb("bvu_sb", [1, 2048], BF16)
    bqup_sb = p.sb("bqup_sb", [1, 1024], BF16)
    bkup_sb = p.sb("bkup_sb", [1, 64], BF16)
    cos_sb = p.sb("cos_sb", [128, 8 * 32], F32)
    sin_sb = p.sb("sin_sb", [128, 8 * 32], F32)
    cosq_sb = p.sb("cosq_sb", [128, 2 * 512], F32)
    sinq_sb = p.sb("sinq_sb", [128, 2 * 512], F32)
    tmt = p.sb("tmt", [128, 2 * 1024], BF16)
    rpt = p.sb("rpt", [128, 2 * 1024], BF16)

    for k in range(4):
        p.dma("sy", wv_sb[:, k * 2048:(k + 1) * 2048], WVU[k * 128:(k + 1) * 128, :], "ldc")
        p.dma("sy", wkup_sb[:, k * 64:(k + 1) * 64], WKUP[k * 128:(k + 1) * 128, :], "ldc")
    for k in range(12):
        p.dma("sy", wqup_sb[:, k * 1024:(k + 1) * 1024], WQUP[k * 128:(k + 1) * 128, :], "ldc")
    p.dma("sy", bvu_sb[:], BVU[:], "ldc")
    p.dma("sy", bqup_sb[:], BQUP[:], "ldc")
    p.dma("sy", bkup_sb[:], BKUP[:], "ldc")
    for tc in range(8):
        p.dma("sy", cos_sb[:, tc * 32:(tc + 1) * 32], COS[tc * 128:(tc + 1) * 128, :], "ldc")
        p.dma("sy", sin_sb[:, tc * 32:(tc + 1) * 32], SIN[tc * 128:(tc + 1) * 128, :], "ldc")
    for e in ("te", "sc", "ve"):
        p.wait(e, "ldc", p.cnt["ldc"])

    pb = [p.ps(f"pb{i}", [128, 512]) for i in range(6)]
    pt = p.ps("pt", [128, 128], BF16)
    pn = p.ps("pn", [4, 512])

    for half in range(2):
        hs = half * 512
        tag = f"m{half}"
        if half == 1:
            p.wait("sy", "lnpe_m0", p.cnt["lnpe_m0"])
            p.wait("sc", "lnpe_m0", p.cnt["lnpe_m0"])
            p.wait("sy", "xlnap", p.cnt["xlnap"])
        emit_ln_stats(p, tag, lambda kc, hs=hs: xT[kc * 128:(kc + 1) * 128, hs:hs + 512],
                      512, NKD, (xbuf, sqt), pb[0], pb[1], row4, Rb, Mb,
                      oc, orw, 1e-5)
        emit_ln_apply(p, tag, lambda kc, hs=hs: xT[kc * 128:(kc + 1) * 128, hs:hs + 512],
                      [xln[:, kc * TPC + hs:kc * TPC + hs + 512] for kc in range(NKD)],
                      512, NKD, abuf, Rb, Mb, app="xlnap")
    p.wait("te", "xlnap", p.cnt["xlnap"])

    ctx = ProjCtx(p, wt, pb[0:4])

    def mk_lat_evac(dst, bias_sb):
        def evac(m, half, ps_, pev):
            hs = half * 512
            p.wait("sc", ctx.pe, pev)
            p.op("sc", lambda eng, m=m, hs=hs, ps_=ps_: eng.activation(
                dst[:, m * TPC + hs:m * TPC + hs + 512], ps_[:], AF.Identity,
                bias=bias_sb[:, m:m + 1]), inc=(ctx.ev, 1))
        return evac

    emit_proj_fm(p, ctx, WQD, NKD, 12,
                 lambda k, hs: xln[:, k * TPC + hs:k * TPC + hs + 512],
                 mk_lat_evac(qlat, bqd))
    emit_proj_fm(p, ctx, WKVD, NKD, 4,
                 lambda k, hs: xln[:, k * TPC + hs:k * TPC + hs + 512],
                 mk_lat_evac(kvlat, bkvd))
    latev = p.cnt[ctx.ev]
    p.wait("te", ctx.ev, latev)

    if stage < 2:
        p.finish("gp"); p.finalize(); return p
    kro = p.outstream("kro", 2)
    krn = p.outstream("krn", 2)
    krnsq = p.sb("krnsq", [128, 64], F32)
    krnred = p.sb("krnred", [128, 2], F32)
    for tc in range(8):
        ts0 = tc * 128
        pv = pb[4 + tc % 2]
        if tc >= 2:
            p.wait("te", "krrp", tc - 1)
        for k in range(4):
            p.op("te", lambda eng, k=k, ts0=ts0, pv=pv: eng.matmul(
                pv[:, 0:64], kvlat[:, k * TPC + ts0:k * TPC + ts0 + 128],
                wkup_sb[:, k * 64:(k + 1) * 64], start=(k == 0), stop=False))
        p.op("te", lambda eng, pv=pv: eng.matmul(
            pv[:, 0:64], orb[0:1, :], bkup_sb[0:1, :], start=False, stop=True),
            inc=("krpe", 1))
        p.wait("ve", "krpe", p.cnt["krpe"])
        sl = kro.ready("ve")
        x1 = pv[:, 0:32]; x2 = pv[:, 32:64]
        c_ = cos_sb[:, tc * 32:(tc + 1) * 32]
        s_ = sin_sb[:, tc * 32:(tc + 1) * 32]
        o1 = rpt[:, sl * 1024:sl * 1024 + 32]
        o2 = rpt[:, sl * 1024 + 32:sl * 1024 + 64]
        t1 = rpt[:, sl * 1024 + 64:sl * 1024 + 96]
        p.rop("ve", lambda eng, x1=x1, c_=c_, o1=o1: eng.tensor_tensor(o1, x1, c_, ALU.mult))
        p.rop("ve", lambda eng, x2=x2, s_=s_, t1=t1: eng.tensor_tensor(t1, x2, s_, ALU.mult))
        p.chain("ve")
        p.rop("ve", lambda eng, o1=o1, t1=t1: eng.tensor_sub(o1, o1, t1))
        p.chain("ve")
        p.rop("ve", lambda eng, x2=x2, c_=c_, o2=o2: eng.tensor_tensor(o2, x2, c_, ALU.mult))
        p.rop("ve", lambda eng, x1=x1, s_=s_, t1=t1: eng.tensor_tensor(t1, x1, s_, ALU.mult))
        p.chain("ve")
        p.op("ve", lambda eng, o2=o2, t1=t1: eng.tensor_add(o2, o2, t1),
             inc=("krrp", 1))
        kro.store("sy", KR_TM[ts0:ts0 + 128, :], rpt[:, sl * 1024:sl * 1024 + 64],
                  after=("krrp", p.cnt["krrp"]))
        p.wait("sc", "krrp", p.cnt["krrp"])
        if tc >= 2:
            p.wait("sc", "krnv", tc - 1)
        p.op("sc", lambda eng, sl=sl: eng.activation(
            krnsq[:], rpt[:, sl * 1024:sl * 1024 + 64], AF.Square),
            inc=("krns", 1))
        p.wait("ve", "krns", p.cnt["krns"])
        sn = krn.ready("ve")
        p.op("ve", lambda eng, sn=sn: eng.reduce_sum(
            krnred[:, sn:sn + 1], krnsq[:], axis=AX.X), inc=("krnv", 1))
        krn.store("sy", KRN2T[ts0:ts0 + 128, :], krnred[:, sn:sn + 1],
                  after=("krnv", p.cnt["krnv"]))

    if stage < 3:
        p.finish("gp"); p.finalize(); return p
    p.wait("te", "krrp", p.cnt["krrp"])
    do_ = p.outstream("do", 2)
    vo = do_
    for tc in range(8):
        ts0 = tc * 128
        for n in range(4):
            gi = tc * 4 + n
            pv = pb[4 + gi % 2]
            if gi >= 2:
                p.wait("te", "vev", gi - 1)
            for k in range(4):
                p.op("te", lambda eng, k=k, ts0=ts0, pv=pv, n=n: eng.matmul(
                    pv[:], kvlat[:, k * TPC + ts0:k * TPC + ts0 + 128],
                    wv_sb[:, k * 2048 + n * 512:k * 2048 + (n + 1) * 512],
                    start=(k == 0), stop=False))
            p.op("te", lambda eng, pv=pv, n=n: eng.matmul(
                pv[:], orb[0:1, :], bvu_sb[0:1, n * 512:(n + 1) * 512],
                start=False, stop=True), inc=("vpe", 1))
            p.wait("sc", "vpe", p.cnt["vpe"])
            so = vo.ready("sc")
            p.op("sc", lambda eng, pv=pv, so=so: eng.copy(
                ev_tmp[:, so * 512:(so + 1) * 512], pv[:]), inc=("vev", 1))
            vo.store("sy", VT[ts0:ts0 + 128, n * 512:(n + 1) * 512],
                     ev_tmp[:, so * 512:(so + 1) * 512], after=("vev", p.cnt["vev"]))

    if stage < 4:
        p.finish("gp"); p.finalize(); return p
    p.wait("te", "vev", p.cnt["vev"])
    p.wait("ve", "kros0", p.cnt["kros0"])
    p.wait("ve", "kros1", p.cnt["kros1"])
    cq = p.instream("cq", 2)
    qo = p.outstream("qo", 2)
    qn = p.outstream("qn", 2)
    qsq = p.sb("qsq", [128, 512], F32)
    qnred = p.sb("qnred", [128, 2 * 16], F32)
    for tc in range(8):
        ts0 = tc * 128
        csl = cq.i % 2
        tokc = cq.load("sy", [(cosq_sb[:, csl * 512:(csl + 1) * 512], COSQ[ts0:ts0 + 128, :]),
                              (sinq_sb[:, csl * 512:(csl + 1) * 512], SINQ[ts0:ts0 + 128, :])])
        sn = qn.ready("ve")
        for n in range(2):
            gi = tc * 2 + n
            pv = pb[4 + gi % 2]
            if gi >= 2:
                p.wait("te", "qrrp", gi - 1)
            for k in range(12):
                p.op("te", lambda eng, k=k, ts0=ts0, pv=pv, n=n: eng.matmul(
                    pv[:], qlat[:, k * TPC + ts0:k * TPC + ts0 + 128],
                    wqup_sb[:, k * 1024 + n * 512:k * 1024 + (n + 1) * 512],
                    start=(k == 0), stop=False))
            p.op("te", lambda eng, pv=pv, n=n: eng.matmul(
                pv[:], orb[0:1, :], bqup_sb[0:1, n * 512:(n + 1) * 512],
                start=False, stop=True), inc=("qrpe", 1))
            p.wait("ve", "qrpe", p.cnt["qrpe"])
            cq.wait("ve", tokc)
            so = qo.ready("ve")
            dst = tmt[:, so * 1024:so * 1024 + 512]
            pvv = pv[:, 0:512].rearrange("p (h t f) -> p h t f", h=8, t=2, f=32)
            dvv = dst.rearrange("p (h t f) -> p h t f", h=8, t=2, f=32)
            cqv = cosq_sb[:, csl * 512:(csl + 1) * 512].rearrange("p (h f) -> p h f", h=16, f=32)
            sqv = sinq_sb[:, csl * 512:(csl + 1) * 512].rearrange("p (h f) -> p h f", h=16, f=32)
            cqh = cqv[:, n * 8:(n + 1) * 8, :]
            sqh = sqv[:, n * 8:(n + 1) * 8, :]
            x1 = pvv[:, :, 0, :]; x2 = pvv[:, :, 1, :]
            d1 = dvv[:, :, 0, :]; d2 = dvv[:, :, 1, :]
            tscr = rpt[:, 128:384].rearrange("p (h f) -> p h f", h=8, f=32)
            incs = [("qrrp", 1)] + ([cq.free(tokc)] if n == 1 else [])
            p.rop("ve", lambda eng, d1=d1, x1=x1, cqh=cqh: eng.tensor_tensor(d1, x1, cqh, ALU.mult))
            p.rop("ve", lambda eng, tscr=tscr, x2=x2, sqh=sqh: eng.tensor_tensor(tscr, x2, sqh, ALU.mult))
            p.chain("ve")
            p.rop("ve", lambda eng, d1=d1, tscr=tscr: eng.tensor_sub(d1, d1, tscr))
            p.chain("ve")
            p.rop("ve", lambda eng, d2=d2, x2=x2, cqh=cqh: eng.tensor_tensor(d2, x2, cqh, ALU.mult))
            p.rop("ve", lambda eng, tscr=tscr, x1=x1, sqh=sqh: eng.tensor_tensor(tscr, x1, sqh, ALU.mult))
            p.chain("ve")
            p.op("ve", lambda eng, d2=d2, tscr=tscr: eng.tensor_add(d2, d2, tscr),
                 inc=incs)
            p.wait("ve", "qrrp", p.cnt["qrrp"])
            qo.store("sy", QR_TM[ts0:ts0 + 128, n * 512:(n + 1) * 512], dst,
                     after=("qrrp", p.cnt["qrrp"]))
            p.wait("sc", "qrrp", p.cnt["qrrp"])
            if gi >= 2:
                p.wait("sc", "qnv", gi - 1)
            p.op("sc", lambda eng, dst=dst: eng.activation(qsq[:], dst, AF.Square),
                 inc=("qns", 1))
            p.wait("ve", "qns", p.cnt["qns"])
            qv = qsq.rearrange("p (h f) -> p h f", h=8, f=64)
            p.op("ve", lambda eng, qv=qv, sn=sn, n=n: eng.reduce_sum(
                qnred[:, sn * 16 + n * 8:sn * 16 + (n + 1) * 8], qv, axis=AX.X),
                inc=("qnv", 1))
        qn.store("sy", QRN2T[ts0:ts0 + 128, :], qnred[:, sn * 16:(sn + 1) * 16],
                 after=("qnv", p.cnt["qnv"]))

    if stage < 5:
        p.finish("gp"); p.finalize(); return p
    no = p.outstream("no", 2)

    def mk_head_evac(dst_dram, bias_sb, n2_dram, tagn, extra_n2_row=None,
                     extra_fn=None):
        def evac(m, half, ps_, pev):
            hs = half * 512
            gi = m * 2 + half
            sl = gi % 2
            p.wait("sc", ctx.pe, pev)
            if gi >= 2:
                p.wait("sc", "pn_" + tagn, gi - 1)
            so = do_.ready("sc")
            p.op("sc", lambda eng: eng.activation(
                ev_tmp[:, so * 512:so * 512 + 512], ps_[:], AF.Identity,
                bias=bias_sb[:, m:m + 1]), inc=(ctx.ev, 1))
            p.wait("sc", ctx.ev, p.cnt[ctx.ev])
            p.op("sc", lambda eng: eng.activation(
                sq2[:, sl * 512:(sl + 1) * 512],
                ev_tmp[:, so * 512:so * 512 + 512], AF.Square),
                inc=("sq_" + tagn, 1))
            do_.store("sy", dst_dram[m * 128:(m + 1) * 128, hs:hs + 512],
                      ev_tmp[:, so * 512:so * 512 + 512],
                      after=("sq_" + tagn, p.cnt["sq_" + tagn]))
            p.wait("te", "sq_" + tagn, p.cnt["sq_" + tagn])
            if gi >= 1:
                p.wait("te", "nv_" + tagn, gi)
            p.op("te", lambda eng: eng.matmul(
                pn[0:1, 0:512], oc[:, 0:1],
                sq2[:, sl * 512:(sl + 1) * 512], start=True, stop=True),
                inc=("pn_" + tagn, 1))
            p.wait("ve", "pn_" + tagn, p.cnt["pn_" + tagn])
            sn = no.ready("ve")
            if extra_fn is not None:
                ex = extra_fn(m, hs)
                p.op("ve", lambda eng, sn=sn, ex=ex: eng.tensor_tensor(
                    nrow[0:1, sn * 512:(sn + 1) * 512], pn[0:1, 0:512],
                    ex, ALU.add), inc=("nv_" + tagn, 1))
            elif extra_n2_row is None:
                p.op("ve", lambda eng, sn=sn: eng.tensor_copy(
                    nrow[0:1, sn * 512:(sn + 1) * 512], pn[0:1, 0:512]),
                    inc=("nv_" + tagn, 1))
            else:
                p.op("ve", lambda eng, sn=sn, hs=hs: eng.tensor_tensor(
                    nrow[0:1, sn * 512:(sn + 1) * 512], pn[0:1, 0:512],
                    extra_n2_row[0:1, hs:hs + 512], ALU.add),
                    inc=("nv_" + tagn, 1))
            no.store("sy", n2_dram[m:m + 1, hs:hs + 512],
                     nrow[0:1, sn * 512:(sn + 1) * 512],
                     after=("nv_" + tagn, p.cnt["nv_" + tagn]))
        return evac

    emit_proj_fm(p, ctx, WQUC, 12, 16,
                 lambda k, hs: qlat[:, k * TPC + hs:k * TPC + hs + 512],
                 mk_head_evac(QC, bquc, QN2, "qc"))
    emit_proj_fm(p, ctx, WKUC, 4, 16,
                 lambda k, hs: kvlat[:, k * TPC + hs:k * TPC + hs + 512],
                 mk_head_evac(KC, bkuc, KN2, "kc"))

    p.finish("gp")
    p.finalize()
    return p


def build_attn(mla: bool):
    p = Prog("attn_mla" if mla else "attn_mha")
    NH = 16
    scale = 1.0 / np.sqrt(192.0) if mla else 1.0 / np.sqrt(128.0)
    QC = p.din("QC", [D, TPC], BF16)
    KCB = p.din("KCB", [32, D, 128], BF16)
    VTB = p.din("VTB", [32, 128, D], BF16)
    KCD = p.din("KCD", [8, D, 128], BF16)
    VTD = p.din("VTD", [8, 128, D], BF16)
    QN2 = p.din("QN2", [16, TPC], F32)
    KN2F = p.din("KN2F", [16, SEQ], F32)
    VIS = p.din("VIS", [1, 64], F32)
    if mla:
        QRF = p.din("QRF", [1024, TPC], BF16)
        QRN2 = p.din("QRN2", [16, TPC], F32)
        KRN2F = p.din("KRN2F", [1, SEQ], F32)
        KRF = p.din("KRF", [128, SEQ], BF16)
        KRD = p.din("KRD", [128, TPC], BF16)
        xT = p.din("xT", [D, TPC], F32)
        xoT = p.dout("xoT", [D, TPC], F32)
    else:
        OF = p.dout("OF", [D, TPC], BF16)

    orw = p.sb("orw", [1, 128], F32)
    ocb = p.sb("ocb", [128, 1], BF16)
    tri = p.sb("tri", [128, 4 * 512], BF16)
    tri_np = np.zeros((128, 4 * 512), dtype=BF)
    for t_ in range(4):
        kk = np.arange(128)[:, None]
        qq = np.arange(512)[None, :]
        tri_np[:, t_ * 512:(t_ + 1) * 512] = (128 * t_ + kk <= qq).astype(BF)
    p.dma("sy", orw[:], p.inline(np.ones((1, 128), np.float32), "onesr")[:], "ldc")
    p.dma("sy", ocb[:], p.inline(np.ones((128, 1), BF), "onescb")[:], "ldc")
    p.dma("sy", tri[:], p.inline(tri_np, "trie")[:], "ldc")

    qc_sb = p.sb("qc_sb", [128, 16 * TPC], BF16)
    for h in range(16):
        p.dma("sy", qc_sb[:, h * TPC:(h + 1) * TPC], QC[h * 128:(h + 1) * 128, :], "ldq")
    if mla:
        qr_sb = p.sb("qr_sb", [128, 8 * TPC], BF16)
        for cix in range(8):
            p.dma("sy", qr_sb[:, cix * TPC:(cix + 1) * TPC],
                  QRF[cix * 128:(cix + 1) * 128, :], "ldq")
        krf_sb = p.sb("krf_sb2", [128, SEQ], BF16)
        krd_sb = p.sb("krd_sb", [128, TPC], BF16)
        p.dma("sy", krf_sb[:], KRF[:], "ldq")
        p.dma("sy", krd_sb[:], KRD[:], "ldq")
    qn2_sb = p.sb("qn2_sb", [16, TPC], F32)
    kn2_sb = p.sb("kn2_sb", [16, SEQ], F32)
    p.dma("sy", qn2_sb[:], QN2[:], "ldc")
    p.dma("sy", kn2_sb[:], KN2F[:], "ldc")
    if mla:
        qrn2_sb = p.sb("qrn2_sb", [16, TPC], F32)
        p.dma("sy", qrn2_sb[:], QRN2[:], "ldc")
        krn2_sb = p.sb("krn2_sb", [1, SEQ], F32)
        p.dma("sy", krn2_sb[:], KRN2F[:], "ldc")
        o16 = p.sb("o16", [1, 16], F32)
        p.dma("sy", o16[:], p.inline(np.ones((1, 16), np.float32), "ones16")[:], "ldc")
    vis_sb = p.sb("vis_sb", [1, 64], F32)
    p.dma("sy", vis_sb[:], VIS[:], "ldc")

    kbuf = p.sb("kbuf", [128, 3 * 256], BF16)
    vbuf = p.sb("vbuf", [128, 3 * 256], BF16)
    pT = p.sb("pT", [128, 4 * 512], BF16)
    rows = p.sb("rows", [1, 4 * 512], F32)
    rsb = p.sb("rsb", [128, 2 * 512], F32)
    och = p.sb("och", [128, 2 * 512], F32 if mla else BF16)
    xrb = p.sb("xrb", [128, 2 * 512], F32) if mla else None
    small = p.sb("small", [16, 8], F32)
    negrow = p.sb("negrow", [1, 32], F32)
    negBb = p.sb("negBb", [128, 32], F32)
    negBpB = p.sb("negBpB", [128, 32], F32)
    visb = p.sb("visb", [128, 64], F32)
    bias_all = p.sb("bias_all", [128, 1152], F32)
    scratch_d = p.dtmp("scr", [16, 2], F32)

    psS = [p.ps(f"psS{i}", [128, 512]) for i in range(4)]
    psO = [p.ps(f"psO{i}", [128, 512]) for i in range(2)]
    psR = [p.ps(f"psR{i}", [1, 512]) for i in range(2)]

    ldcv = p.cnt["ldc"]
    for e in ("te", "sc", "ve"):
        p.wait(e, "ldc", ldcv)

    qn2t = qn2_sb
    if mla:
        p.rop("ve", lambda eng: eng.tensor_tensor(qn2_sb[:], qn2_sb[:], qrn2_sb[:], ALU.add))
        p.chain("ve")
        for j8 in range(8):
            p.op("te", lambda eng, j8=j8: eng.matmul(
                psS[2][0:16, 0:512], orw[0:1, 0:16], krn2_sb[0:1, j8 * 512:(j8 + 1) * 512],
                start=True, stop=True), inc=("krb", 1))
            p.wait("ve", "krb", p.cnt["krb"])
            p.op("ve", lambda eng, j8=j8: eng.tensor_tensor(
                kn2_sb[:, j8 * 512:(j8 + 1) * 512], kn2_sb[:, j8 * 512:(j8 + 1) * 512],
                psS[2][0:16, 0:512], ALU.add), inc=("krbv", 1))
            p.wait("te", "krbv", p.cnt["krbv"])
        p.wait("ve", "krbv", p.cnt["krbv"])
    p.rop("ve", lambda eng: eng.reduce_max(small[:, 0:1], kn2_sb[:], axis=AX.X))
    p.chain("ve")
    p.rop("ve", lambda eng: eng.reduce_max(small[:, 1:2], qn2_sb[:, 0:512], axis=AX.X))
    p.chain("ve")
    p.rop("ve", lambda eng: eng.reduce_max(small[:, 2:3], qn2_sb[:, 512:1024], axis=AX.X))
    p.chain("ve")
    p.rop("ve", lambda eng: eng.tensor_scalar(small[:, 4:6], small[:, 1:3],
                                              small[:, 0:1], None, ALU.mult))
    p.chain("ve")
    p.op("ve", lambda eng: eng.tensor_copy(small[:, 6:8], small[:, 4:6]),
         inc=("bnd", 1))
    p.wait("sc", "bnd", p.cnt["bnd"])
    p.rop("sc", lambda eng: eng.activation(small[:, 6:8], small[:, 6:8], AF.Ln))
    p.chain("sc")
    p.op("sc", lambda eng: eng.activation(small[:, 6:8], small[:, 6:8],
                                          AF.Exp, scale=0.5), inc=("bnd", 1))
    p.wait("ve", "bnd", p.cnt["bnd"])
    p.op("ve", lambda eng: eng.tensor_scalar(small[:, 6:8], small[:, 6:8],
                                             -float(scale), None, ALU.mult),
         inc=("bnd", 1))
    p.wait("sy", "bnd", p.cnt["bnd"])
    p.dma("sy", scratch_d[:], small[:, 6:8], "bncA")
    p.wait("sy", "bncA", p.cnt["bncA"])
    p.dma("sy", negrow[:], scratch_d.rearrange("h t -> (h t)")[None, :] if False else scratch_d.rearrange("(o h) t -> o (h t)", o=1), "bncB")
    p.wait("te", "bncB", p.cnt["bncB"])
    p.op("te", lambda eng: eng.matmul(psS[0][:, 0:32], orw[0:1, :], negrow[0:1, :],
                                      start=True, stop=True), inc=("bnd", 1))
    p.wait("ve", "bnd", p.cnt["bnd"])
    p.rop("ve", lambda eng: eng.tensor_copy(negBb[:], psS[0][:, 0:32]))
    p.chain("ve")
    p.rop("ve", lambda eng: eng.tensor_scalar(negBpB[:], negBb[:], BIG, None, ALU.add))
    p.chain("ve")
    p.wait("te", "chain_ve", p.cnt["chain_ve"])
    p.op("te", lambda eng: eng.matmul(psS[1][:, 0:64], orw[0:1, :], vis_sb[0:1, :],
                                      start=True, stop=True), inc=("bnd", 1))
    p.wait("ve", "bnd", p.cnt["bnd"])
    p.rop("ve", lambda eng: eng.tensor_copy(visb[:], psS[1][:, 0:64]))
    p.chain("ve")
    for t_ in range(2):
        for H in range(16):
            base = (t_ * 16 + H) * 36
            col = 2 * H + t_
            p.rop("ve", lambda eng, base=base, col=col, t_=t_: eng.tensor_scalar(
                bias_all[:, base:base + 32], visb[:, t_ * 32:(t_ + 1) * 32],
                negBpB[:, col:col + 1], BIG, ALU.mult, ALU.subtract))
            p.rop("ve", lambda eng, base=base, col=col: eng.tensor_scalar(
                bias_all[:, base + 32:base + 36],
                visb[:, 0:4] if False else negBb[:, col:col + 1].rearrange("p one -> p one") if False else visb[:, 0:4],
                negBb[:, col:col + 1], None, ALU.mult) if False else None)
            for dj in range(4):
                p.rop("ve", lambda eng, base=base, col=col, dj=dj: eng.tensor_copy(
                    bias_all[:, base + 32 + dj:base + 33 + dj], negBb[:, col:col + 1]))
    p.op("ve", lambda eng: eng.tensor_copy(small[:, 3:4], small[:, 0:1]),
         inc=("bsv", 1))
    p.wait("sc", "bsv", p.cnt["bsv"])

    kst = p.instream("kst", 3)
    if mla:
        xst = p.instream("xst", 2)
        oo = p.outstream("oo", 2)
    else:
        oo = p.outstream("oo", 2)

    scp = 0
    avp = 0
    oev = 0
    scp_at = {}
    avp_at = {}

    def q_ap(H, t_, mla_part):
        if not mla_part:
            return qc_sb[:, H * TPC + t_ * 512: H * TPC + t_ * 512 + 512]
        base = (H // 2) * TPC + t_ * 512
        r0 = 64 * (H % 2)
        return qr_sb[r0:r0 + 64, base:base + 512]

    vem_at = {}

    def emit_sumav(hb, t_, j, sl, tok):
        nonlocal_avp = None
        for h in range(2):
            pslot = pT[:, (2 * h + j % 2) * 512:(2 * h + j % 2 + 1) * 512]
            if j >= 32:
                p.wait("te", "vem", vem_at[(hb, t_, j, h)])
            else:
                p.wait("te", "scp", scp_at[(hb, t_, j, h)])
            p.op("te", lambda eng, pslot=pslot, h=h, j=j: eng.matmul(
                psR[h][0:1, :], ocb[:, 0:1], pslot,
                start=(j == 0), stop=(j == 35)))
            incs2 = [("avp", 1)]
            if h == 1:
                incs2.append(kst.free(tok))
            p.op("te", lambda eng, pslot=pslot, h=h, j=j, sl=sl: eng.matmul(
                psO[h][:], vbuf[:, sl * 256 + h * 128:sl * 256 + h * 128 + 128],
                pslot, start=(j == 0), stop=(j == 35)), inc=incs2)
            avp_at[(hb, t_, j, h)] = p.cnt["avp"]

    p.wait("te", "ldq", p.cnt["ldq"])
    p.wait("te", "chain_ve", p.cnt["chain_ve"])
    p.wait("te", "bsv", p.cnt["bsv"])
    for hb in range(8):
        for t_ in range(2):
            pend = None
            for j in range(NSLOT := 36):
                sl = kst.i % 3
                if j < 32:
                    ksrc = KCB[j]; vsrc = VTB[j]
                else:
                    jd = 4 * t_ + (j - 32)
                    ksrc = KCD[jd]; vsrc = VTD[jd]
                tok = kst.load("sy", [
                    (kbuf[:, sl * 256:sl * 256 + 128], ksrc[256 * hb:256 * hb + 128, :]),
                    (kbuf[:, sl * 256 + 128:sl * 256 + 256], ksrc[256 * hb + 128:256 * hb + 256, :]),
                    (vbuf[:, sl * 256:(sl + 1) * 256], vsrc[:, 256 * hb:256 * hb + 256]),
                ])
                kst.wait("te", tok)
                for h in range(2):
                    H = 2 * hb + h
                    pS = psS[2 * h + j % 2]
                    if j >= 2:
                        p.wait("te", "scp", scp_at[(hb, t_, j - 2, h)])
                    elif (hb, t_) != (0, 0):
                        prev = (hb, t_ - 1) if t_ == 1 else (hb - 1, 1)
                        p.wait("te", "scp", scp_at[(prev[0], prev[1], 34 + (j % 2), h)])
                    incs = [("spe", 1)]
                    if not mla:
                        p.op("te", lambda eng, pS=pS, sl=sl, h=h, H=H, t_=t_: eng.matmul(
                            pS[:], kbuf[:, sl * 256 + h * 128:sl * 256 + h * 128 + 128],
                            q_ap(H, t_, False), start=True, stop=True), inc=incs)
                    else:
                        p.op("te", lambda eng, pS=pS, sl=sl, h=h, H=H, t_=t_: eng.matmul(
                            pS[:], kbuf[:, sl * 256 + h * 128:sl * 256 + h * 128 + 128],
                            q_ap(H, t_, False), start=True, stop=False))
                        if j < 32:
                            krap = krf_sb[64 * (H % 2):64 * (H % 2) + 64, 128 * j:128 * j + 128]
                        else:
                            jd = 4 * t_ + (j - 32)
                            krap = krd_sb[64 * (H % 2):64 * (H % 2) + 64, 128 * jd:128 * jd + 128]
                        p.op("te", lambda eng, pS=pS, krap=krap, H=H, t_=t_: eng.matmul(
                            pS[:], krap, q_ap(H, t_, True), start=False, stop=True),
                            inc=incs)
                    p.wait("sc", "spe", p.cnt["spe"])
                    if scp >= 4:
                        p.wait("sc", "avp", avp_at.get((hb, t_, j - 2, h), avp))
                    bcol = bias_all[:, (t_ * 16 + H) * 36 + j:(t_ * 16 + H) * 36 + j + 1]
                    pslot = pT[:, (2 * h + j % 2) * 512:(2 * h + j % 2 + 1) * 512]
                    p.op("sc", lambda eng, pslot=pslot, pS=pS, bcol=bcol: eng.activation(
                        pslot, pS[:], AF.Exp, bias=bcol, scale=float(scale)),
                        inc=("scp", 1))
                    scp += 1
                    scp_at[(hb, t_, j, h)] = p.cnt["scp"]
                    if j >= 32:
                        p.wait("ve", "scp", p.cnt["scp"])
                        p.op("ve", lambda eng, pslot=pslot, t_=t_, j=j: eng.tensor_tensor(
                            pslot, pslot, tri[:, (j - 32) * 512:(j - 31) * 512],
                            ALU.mult), inc=("vem", 1))
                        vem_at[(hb, t_, j, h)] = p.cnt["vem"]
                    avp += 1
                if pend is not None:
                    emit_sumav(*pend)
                pend = (hb, t_, j, sl, tok)
            emit_sumav(*pend)
            avv = p.cnt["avp"]
            for h in range(2):
                H = 2 * hb + h
                p.wait("ve", "avp", avv)
                p.rop("ve", lambda eng, h=h: eng.tensor_copy(
                    rows[0:1, h * 512:(h + 1) * 512], psR[h][0:1, :]))
                p.chain("ve")
                p.op("ve", lambda eng, h=h: eng.reciprocal(
                    rows[0:1, 1024 + h * 512:1024 + (h + 1) * 512],
                    rows[0:1, h * 512:(h + 1) * 512]), inc=("rcp", 1))
            p.wait("te", "rcp", p.cnt["rcp"])
            for h in range(2):
                p.op("te", lambda eng, h=h: eng.matmul(
                    psS[h][:, 0:512], orw[0:1, :],
                    rows[0:1, 1024 + h * 512:1024 + (h + 1) * 512],
                    start=True, stop=True), inc=("rbc", 1))
            p.wait("ve", "rbc", p.cnt["rbc"])
            for h in range(2):
                H = 2 * hb + h
                p.rop("ve", lambda eng, h=h: eng.tensor_copy(
                    rsb[:, h * 512:(h + 1) * 512], psS[h][:, 0:512]))
                p.chain("ve")
                so = oo.ready("ve")
                if mla:
                    sx = xst.i % 2
                    tokx = xst.load("sy", (xrb[:, sx * 512:(sx + 1) * 512],
                                           xT[H * 128:H * 128 + 128, t_ * 512:t_ * 512 + 512]))
                    xst.wait("ve", tokx)
                    p.rop("ve", lambda eng, h=h, so=so: eng.tensor_tensor(
                        och[:, so * 512:(so + 1) * 512], psO[h][:],
                        rsb[:, h * 512:(h + 1) * 512], ALU.mult))
                    p.chain("ve")
                    p.op("ve", lambda eng, so=so, sx=sx: eng.tensor_tensor(
                        och[:, so * 512:(so + 1) * 512],
                        och[:, so * 512:(so + 1) * 512],
                        xrb[:, sx * 512:(sx + 1) * 512], ALU.add),
                        inc=[("oev", 1), xst.free(tokx)])
                    oo.store("sy", xoT[H * 128:H * 128 + 128, t_ * 512:t_ * 512 + 512],
                             och[:, so * 512:(so + 1) * 512], after=("oev", p.cnt["oev"]))
                else:
                    p.op("ve", lambda eng, h=h, so=so: eng.tensor_tensor(
                        och[:, so * 512:(so + 1) * 512], psO[h][:],
                        rsb[:, h * 512:(h + 1) * 512], ALU.mult), inc=("oev", 1))
                    oo.store("sy", OF[H * 128:H * 128 + 128, t_ * 512:t_ * 512 + 512],
                             och[:, so * 512:(so + 1) * 512], after=("oev", p.cnt["oev"]))
                oev += 1
            p.wait("te", "oev", p.cnt["oev"])

    p.finish("gp")
    p.finalize()
    return p


def build_mha_pre():
    p = Prog("mhapre")
    xT = p.din("xT", [D, TPC], F32)
    WQ = p.din("WQ", [D, D], BF16)
    WK = p.din("WK", [D, D], BF16)
    WV = p.din("WV", [D, D], BF16)
    BQ = p.din("BQ", [1, D], BF16)
    BK = p.din("BK", [1, D], BF16)
    BV = p.din("BV", [1, D], BF16)
    LN1W = p.din("LN1W", [D, 1], F32)
    LN1B = p.din("LN1B", [D, 1], F32)
    COSQ = p.din("COSQ", [TPC, 1024], F32)
    SINQ = p.din("SINQ", [TPC, 1024], F32)
    H0 = p.dout("H0", [D, TPC], F32)
    Q_TM = p.dout("Q_TM", [TPC, D], BF16)
    K_TM = p.dout("K_TM", [TPC, D], BF16)
    VT = p.dout("VT", [TPC, D], BF16)
    QN2T = p.dout("QN2T", [TPC, 16], F32)
    KN2T = p.dout("KN2T", [TPC, 16], F32)

    oc = p.sb("oc", [128, 1], F32)
    orw = p.sb("orw", [1, 128], F32)
    orb = p.sb("orb", [1, 128], BF16)
    p.dma("sy", oc[:], p.inline(np.ones((128, 1), np.float32), "onesc")[:], "ldc")
    p.dma("sy", orw[:], p.inline(np.ones((1, 128), np.float32), "onesr")[:], "ldc")
    p.dma("sy", orb[:], p.inline(np.ones((1, 128), BF), "onesrb")[:], "ldc")
    p.consts([1e-5])
    ln1w = load_bias_cols(p, "ln1w", LN1W, 16)
    ln1b = load_bias_cols(p, "ln1b", LN1B, 16)
    bq_sb = p.sb("bq_sb", [1, D], BF16)
    bk_sb = p.sb("bk_sb", [1, D], BF16)
    bv_sb = p.sb("bv_sb", [1, D], BF16)
    p.dma("sy", bq_sb[:], BQ[:], "ldc")
    p.dma("sy", bk_sb[:], BK[:], "ldc")
    p.dma("sy", bv_sb[:], BV[:], "ldc")
    for e in ("te", "sc", "ve"):
        p.wait(e, "ldc", p.cnt["ldc"])

    xbuf = p.sb("xbuf", [128, 4 * 512], F32)
    sqt = p.sb("sqt", [128, 4 * 512], F32)
    abuf = p.sb("abuf", [128, 4 * 512], F32)
    row6 = p.sb("row6", [1, 6 * 512], F32)
    Rb = p.sb("Rb", [128, 512], F32)
    Mb = p.sb("Mb", [128, 512], F32)
    R2b = p.sb("R2b", [128, 512], F32)
    h0t = p.sb("h0t", [128, 2 * 512], F32)
    xln = p.sb("xln", [128, 16 * TPC], BF16)
    wsl = [p.sb(f"wsl{i}", [128, 16 * 512], BF16) for i in range(2)]
    qtm = p.sb("qtm", [128, 8 * D], BF16)
    cosq_sb = p.sb("cosq_sb", [128, 2 * 1024], F32)
    sinq_sb = p.sb("sinq_sb", [128, 2 * 1024], F32)
    rsc = p.sb("rsc", [128, 2048], BF16)
    qsq = p.sb("qsq", [128, 2048], F32)
    qnred = p.sb("qnred", [128, 2 * 16], F32)
    vtt = p.sb("vtt", [128, 2 * 512], BF16)

    pb = [p.ps(f"pb{i}", [128, 512]) for i in range(8)]

    ho = p.outstream("ho", 2)

    for half in range(2):
        hs = half * 512
        tag = f"d{half}"
        if half == 1:
            p.wait("sy", "lnpe_d0", p.cnt["lnpe_d0"])
            p.wait("sc", "lnpe_d0", p.cnt["lnpe_d0"])
            p.wait("sy", "xlnap", p.cnt["xlnap"])
        emit_ln_stats(p, tag, lambda kc, hs=hs: xT[kc * 128:(kc + 1) * 128, hs:hs + 512],
                      512, NKD, (xbuf, sqt), pb[0], pb[1], row6, Rb, Mb,
                      oc, orw, 1e-5)
        s_ = row6[0:1, 0:512]; q_ = row6[0:1, 512:1024]
        r_ = row6[0:1, 1024:1536]
        v_ = row6[0:1, 2048:2560]; r2_ = row6[0:1, 2560:3072]
        p.rop("ve", lambda eng: eng.tensor_tensor(v_, s_, s_, ALU.mult))
        p.chain("ve")
        p.rop("ve", lambda eng: eng.tensor_sub(v_, q_, v_))
        p.chain("ve")
        p.rop("ve", lambda eng: eng.tensor_tensor(r2_, r_, r_, ALU.mult))
        p.chain("ve")
        p.op("ve", lambda eng: eng.tensor_tensor(r2_, r2_, v_, ALU.mult),
             inc=("r2v_" + tag, 1))
        p.wait("sc", "r2v_" + tag, p.cnt["r2v_" + tag])
        p.rop("sc", lambda eng: eng.activation(r2_, r2_, AF.Ln, bias=1e-5))
        p.chain("sc")
        p.op("sc", lambda eng: eng.activation(r2_, r2_, AF.Exp, scale=-0.5),
             inc=("r2s_" + tag, 1))
        p.wait("te", "r2s_" + tag, p.cnt["r2s_" + tag])
        p.op("te", lambda eng: eng.matmul(pb[2][:, 0:512], orw[0:1, :], r2_,
                                          start=True, stop=True), inc=("r2p_" + tag, 1))
        p.wait("ve", "r2p_" + tag, p.cnt["r2p_" + tag])
        p.op("ve", lambda eng: eng.tensor_copy(R2b[:], pb[2][:, 0:512]),
             inc=("r2b_" + tag, 1))
        p.wait("ve", "r2b_" + tag, p.cnt["r2b_" + tag])
        st = p.instream("hx" + tag, 4)
        for kc in range(NKD):
            sl = kc % 4
            tok = st.load("sy", (abuf[:, sl * 512:(sl + 1) * 512],
                                 xT[kc * 128:(kc + 1) * 128, hs:hs + 512]))
            st.wait("ve", tok)
            hsl = ho.ready("ve")
            h0c = h0t[:, hsl * 512:(hsl + 1) * 512]
            x_ = abuf[:, sl * 512:(sl + 1) * 512]
            dst = xln[:, kc * TPC + hs:kc * TPC + hs + 512]
            p.rop("ve", lambda eng, h0c=h0c, x_=x_: eng.tensor_tensor(h0c, x_, Rb[:], ALU.mult))
            p.chain("ve")
            p.op("ve", lambda eng, h0c=h0c: eng.tensor_tensor(h0c, h0c, Mb[:], ALU.add),
                 inc=[("h0v", 1), st.free(tok)])
            p.wait("ve", "h0v", p.cnt["h0v"])
            ho.store("sy", H0[kc * 128:(kc + 1) * 128, hs:hs + 512], h0c,
                     after=("h0v", p.cnt["h0v"]))
            p.rop("ve", lambda eng, dst=dst, h0c=h0c: eng.tensor_tensor(
                dst, h0c, R2b[:], ALU.mult))
            p.chain("ve")
            p.op("ve", lambda eng, dst=dst, kc=kc: eng.tensor_scalar(
                dst, dst, ln1w[:, kc:kc + 1], ln1b[:, kc:kc + 1],
                ALU.mult, ALU.add), inc=("xlnap", 1))
    p.wait("te", "xlnap", p.cnt["xlnap"])

    wst = p.instream("wst", 2)
    cq = p.instream("cqd", 2)
    qno = p.outstream("qno", 2)
    tmo = p.outstream("tmo", 2)

    def proj_tm(which, Wap, Bap, OUTap, N2ap, do_rope, call_idx=0, prev=None):
        for n in range(4):
            b = wst.i % 2
            tokw = wst.load("sy", [(wsl[b][:, k * 512:(k + 1) * 512],
                                    Wap[k * 128:(k + 1) * 128, n * 512:(n + 1) * 512])
                                   for k in range(NKD)])
            wst.wait("te", tokw)
            for tc in range(8):
                gi = n * 8 + tc
                pv = pb[4 + gi % 2]
                if gi >= 2:
                    p.wait("te", "tmev_" + which, gi - 1)
                for k in range(NKD):
                    p.op("te", lambda eng, k=k, b=b, tc=tc, pv=pv: eng.matmul(
                        pv[:], xln[:, k * TPC + tc * 128:k * TPC + tc * 128 + 128],
                        wsl[b][:, k * 512:(k + 1) * 512],
                        start=(k == 0), stop=False))
                incs = [("tmpe_" + which, 1)]
                if tc == 7:
                    incs.append(wst.free(tokw))
                p.op("te", lambda eng, pv=pv, n=n: eng.matmul(
                    pv[:], orb[0:1, :], Bap[0:1, n * 512:(n + 1) * 512],
                    start=False, stop=True), inc=incs)
                p.wait("sc", "tmpe_" + which, p.cnt["tmpe_" + which])
                if call_idx > 0 and n == 0:
                    p.wait("sc", f"tmos{tc % 2}", 64 * call_idx)
                    if prev is not None:
                        p.wait("sc", "n2s_" + prev, 8)
                p.op("sc", lambda eng, pv=pv, tc=tc, n=n: eng.copy(
                    qtm[:, tc * D + n * 512:tc * D + (n + 1) * 512], pv[:]),
                    inc=("tmev_" + which, 1))
        p.wait("ve", "tmev_" + which, p.cnt["tmev_" + which])
        p.wait("sy", "tmev_" + which, p.cnt["tmev_" + which])
        for tc in range(8):
            ts0 = tc * 128
            src_ = qtm[:, tc * D:(tc + 1) * D]
            if do_rope:
                csl = cq.i % 2
                tokc = cq.load("sy", [(cosq_sb[:, csl * 1024:(csl + 1) * 1024], COSQ[ts0:ts0 + 128, :]),
                                      (sinq_sb[:, csl * 1024:(csl + 1) * 1024], SINQ[ts0:ts0 + 128, :])])
                cq.wait("ve", tokc)
                vv_ = src_.rearrange("p (h t f) -> p h t f", h=16, t=2, f=64)
                x1 = vv_[:, :, 0, :]; x2 = vv_[:, :, 1, :]
                cqv = cosq_sb[:, csl * 1024:(csl + 1) * 1024].rearrange("p (h f) -> p h f", h=16, f=64)
                sqv = sinq_sb[:, csl * 1024:(csl + 1) * 1024].rearrange("p (h f) -> p h f", h=16, f=64)
                A = rsc[:, 0:1024].rearrange("p (h f) -> p h f", h=16, f=64)
                B = rsc[:, 1024:2048].rearrange("p (h f) -> p h f", h=16, f=64)
                p.rop("ve", lambda eng, A=A, x1=x1, cqv=cqv: eng.tensor_tensor(A, x1, cqv, ALU.mult))
                p.rop("ve", lambda eng, B=B, x1=x1, sqv=sqv: eng.tensor_tensor(B, x1, sqv, ALU.mult))
                p.chain("ve")
                p.rop("ve", lambda eng, x1=x1, x2=x2, sqv=sqv: eng.tensor_tensor(x1, x2, sqv, ALU.mult))
                p.chain("ve")
                p.rop("ve", lambda eng, x1=x1, A=A: eng.tensor_sub(x1, A, x1))
                p.rop("ve", lambda eng, x2=x2, cqv=cqv: eng.tensor_tensor(x2, x2, cqv, ALU.mult))
                p.chain("ve")
                incs = [("rpd_" + which, 1)] + [cq.free(tokc)]
                p.op("ve", lambda eng, x2=x2, B=B: eng.tensor_add(x2, x2, B), inc=incs)
                p.wait("ve", "rpd_" + which, p.cnt["rpd_" + which])
            else:
                p.op("ve", lambda eng: eng.tensor_copy(rsc[:, 0:4], rsc[:, 4:8]),
                     inc=("rpd_" + which, 1))
                p.wait("ve", "rpd_" + which, p.cnt["rpd_" + which])
            tmo.store("sy", OUTap[ts0:ts0 + 128, :], src_,
                      after=("rpd_" + which, p.cnt["rpd_" + which]))
            if N2ap is not None:
                p.wait("sc", "rpd_" + which, p.cnt["rpd_" + which])
                if tc >= 1:
                    p.wait("sc", "n2v_" + which, tc)
                p.op("sc", lambda eng, src_=src_: eng.activation(qsq[:], src_, AF.Square),
                     inc=("n2s_" + which, 1))
                p.wait("ve", "n2s_" + which, p.cnt["n2s_" + which])
                sn = qno.ready("ve")
                qv = qsq.rearrange("p (h f) -> p h f", h=16, f=128)
                p.op("ve", lambda eng, qv=qv, sn=sn: eng.reduce_sum(
                    qnred[:, sn * 16:(sn + 1) * 16], qv, axis=AX.X),
                    inc=("n2v_" + which, 1))
                qno.store("sy", N2ap[ts0:ts0 + 128, :], qnred[:, sn * 16:(sn + 1) * 16],
                          after=("n2v_" + which, p.cnt["n2v_" + which]))

    proj_tm("q", WQ, bq_sb, Q_TM, QN2T, True, 0)
    proj_tm("k", WK, bk_sb, K_TM, KN2T, True, 1, "q")
    proj_tm("v", WV, bv_sb, VT, None, False, 2, "k")
    p.finish("gp")
    p.finalize()
    return p


def build_mha_post():
    p = Prog("mhapost")
    OF = p.din("OF", [D, TPC], BF16)
    WO = p.din("WO", [16, 128, D], BF16)
    BO = p.din("BO", [D, 1], F32)
    H0 = p.din("H0", [D, TPC], F32)
    xT = p.din("xT", [D, TPC], F32)
    LN2W = p.din("LN2W", [D, 1], F32)
    LN2B = p.din("LN2B", [D, 1], F32)
    WG4 = p.din("WG4", [D, 4], BF16)
    WE1 = p.din("WE1", [4, 16, 128, D], BF16)
    BE1 = p.din("BE1", [4 * D, 1], F32)
    WE2 = p.din("WE2", [4, 16, 128, D], BF16)
    BE2 = p.din("BE2", [4, D], BF16)
    xoT = p.dout("xoT", [D, TPC], F32)
    hdram = p.dtmp("hdram", [D, TPC], F32)
    xhdram = p.dtmp("xhdram", [D, TPC], F32)

    oc = p.sb("oc", [128, 1], F32)
    orw = p.sb("orw", [1, 128], F32)
    idf = p.sb("idf", [128, 128], F32)
    p.dma("sy", oc[:], p.inline(np.ones((128, 1), np.float32), "onesc")[:], "ldc")
    p.dma("sy", orw[:], p.inline(np.ones((1, 128), np.float32), "onesr")[:], "ldc")
    p.dma("sy", idf[:], p.inline(np.eye(128, dtype=np.float32), "idfi")[:], "ldc")
    p.consts([1e-5])
    bo_c = load_bias_cols(p, "bo", BO, 16)
    ln2w = load_bias_cols(p, "ln2w", LN2W, 16)
    ln2b = load_bias_cols(p, "ln2b", LN2B, 16)
    be1_c = load_bias_cols(p, "be1", BE1, 64)
    wg4_sb = p.sb("wg4_sb", [128, 16 * 4], BF16)
    for k in range(NKD):
        p.dma("sy", wg4_sb[:, 4 * k:4 * k + 4], WG4[k * 128:(k + 1) * 128, :], "ldc")
    be2rr = [p.sb(f"be2r{e}", [1, D], BF16) for e in range(4)]
    for e in range(4):
        p.dma("sy", be2rr[e][:], BE2[e:e + 1, :], "ldc")
    big = p.sb("big", [128, 32768], BF16)
    of_sb = big
    for h in range(16):
        p.dma("sy", of_sb[:, h * TPC:(h + 1) * TPC], OF[h * 128:(h + 1) * 128, :], "ldc")
    for e in ("te", "sc", "ve"):
        p.wait(e, "ldc", p.cnt["ldc"])

    xbuf = p.sb("xbuf", [128, 4 * 512], F32)
    sqt = p.sb("sqt", [128, 4 * 512], F32)
    abuf = p.sb("abuf", [128, 4 * 512], F32)
    bbuf = p.sb("bbuf", [128, 4 * 512], F32)
    row4 = p.sb("row4", [1, 4 * 512], F32)
    Rb = p.sb("Rb", [128, 512], F32)
    Mb = p.sb("Mb", [128, 512], F32)
    hbuf = p.sb("hbuf", [128, 2 * 512], F32)
    xhb = p.sb("xhb", [128, 2 * 512], F32)
    hln2 = p.sb("hln2", [128, 16 * TPC], BF16)
    wt = [p.sb(f"wt{i}", [128, 16 * 128], BF16) for i in range(2)]
    pb = [p.ps(f"pb{i}", [128, 512]) for i in range(8)]

    ho_ = p.outstream("hoo", 2)
    xho = p.outstream("xho", 2)
    h0st = p.instream("h0st", 2)
    xst = p.instream("xst2", 2)

    ctx = ProjCtx(p, wt, pb[0:4])

    def evac_f1(m, half, ps_, pev):
        hs = half * 512
        p.wait("sc", ctx.pe, pev)
        hsl = ho_.ready("sc")
        p.op("sc", lambda eng, m=m, ps_=ps_, hsl=hsl: eng.activation(
            hbuf[:, hsl * 512:(hsl + 1) * 512], ps_[:], AF.Identity,
            bias=bo_c[:, m:m + 1]), inc=(ctx.ev, 1))
        sl0 = h0st.i % 2
        tok0 = h0st.load("sy", (abuf[:, 2048 + sl0 * 512:2048 + (sl0 + 1) * 512]
                                if False else bbuf[:, sl0 * 512:(sl0 + 1) * 512],
                                H0[m * 128:(m + 1) * 128, hs:hs + 512]))
        h0st.wait("ve", tok0)
        p.wait("ve", ctx.ev, p.cnt[ctx.ev])
        p.op("ve", lambda eng, hsl=hsl, sl0=sl0: eng.tensor_tensor(
            hbuf[:, hsl * 512:(hsl + 1) * 512], hbuf[:, hsl * 512:(hsl + 1) * 512],
            bbuf[:, sl0 * 512:(sl0 + 1) * 512], ALU.add),
            inc=[("hv", 1), h0st.free(tok0)])
        ho_.store("sy", hdram[m * 128:(m + 1) * 128, hs:hs + 512],
                  hbuf[:, hsl * 512:(hsl + 1) * 512], after=("hv", p.cnt["hv"]))
        slx = xst.i % 2
        tokx = xst.load("sy", (bbuf[:, 1024 + slx * 512:1024 + (slx + 1) * 512],
                               xT[m * 128:(m + 1) * 128, hs:hs + 512]))
        xst.wait("ve", tokx)
        xsl = xho.ready("ve")
        p.op("ve", lambda eng, hsl=hsl, slx=slx, xsl=xsl: eng.scalar_tensor_tensor(
            xhb[:, xsl * 512:(xsl + 1) * 512], hbuf[:, hsl * 512:(hsl + 1) * 512],
            0.5, bbuf[:, 1024 + slx * 512:1024 + (slx + 1) * 512],
            ALU.mult, ALU.add), inc=[("xhv", 1), xst.free(tokx)])
        xho.store("sy", xhdram[m * 128:(m + 1) * 128, hs:hs + 512],
                  xhb[:, xsl * 512:(xsl + 1) * 512], after=("xhv", p.cnt["xhv"]))

    emit_proj_fm(p, ctx, WO, NKD, NKD,
                 lambda k, hs: of_sb[:, k * TPC + hs:k * TPC + hs + 512], evac_f1)
    p.wait("te", "xhv", p.cnt["xhv"])
    p.wait("te", "hv", p.cnt["hv"])

    for s_ in range(2):
        p.wait("sy", f"hoos{s_}", p.cnt.get(f"hoos{s_}", 0))
    for half in range(2):
        hs = half * 512
        tag = f"p{half}"
        if half == 1:
            p.wait("sy", "lnpe_p0", p.cnt["lnpe_p0"])
            p.wait("sc", "lnpe_p0", p.cnt["lnpe_p0"])
            p.wait("sy", "xlnap", p.cnt["xlnap"])
        emit_ln_stats(p, tag, lambda kc, hs=hs: hdram[kc * 128:(kc + 1) * 128, hs:hs + 512],
                      512, NKD, (xbuf, sqt), pb[0], pb[1], row4, Rb, Mb,
                      oc, orw, 1e-5)
        emit_ln_apply(p, tag, lambda kc, hs=hs: hdram[kc * 128:(kc + 1) * 128, hs:hs + 512],
                      [hln2[:, kc * TPC + hs:kc * TPC + hs + 512] for kc in range(NKD)],
                      512, NKD, abuf, Rb, Mb,
                      gain=[ln2w[:, kc:kc + 1] for kc in range(NKD)],
                      bias=[ln2b[:, kc:kc + 1] for kc in range(NKD)], app="xlnap")
    p.wait("te", "xlnap", p.cnt["xlnap"])

    lgt_all = p.sb("lgt_all", [128, 32], F32)
    raw_all = p.sb("raw_all", [128, 32], F32)
    m1w = p.sb("m1w", [128, 32], F32)
    m2w = p.sb("m2w", [128, 32], F32)
    comb_all = p.sb("comb_all", [128, 32], F32)
    gax = p.sb("gax", [128, 64], F32)
    crowb = p.sb("crowb", [1, 4 * TPC], BF16)
    cbc = p.sb("cbc", [128, 4 * TPC], BF16)
    mx = gax[:, 0:8]; nmx = gax[:, 8:16]; sm = gax[:, 16:24]; rc = gax[:, 24:32]
    sec = gax[:, 32:40]; dif = gax[:, 40:48]; w1 = gax[:, 48:56]; w2 = gax[:, 56:64]
    for tt in range(8):
        for k in range(NKD):
            p.op("te", lambda eng, k=k, tt=tt: eng.matmul(
                pb[4][:, 4 * tt:4 * tt + 4],
                hln2[:, k * TPC + tt * 128:k * TPC + tt * 128 + 128],
                wg4_sb[:, 4 * k:4 * k + 4], start=(k == 0), stop=(k == NKD - 1)),
                inc=("lgp", 1) if (tt == 7 and k == NKD - 1) else None)
    p.wait("ve", "lgp", p.cnt["lgp"])
    p.op("ve", lambda eng: eng.tensor_copy(lgt_all[:], pb[4][:, 0:32]), inc=("lgv", 1))
    p.wait("ve", "lgv", p.cnt["lgv"])
    lg_g = lgt_all.rearrange("p (t e) -> p t e", t=8)
    raw_g = raw_all.rearrange("p (t e) -> p t e", t=8)
    m2_g = m2w.rearrange("p (t e) -> p t e", t=8)
    p.rop("ve", lambda eng: eng.reduce_max(mx, lg_g, axis=AX.X))
    p.chain("ve")
    p.op("ve", lambda eng: eng.tensor_scalar(nmx, mx, -1.0, None, ALU.mult),
         inc=("g1", 1))
    p.wait("sc", "g1", p.cnt["g1"])
    for tt in range(8):
        p.op("sc", lambda eng, tt=tt: eng.activation(
            raw_all[:, 4 * tt:4 * tt + 4], lgt_all[:, 4 * tt:4 * tt + 4], AF.Exp,
            bias=nmx[:, tt:tt + 1]), inc=("g2", 1) if tt == 7 else None)
    p.wait("ve", "g2", p.cnt["g2"])
    p.rop("ve", lambda eng: eng.reduce_sum(sm, raw_g, axis=AX.X))
    p.chain("ve")
    p.rop("ve", lambda eng: eng.reciprocal(rc, sm))
    p.chain("ve")
    for tt in range(8):
        p.rop("ve", lambda eng, tt=tt: eng.tensor_scalar(
            raw_all[:, 4 * tt:4 * tt + 4], raw_all[:, 4 * tt:4 * tt + 4],
            rc[:, tt:tt + 1], None, ALU.mult))
    p.chain("ve")
    p.rop("ve", lambda eng: eng.reduce_max(mx, raw_g, axis=AX.X))
    p.chain("ve")
    for tt in range(8):
        p.rop("ve", lambda eng, tt=tt: eng.tensor_scalar(
            m1w[:, 4 * tt:4 * tt + 4], raw_all[:, 4 * tt:4 * tt + 4],
            mx[:, tt:tt + 1], None, ALU.is_ge))
    p.chain("ve")
    p.rop("ve", lambda eng: eng.tensor_tensor(m2w[:], raw_all[:], m1w[:], ALU.mult))
    p.chain("ve")
    p.rop("ve", lambda eng: eng.tensor_sub(m2w[:], raw_all[:], m2w[:]))
    p.chain("ve")
    p.rop("ve", lambda eng: eng.reduce_max(sec, m2_g, axis=AX.X))
    p.chain("ve")
    for tt in range(8):
        p.rop("ve", lambda eng, tt=tt: eng.tensor_scalar(
            m2w[:, 4 * tt:4 * tt + 4], m2w[:, 4 * tt:4 * tt + 4],
            sec[:, tt:tt + 1], None, ALU.is_ge))
    p.chain("ve")
    p.op("ve", lambda eng: eng.tensor_sub(dif, mx, sec), inc=("g3", 1))
    p.wait("sc", "g3", p.cnt["g3"])
    p.op("sc", lambda eng: eng.activation(w1, dif, AF.Sigmoid), inc=("g4", 1))
    p.wait("ve", "g4", p.cnt["g4"])
    p.rop("ve", lambda eng: eng.tensor_scalar(w2, w1, -1.0, 1.0, ALU.mult, ALU.add))
    p.chain("ve")
    for tt in range(8):
        p.rop("ve", lambda eng, tt=tt: eng.tensor_scalar(
            m1w[:, 4 * tt:4 * tt + 4], m1w[:, 4 * tt:4 * tt + 4],
            w1[:, tt:tt + 1], None, ALU.mult))
        p.rop("ve", lambda eng, tt=tt: eng.tensor_scalar(
            m2w[:, 4 * tt:4 * tt + 4], m2w[:, 4 * tt:4 * tt + 4],
            w2[:, tt:tt + 1], None, ALU.mult))
    p.chain("ve")
    p.op("ve", lambda eng: eng.tensor_add(comb_all[:], m1w[:], m2w[:]),
         inc=("g5", 1))
    p.wait("te", "g5", p.cnt["g5"])
    crow_g = crowb.rearrange("o (e t) -> o e t", e=4)
    for tt in range(8):
        b = 5 + tt % 2
        if tt >= 2:
            p.wait("te", "ctv", p.cnt["ctv"] - 1)
        for e in range(4):
            p.op("te", lambda eng, e=e, tt=tt, b=b: eng.matmul(
                pb[b][0:1, 128 * e:128 * e + 128], comb_all[:, 4 * tt + e:4 * tt + e + 1],
                idf[:], start=True, stop=True),
                inc=("ctp", 1) if e == 3 else None)
        p.wait("ve", "ctp", p.cnt["ctp"])
        p.op("ve", lambda eng, tt=tt, b=b: eng.tensor_copy(
            crow_g[0:1, :, tt * 128:(tt + 1) * 128],
            pb[b][0:1, 0:512].rearrange("p (e c) -> p e c", e=4)), inc=("ctv", 1))
    p.wait("te", "ctv", p.cnt["ctv"])
    orb2 = p.sb("orb2", [1, 128], BF16)
    p.dma("sy", orb2[:], p.inline(np.ones((1, 128), BF), "onesrb2")[:], "cb0")
    p.wait("te", "cb0", p.cnt["cb0"])
    for e in range(4):
        for half in range(2):
            p.op("te", lambda eng, e=e, half=half: eng.matmul(
                pb[4][:, 0:512], orb2[0:1, :],
                crowb[0:1, e * TPC + half * 512:e * TPC + (half + 1) * 512],
                start=True, stop=True), inc=("cbp", 1))
            p.wait("ve", "cbp", p.cnt["cbp"])
            p.op("ve", lambda eng, e=e, half=half: eng.tensor_copy(
                cbc[:, e * TPC + half * 512:e * TPC + (half + 1) * 512],
                pb[4][:, 0:512]), inc=("cbv", 1))
            p.wait("te", "cbv", p.cnt["cbv"])
    p.wait("ve", "cbv", p.cnt["cbv"])

    hid = big
    w1st = p.instream("w1st", 2)
    w2st = p.instream("w2st", 3)
    wt2 = [p.sb(f"wt2{i}", [128, 2048], BF16) for i in range(3)]
    xhst = p.instream("xhst", 2)
    yo = p.outstream("yo2", 2)
    yb2 = xhb
    for s_ in range(2):
        p.wait("sy", f"xhos{s_}", p.cnt.get(f"xhos{s_}", 0))
    p.wait("ve", "xhv", p.cnt["xhv"])
    for s_ in range(2):
        p.wait("ve", f"xhos{s_}", p.cnt.get(f"xhos{s_}", 0))
    for half in range(2):
        hs = half * 512
        for e in range(4):
            for mi in range(NKD):
                b = w1st.i % 2
                tokw = w1st.load("sy", (wt[b][:], WE1[e, mi]))
                w1st.wait("te", tokw)
                gi = (half * 4 + e) * NKD + mi
                ps_ = pb[gi % 4]
                if gi >= 4:
                    p.wait("te", "hev", gi - 3)
                for k in range(NKD):
                    incs = []
                    if k == NKD - 1:
                        incs = [("hpe", 1), w1st.free(tokw)]
                    p.op("te", lambda eng, k=k, b=b, hs=hs, ps_=ps_: eng.matmul(
                        ps_[:], wt[b][:, k * 128:(k + 1) * 128],
                        hln2[:, k * TPC + hs:k * TPC + hs + 512],
                        start=(k == 0), stop=(k == NKD - 1)),
                        inc=incs if incs else None)
                p.wait("sc", "hpe", p.cnt["hpe"])
                p.op("sc", lambda eng, ps_=ps_, e=e, mi=mi: eng.activation(
                    hid[:, (e * 16 + mi) * 512:(e * 16 + mi + 1) * 512], ps_[:],
                    AF.Gelu, bias=be1_c[:, e * 16 + mi:e * 16 + mi + 1]),
                    inc=("hsc", 1))
                p.wait("ve", "hsc", p.cnt["hsc"])
                p.op("ve", lambda eng, e=e, mi=mi, hs=hs: eng.tensor_tensor(
                    hid[:, (e * 16 + mi) * 512:(e * 16 + mi + 1) * 512],
                    hid[:, (e * 16 + mi) * 512:(e * 16 + mi + 1) * 512],
                    cbc[:, e * TPC + hs:e * TPC + hs + 512], ALU.mult),
                    inc=("hev", 1))
        p.wait("te", "hev", p.cnt["hev"])
        for dci in range(NKD):
            gd = half * NKD + dci
            pd = pb[4 + gd % 2]
            toks = []
            for e in range(4):
                b = w2st.i % 3
                toks.append((e, b, w2st.load("sy", (wt2[b][:], WE2[e, dci]))))
            if gd >= 2:
                p.wait("te", "mev", gd - 1)
            for e, b, tokw in toks:
                w2st.wait("te", tokw)
                for k in range(NKD):
                    incs = []
                    if k == NKD - 1:
                        incs = [w2st.free(tokw)]
                    p.op("te", lambda eng, k=k, b=b, e=e, hs=hs, pd=pd, gd=gd: eng.matmul(
                        pd[:], wt2[b][:, k * 128:(k + 1) * 128],
                        hid[:, (e * 16 + k) * 512:(e * 16 + k + 1) * 512],
                        start=(e == 0 and k == 0), stop=False),
                        inc=incs if incs else None)
            for e in range(4):
                p.op("te", lambda eng, e=e, pd=pd, hs=hs, dci=dci: eng.matmul(
                    pd[:], be2rr[e][0:1, dci * 128:(dci + 1) * 128],
                    crowb[0:1, e * TPC + hs:e * TPC + hs + 512],
                    start=False, stop=(e == 3)), inc=("mpe", 1) if e == 3 else None)
            slx = xhst.i % 2
            tokx = xhst.load("sy", (bbuf[:, 1024 + slx * 512:1024 + (slx + 1) * 512],
                                    xhdram[dci * 128:(dci + 1) * 128, hs:hs + 512]))
            xhst.wait("ve", tokx)
            p.wait("ve", "mpe", p.cnt["mpe"])
            ysl = yo.ready("ve")
            p.op("ve", lambda eng, pd=pd, slx=slx, ysl=ysl: eng.scalar_tensor_tensor(
                yb2[:, ysl * 512:(ysl + 1) * 512], pd[:], 0.5,
                bbuf[:, 1024 + slx * 512:1024 + (slx + 1) * 512],
                ALU.mult, ALU.add), inc=[("mev", 1), xhst.free(tokx)])
            yo.store("sy", xoT[dci * 128:(dci + 1) * 128, hs:hs + 512],
                     yb2[:, ysl * 512:(ysl + 1) * 512], after=("mev", p.cnt["mev"]))

    p.finish("gp")
    p.finalize()
    return p


def build_la_pre():
    p = Prog("lapre")
    xT = p.din("xT", [D, TPC], F32)
    WQl = p.din("WQl", [16, 128, D], BF16)
    WKl = p.din("WKl", [16, 128, D], BF16)
    WVl = p.din("WVl", [16, 128, D], BF16)
    WF = p.din("WF", [D, 16], BF16)
    OPART = p.dout("OPART", [TPC, D], BF16)
    QS = p.dout("QS", [16, 128, 1024], BF16)
    PD = p.dout("PD", [16, 16], F32)
    SLOC = p.dout("SLOC", [16, 128, 128], F32)
    TOTD = p.dout("TOTD", [1, 16], F32)
    cumd = p.dtmp("cumd", [16, 16, 64], F32)

    oc = p.sb("oc", [128, 1], F32)
    orw = p.sb("orw", [1, 128], F32)
    idb = p.sb("idb", [128, 128], BF16)
    id64 = p.sb("id64", [64, 64], F32)
    triu = p.sb("triu", [64, 64], F32)
    triub = p.sb("triub", [64, 64], BF16)
    p.dma("sy", oc[:], p.inline(np.ones((128, 1), np.float32), "onesc")[:], "ldc")
    p.dma("sy", orw[:], p.inline(np.ones((1, 128), np.float32), "onesr")[:], "ldc")
    p.dma("sy", idb[:], p.inline(np.eye(128, dtype=BF), "idb_")[:], "ldc")
    p.dma("sy", id64[:], p.inline(np.eye(64, dtype=np.float32), "id64_")[:], "ldc")
    tri_np = (np.arange(64)[:, None] <= np.arange(64)[None, :])
    p.dma("sy", triu[:], p.inline(tri_np.astype(np.float32), "triu_")[:], "ldc")
    p.dma("sy", triub[:], p.inline(tri_np.astype(BF), "triub_")[:], "ldc")
    p.consts([1e-5, 1.0])
    wf_sb = p.sb("wf_sb", [128, 16 * 16], BF16)
    for k in range(NKD):
        p.dma("sy", wf_sb[:, 16 * k:16 * k + 16], WF[k * 128:(k + 1) * 128, :], "ldc")
    for e in ("te", "sc", "ve"):
        p.wait(e, "ldc", p.cnt["ldc"])

    xbuf = p.sb("xbuf", [128, 4 * 512], F32)
    sqt = p.sb("sqt", [128, 4 * 512], F32)
    abuf = p.sb("abuf", [128, 4 * 512], F32)
    row4 = p.sb("row4", [1, 4 * 512], F32)
    Rb = p.sb("Rb", [128, 512], F32)
    Mb = p.sb("Mb", [128, 512], F32)
    xln = p.sb("xln", [128, 16 * TPC], BF16)
    q_sb = p.sb("q_sb", [128, 16 * TPC], BF16)
    k_sb = p.sb("k_sb", [128, 16 * TPC], BF16)
    v_sb = p.sb("v_sb", [128, 16 * TPC], BF16)
    wt = [p.sb(f"wt{i}", [128, 16 * 128], BF16) for i in range(2)]
    pb = [p.ps(f"pb{i}", [128, 512]) for i in range(8)]

    for half in range(2):
        hs = half * 512
        tag = f"g{half}"
        if half == 1:
            p.wait("sy", "lnpe_g0", p.cnt["lnpe_g0"])
            p.wait("sc", "lnpe_g0", p.cnt["lnpe_g0"])
            p.wait("sy", "xlnap", p.cnt["xlnap"])
        emit_ln_stats(p, tag, lambda kc, hs=hs: xT[kc * 128:(kc + 1) * 128, hs:hs + 512],
                      512, NKD, (xbuf, sqt), pb[0], pb[1], row4, Rb, Mb,
                      oc, orw, 1e-5)
        emit_ln_apply(p, tag, lambda kc, hs=hs: xT[kc * 128:(kc + 1) * 128, hs:hs + 512],
                      [xln[:, kc * TPC + hs:kc * TPC + hs + 512] for kc in range(NKD)],
                      512, NKD, abuf, Rb, Mb, app="xlnap")
    p.wait("te", "xlnap", p.cnt["xlnap"])

    ctx = ProjCtx(p, wt, pb[0:4])

    def mk_evac(dst, act):
        def evac(m, half, ps_, pev):
            hs = half * 512
            p.wait("sc", ctx.pe, pev)
            p.op("sc", lambda eng, m=m, hs=hs, ps_=ps_: eng.activation(
                dst[:, m * TPC + hs:m * TPC + hs + 512], ps_[:], act),
                inc=(ctx.ev, 1))
        return evac

    emit_proj_fm(p, ctx, WQl, NKD, NKD,
                 lambda k, hs: xln[:, k * TPC + hs:k * TPC + hs + 512],
                 mk_evac(q_sb, AF.Silu))
    emit_proj_fm(p, ctx, WKl, NKD, NKD,
                 lambda k, hs: xln[:, k * TPC + hs:k * TPC + hs + 512],
                 mk_evac(k_sb, AF.Silu))
    emit_proj_fm(p, ctx, WVl, NKD, NKD,
                 lambda k, hs: xln[:, k * TPC + hs:k * TPC + hs + 512],
                 mk_evac(v_sb, AF.Identity))
    p.wait("te", ctx.ev, p.cnt[ctx.ev])

    ldc_ = p.sb("ldc_", [64, 16], F32)
    cumt = p.sb("cumt", [64, 16], F32)
    negcum = p.sb("negcum", [64, 16], F32)
    cumTs = p.sb("cumTs", [16, 64], F32)
    cumrow = p.sb("cumrow", [1, 1024], F32)
    erow = p.sb("erow", [1, 1024], F32)
    totrow = p.sb("totrow", [1, 16], F32)
    ecb_all = sqt[:, 0:1024]
    xb16 = xbuf[:].bitcast(BF16)
    ab16 = abuf[:].bitcast(BF16)
    sq16 = sqt[:].bitcast(BF16)
    Dm_all = xb16[0:64, 0:1024]
    W_all = xb16[0:64, 1024:2048]
    kdt_all = ab16[0:64, 0:2048]
    vtt_all = ab16[0:64, 2048:4096]
    triu16 = sq16[0:64, 2048:3072]
    qs_t = p.sb("qs_t", [128, 16 * 64], BF16)
    ecol_all = p.sb("ecol_all", [64, 16], F32)
    etb = p.sb("etb", [128, 16], F32)
    S_sb = p.sb("S_sb", [128, 16 * 128], F32)
    S_bf = p.sb("S_bf", [128, 16 * 128], BF16)
    ptot = p.sb("ptot", [1, 3 * 16], F32)
    oc_m = p.sb("oc_m", [64, 2 * 2048], BF16)
    ldsc = p.sb("ldsc", [64, 3 * 16], F32)
    p.wait("sy", "lnpe_g0", p.cnt.get("lnpe_g0", 0))
    p.wait("sy", "lnpe_g1", p.cnt.get("lnpe_g1", 0))
    p.wait("sy", "lnsq_g0", p.cnt.get("lnsq_g0", 0))
    p.wait("sy", "lnsq_g1", p.cnt.get("lnsq_g1", 0))
    p.wait("sy", "xlnap", p.cnt.get("xlnap", 0))
    p.dma("sy", triu16, p.inline(
        np.tile(tri_np.astype(BF), (1, 16)), "triu16_")[0:64, :], "ld2")
    p.wait("ve", "ld2", p.cnt.get("ld2", 0))
    p.wait("sc", "ld2", p.cnt.get("ld2", 0))

    p.rop("ve", lambda eng: eng.memset(S_sb[:], 0.0))
    p.rop("ve", lambda eng: eng.memset(S_bf[:], 0.0))
    p.op("ve", lambda eng: eng.memset(ptot[:], 0.0), inc=("sz", 1))
    p.wait("ve", "sz", p.cnt.get("sz", 0))
    p.wait("te", "sz", p.cnt.get("sz", 0))
    p.wait("sy", "sz", p.cnt.get("sz", 0))

    om = p.outstream("om", 2)
    qso = p.outstream("qso", 2)

    for m in range(16):
        t0_ = m * 64
        if m >= 1:
            p.wait("te", "sbv", p.cnt.get("sbv", 0))
        for k in range(NKD):
            p.op("te", lambda eng, k=k, t0_=t0_: eng.matmul(
                pb[2][0:64, 0:16], xln[:, k * TPC + t0_:k * TPC + t0_ + 64],
                wf_sb[:, 16 * k:16 * k + 16], start=(k == 0), stop=(k == NKD - 1)),
                inc=("ldp", 1) if k == NKD - 1 else None)
        p.wait("sc", "ldp", p.cnt.get("ldp", 0))
        if m >= 1:
            p.wait("sc", "cumv", p.cnt.get("cumv", 0))
        p.op("sc", lambda eng: eng.activation(ldsc[:, 0:16], pb[2][0:64, 0:16], AF.Abs),
             inc=("lsg", 1))
        p.wait("sc", "lsg", p.cnt.get("lsg", 0))
        p.rop("sc", lambda eng: eng.activation(ldsc[:, 16:32], ldsc[:, 0:16], AF.Exp, scale=-1.0))
        p.chain("sc")
        p.op("sc", lambda eng: eng.activation(ldsc[:, 16:32], ldsc[:, 16:32], AF.Ln, bias=1.0),
             inc=("lsg2", 1))
        p.wait("ve", "lsg2", p.cnt.get("lsg2", 0))
        p.rop("ve", lambda eng: eng.tensor_scalar(ldsc[:, 32:48], pb[2][0:64, 0:16], 0.0, None, ALU.min))
        p.chain("ve")
        p.op("ve", lambda eng: eng.tensor_sub(ldc_[:], ldsc[:, 32:48], ldsc[:, 16:32]),
             inc=("ldv", 1))
        p.wait("te", "ldv", p.cnt.get("ldv", 0))
        p.op("te", lambda eng: eng.matmul(pb[2][0:64, 32:48], triu[:], ldc_[:],
                                          start=True, stop=True), inc=("cump", 1))
        p.wait("ve", "cump", p.cnt.get("cump", 0))
        p.op("ve", lambda eng: eng.tensor_copy(cumt[:], pb[2][0:64, 32:48]), inc=("cumv", 1))
        p.wait("te", "cumv", p.cnt.get("cumv", 0))
        p.op("te", lambda eng: eng.matmul(pb[2][0:16, 64:128], cumt[:], id64[:],
                                          start=True, stop=True), inc=("cum2", 1))
        p.wait("ve", "dmv", p.cnt.get("dmv", 0))
        p.op("ve", lambda eng: eng.tensor_scalar(negcum[:], cumt[:], -1.0, None, ALU.mult),
             inc=("negv", 1))
        p.wait("ve", "cum2", p.cnt.get("cum2", 0))
        p.op("ve", lambda eng: eng.tensor_copy(cumTs[:], pb[2][0:16, 64:128]), inc=("cum3", 1))
        p.wait("sy", "cum3", p.cnt.get("cum3", 0))
        p.dma("sy", cumd[m], cumTs[:], "cbA")
        p.wait("sy", "cbA", p.cnt["cbA"])
        if m >= 1:
            p.wait("sy", "bcp", p.cnt.get("bcp", 0))
            p.wait("sy", "ptv", p.cnt.get("ptv", 0))
            p.wait("sy", "erw", p.cnt.get("erw", 0))
            p.wait("sy", "totv", p.cnt.get("totv", 0))
        p.dma("sy", cumrow[0:1, :], cumd.rearrange("m h i -> m (h i)")[m:m + 1, :], "cbB")
        p.wait("sc", "cbB", p.cnt["cbB"])
        if m >= 1:
            p.wait("sc", "ecp", p.cnt.get("ecp", 0))
        p.op("sc", lambda eng: eng.activation(erow[0:1, :], cumrow[0:1, :], AF.Exp),
             inc=("erw", 1))
        p.wait("ve", "cbB", p.cnt["cbB"])
        if m >= 1:
            p.wait("ve", "bcp", p.cnt.get("bcp", 0))
        p.op("ve", lambda eng: eng.tensor_copy(
            totrow[:], cumrow[0:1, :].rearrange("o (h i) -> o h i", h=16)[0:1, :, 63]),
            inc=("totv", 1))
        p.wait("ve", "pdS", 16 * m)
        p.rop("ve", lambda eng: eng.tensor_copy(ptot[0:1, 16:32], ptot[0:1, 0:16]))
        p.chain("ve")
        p.wait("ve", "totv", p.cnt.get("totv", 0))
        p.op("ve", lambda eng: eng.tensor_tensor(
            ptot[0:1, 0:16], ptot[0:1, 0:16], totrow[:], ALU.add), inc=("ptv", 1))
        p.wait("sy", "ptv", p.cnt.get("ptv", 0))
        p.dma("sy", PD[m:m + 1, :], ptot[0:1, 16:32], "pdS")
        p.wait("te", "erw", p.cnt.get("erw", 0))
        p.op("te", lambda eng: eng.matmul(pb[0][:, 0:512], orw[0:1, :],
                                          erow[0:1, 0:512], start=True, stop=True))
        p.op("te", lambda eng: eng.matmul(pb[1][:, 0:512], orw[0:1, :],
                                          erow[0:1, 512:1024], start=True, stop=True),
             inc=("ecp", 1))
        p.wait("ve", "ecp", p.cnt.get("ecp", 0))
        p.rop("ve", lambda eng: eng.tensor_copy(ecb_all[:, 0:512], pb[0][:, 0:512]))
        p.op("ve", lambda eng: eng.tensor_copy(ecb_all[:, 512:1024], pb[1][:, 0:512]),
             inc=("ecv", 1))
        p.wait("ve", "ecv", p.cnt.get("ecv", 0))
        p.wait("ve", "opb", p.cnt.get("opb", 0))
        p.wait("ve", "qsos0", p.cnt.get("qsos0", 0))
        p.wait("ve", "qsos1", p.cnt.get("qsos1", 0))
        for h in range(16):
            p.op("ve", lambda eng, h=h, t0_=t0_: eng.tensor_tensor(
                qs_t[:, h * 64:(h + 1) * 64], q_sb[:, h * TPC + t0_:h * TPC + t0_ + 64],
                ecb_all[:, h * 64:(h + 1) * 64], ALU.mult),
                inc=("qsv", 1) if h == 15 else None)
        if m >= 1:
            p.wait("te", "ocv", p.cnt.get("ocv", 0))
        for h in range(16):
            dst = pb[4 + h // 8][0:64, 64 * (h % 8):64 * (h % 8) + 64]
            p.op("te", lambda eng, dst=dst, h=h, t0_=t0_: eng.matmul(
                dst, k_sb[:, h * TPC + t0_:h * TPC + t0_ + 64],
                q_sb[:, h * TPC + t0_:h * TPC + t0_ + 64], start=True, stop=True),
                inc=("scp", 1) if h == 15 else None)
        p.wait("te", "totv", p.cnt.get("totv", 0))
        p.op("te", lambda eng: eng.matmul(pb[6][0:64, 0:512], orw[0:1, 0:64],
                                          cumrow[0:1, 0:512], start=True, stop=True))
        p.op("te", lambda eng: eng.matmul(pb[7][0:64, 0:512], orw[0:1, 0:64],
                                          cumrow[0:1, 512:1024], start=True, stop=True))
        p.op("te", lambda eng: eng.matmul(pb[3][0:64, 0:16], orw[0:1, 0:64],
                                          totrow[0:1, :], start=True, stop=True))
        p.op("te", lambda eng: eng.matmul(pb[3][0:128, 16:32], orw[0:1, :],
                                          totrow[0:1, :], start=True, stop=True),
             inc=("bcp", 1))
        p.wait("sc", "bcp", p.cnt.get("bcp", 0))
        p.wait("sc", "negv", p.cnt.get("negv", 0))
        if m >= 1:
            p.wait("sc", "wtv", p.cnt.get("wtv", 0))
        for h in range(16):
            src = pb[6 + h // 8][0:64, 64 * (h % 8):64 * (h % 8) + 64]
            p.op("sc", lambda eng, src=src, h=h: eng.activation(
                Dm_all[:, h * 64:(h + 1) * 64], src, AF.Exp,
                bias=negcum[:, h:h + 1]), inc=("dmv", 1) if h == 15 else None)
        p.wait("ve", "bcp", p.cnt.get("bcp", 0))
        if m >= 1:
            p.wait("ve", "kdv", p.cnt.get("kdv", 0))
        p.op("ve", lambda eng: eng.tensor_sub(ecol_all[:], pb[3][0:64, 0:16], cumt[:]),
             inc=("eclv", 1))
        p.wait("sc", "eclv", p.cnt.get("eclv", 0))
        p.op("sc", lambda eng: eng.activation(ecol_all[:], ecol_all[:], AF.Exp),
             inc=("ecls", 1))
        if m >= 1:
            p.wait("sc", "supv", p.cnt.get("supv", 0))
        p.op("sc", lambda eng: eng.activation(etb[:], pb[3][0:128, 16:32], AF.Exp),
             inc=("etbs", 1))
        p.wait("ve", "scp", p.cnt.get("scp", 0))
        p.wait("ve", "dmv", p.cnt.get("dmv", 0))
        p.rop("ve", lambda eng: eng.tensor_tensor(
            W_all[:, 0:512], pb[4][0:64, 0:512], Dm_all[:, 0:512], ALU.mult))
        p.rop("ve", lambda eng: eng.tensor_tensor(
            W_all[:, 512:1024], pb[5][0:64, 0:512], Dm_all[:, 512:1024], ALU.mult))
        p.chain("ve")
        p.rop("ve", lambda eng: eng.tensor_tensor(
            W_all[:, 0:512], W_all[:, 0:512], triu16[:, 0:512], ALU.mult))
        p.op("ve", lambda eng: eng.tensor_tensor(
            W_all[:, 512:1024], W_all[:, 512:1024], triu16[:, 512:1024], ALU.mult),
            inc=("wtv", 1))
        p.wait("te", "ecv", p.cnt.get("ecv", 0))
        p.wait("te", "cum3", p.cnt.get("cum3", 0))
        p.wait("te", "lsg", p.cnt.get("lsg", 0))
        p.wait("te", "eclv", p.cnt.get("eclv", 0))
        p.wait("te", "etbs", p.cnt.get("etbs", 0))
        for h in range(16):
            dst = pb[h // 4][0:64, 128 * (h % 4):128 * (h % 4) + 128]
            p.op("te", lambda eng, dst=dst, h=h, t0_=t0_: eng.matmul(
                dst, k_sb[:, h * TPC + t0_:h * TPC + t0_ + 64], idb[:],
                start=True, stop=True), inc=("ktp", 1) if h == 15 else None)
        p.wait("ve", "ktp", p.cnt.get("ktp", 0))
        p.wait("ve", "ecls", p.cnt.get("ecls", 0))
        for h in range(16):
            src = pb[h // 4][0:64, 128 * (h % 4):128 * (h % 4) + 128]
            p.op("ve", lambda eng, src=src, h=h: eng.tensor_scalar(
                kdt_all[:, h * 128:(h + 1) * 128], src, ecol_all[:, h:h + 1],
                None, ALU.mult), inc=("kdv", 1) if h == 15 else None)
        p.wait("te", "kdv", p.cnt.get("kdv", 0))
        for h in range(16):
            dst = pb[h // 4][0:64, 128 * (h % 4):128 * (h % 4) + 128]
            p.op("te", lambda eng, dst=dst, h=h, t0_=t0_: eng.matmul(
                dst, v_sb[:, h * TPC + t0_:h * TPC + t0_ + 64], idb[:],
                start=True, stop=True), inc=("vtp", 1) if h == 15 else None)
        p.wait("ve", "vtp", p.cnt.get("vtp", 0))
        for q in range(4):
            p.op("ve", lambda eng, q=q: eng.tensor_copy(
                vtt_all[:, q * 512:(q + 1) * 512], pb[q][0:64, 0:512]),
                inc=("vtv", 1) if q == 3 else None)
        osl = om.ready("ve")
        p.wait("te", "wtv", p.cnt.get("wtv", 0))
        p.wait("te", "qsv", p.cnt.get("qsv", 0))
        p.wait("te", "vtv", p.cnt.get("vtv", 0))
        for h in range(16):
            dstb = pb[4 + h // 4][0:64, 128 * (h % 4):128 * (h % 4) + 128]
            p.rop("te", lambda eng, dstb=dstb, h=h: eng.matmul(
                dstb, W_all[:, h * 64:(h + 1) * 64], vtt_all[:, h * 128:(h + 1) * 128],
                start=True, stop=False))
            p.op("te", lambda eng, dstb=dstb, h=h: eng.matmul(
                dstb, qs_t[:, h * 64:(h + 1) * 64], S_bf[:, h * 128:(h + 1) * 128],
                start=False, stop=True), inc=("opb", 1) if h == 15 else None)
        p.wait("ve", "opb", p.cnt.get("opb", 0))
        for q in range(4):
            p.op("ve", lambda eng, q=q, osl=osl: eng.tensor_copy(
                oc_m[:, osl * 2048 + q * 512:osl * 2048 + (q + 1) * 512],
                pb[4 + q][0:64, 0:512]), inc=("ocv", 1) if q == 3 else None)
        om.store("sy", OPART[t0_:t0_ + 64, :], oc_m[:, osl * 2048:(osl + 1) * 2048],
                 after=("ocv", p.cnt.get("ocv", 0)))
        qso.ready("sy")
        qso.store("sy", QS[m], qs_t[:], after=("qsv", p.cnt.get("qsv", 0)))
        for h in range(16):
            dstb = pb[h // 4][:, 128 * (h % 4):128 * (h % 4) + 128]
            p.op("te", lambda eng, dstb=dstb, h=h: eng.matmul(
                dstb, kdt_all[:, h * 128:(h + 1) * 128], vtt_all[:, h * 128:(h + 1) * 128],
                start=True, stop=True), inc=("sup", 1) if h == 15 else None)
        p.wait("ve", "sup", p.cnt.get("sup", 0))
        p.wait("ve", "etbs", p.cnt.get("etbs", 0))
        for h in range(16):
            src = pb[h // 4][:, 128 * (h % 4):128 * (h % 4) + 128]
            p.op("ve", lambda eng, src=src, h=h: eng.scalar_tensor_tensor(
                S_sb[:, h * 128:(h + 1) * 128], S_sb[:, h * 128:(h + 1) * 128],
                etb[:, h:h + 1], src, ALU.mult, ALU.add),
                inc=("supv", 1) if h == 15 else None)
        p.wait("ve", "supv", p.cnt.get("supv", 0))
        p.op("ve", lambda eng: eng.tensor_copy(S_bf[:], S_sb[:]), inc=("sbv", 1))
    p.wait("sy", "sbv", p.cnt["sbv"])
    for h in range(16):
        p.dma("sy", SLOC[h], S_sb[:, h * 128:(h + 1) * 128], "slst")
    p.wait("sy", "ptv", p.cnt["ptv"])
    p.dma("sy", TOTD[:], ptot[0:1, 0:16], "slst")
    p.finish("gp")
    p.finalize()
    return p


def build_la_post():
    p = Prog("lapost")
    OPART = p.din("OPART", [TPC, D], BF16)
    QS = p.din("QS", [16, 128, 1024], BF16)
    PD = p.din("PD", [16, 16], F32)
    SLA = p.din("SLA", [4, 16, 128, 128], F32)
    TOTA = p.din("TOTA", [4, 16], F32)
    MM = p.din("MM", [4, 4], F32)
    G4 = p.din("G4", [1, 4], F32)
    NORMW = p.din("NORMW", [1, D], F32)
    WOl = p.din("WOl", [16, 128, D], BF16)
    xT = p.din("xT", [D, TPC], F32)
    xoT = p.dout("xoT", [D, TPC], F32)
    wgtd = p.dtmp("wgtd", [16, 4], F32)

    oc = p.sb("oc", [128, 1], F32)
    orw = p.sb("orw", [1, 128], F32)
    idb = p.sb("idb", [128, 128], BF16)
    p.dma("sy", oc[:], p.inline(np.ones((128, 1), np.float32), "onesc")[:], "ldc")
    p.dma("sy", orw[:], p.inline(np.ones((1, 128), np.float32), "onesr")[:], "ldc")
    p.dma("sy", idb[:], p.inline(np.eye(128, dtype=BF), "idb_")[:], "ldc")
    p.consts([1e-6])
    tot_sb = p.sb("tot_sb", [4, 16], F32)
    mm_sb = p.sb("mm_sb", [4, 4], F32)
    g4_sb = p.sb("g4_sb", [1, 4], F32)
    pdrow = p.sb("pdrow", [1, 256], F32)
    p.dma("sy", tot_sb[:], TOTA[:], "ldc")
    p.dma("sy", mm_sb[:], MM[:], "ldc")
    p.dma("sy", g4_sb[:], G4[:], "ldc")
    p.dma("sy", pdrow[:], PD.rearrange("(o m) h -> o (m h)", o=1), "ldc")
    nww = p.sb("nww", [1, D], F32)
    p.dma("sy", nww[:], NORMW[:], "ldc")
    for e in ("te", "sc", "ve"):
        p.wait(e, "ldc", p.cnt["ldc"])

    lwt = p.sb("lwt", [16, 4], F32)
    gb16 = p.sb("gb16", [16, 4], F32)
    wrow = p.sb("wrow", [1, 64], F32)
    wb = p.sb("wb", [128, 64], F32)
    epd = p.sb("epd", [64, 256], F32)
    Sin = p.sb("Sin", [128, 16 * 128], F32)
    Sin_bf = p.sb("Sin_bf", [128, 16 * 128], BF16)
    slb = p.sb("slb", [128, 2 * 2048], F32)
    qsb = p.sb("qsb", [128, 2 * 1024], BF16)
    opb = p.sb("opb", [64, 2 * 2048], BF16)
    o_m = p.sb("o_m", [64, 2048], F32)
    on_bf = p.sb("on_bf", [64, 2048], BF16)
    sqf32 = p.sb("sqf32", [64, 2048], F32)
    rms = p.sb("rms", [64, 4], F32)
    normwb = p.sb("normwb", [128, D], F32)
    of_full = p.sb("of_full", [128, 16 * TPC], BF16)
    wt = [p.sb(f"wt{i}", [128, 16 * 128], BF16) for i in range(2)]
    xrb = p.sb("xrb", [128, 2 * 512], F32)
    yb = p.sb("yb", [128, 2 * 512], F32)

    pb = [p.ps(f"pb{i}", [128, 512]) for i in range(6)]
    pT2 = p.ps("pT2", [128, 128])
    pW = p.ps("pW", [16, 64])

    p.op("te", lambda eng: eng.matmul(pW[0:16, 0:4], tot_sb[:], mm_sb[:],
                                      start=True, stop=True), inc=("w1", 1))
    p.op("te", lambda eng: eng.matmul(pW[0:16, 4:8], orw[0:1, 0:16], g4_sb[0:1, :],
                                      start=True, stop=True), inc=("w1", 1))
    p.wait("ve", "w1", p.cnt["w1"])
    p.rop("ve", lambda eng: eng.tensor_copy(lwt[:], pW[0:16, 0:4]))
    p.op("ve", lambda eng: eng.tensor_copy(gb16[:], pW[0:16, 4:8]), inc=("w2", 1))
    p.wait("ve", "w2", p.cnt["w2"])
    p.wait("sc", "w2", p.cnt["w2"])
    p.op("sc", lambda eng: eng.activation(lwt[:], lwt[:], AF.Exp), inc=("w3", 1))
    p.wait("ve", "w3", p.cnt["w3"])
    p.op("ve", lambda eng: eng.tensor_tensor(lwt[:], lwt[:], gb16[:], ALU.mult),
         inc=("w4", 1))
    p.wait("sy", "w4", p.cnt["w4"])
    p.dma("sy", wgtd[:], lwt[:], "wb1")
    p.wait("sy", "wb1", p.cnt["wb1"])
    p.dma("sy", wrow[:], wgtd.rearrange("(o h) r -> o (h r)", o=1), "wb2")
    p.wait("te", "wb2", p.cnt["wb2"])
    p.op("te", lambda eng: eng.matmul(pT2[:, 0:64], orw[0:1, :], wrow[0:1, :],
                                      start=True, stop=True), inc=("w5", 1))
    p.wait("ve", "w5", p.cnt["w5"])
    p.op("ve", lambda eng: eng.tensor_copy(wb[:], pT2[:, 0:64]), inc=("w6", 1))
    p.wait("ve", "w6", p.cnt["w6"])
    p.op("te", lambda eng: eng.matmul(pb[0][0:64, 0:256], orw[0:1, 0:64], pdrow[0:1, :],
                                      start=True, stop=True), inc=("w7", 1))
    p.wait("sc", "w7", p.cnt["w7"])
    p.op("sc", lambda eng: eng.activation(epd[:], pb[0][0:64, 0:256], AF.Exp),
         inc=("w8", 1))
    p.wait("ve", "w8", p.cnt["w8"])
    for half in range(4):
        p.op("te", lambda eng, half=half: eng.matmul(
            pb[1][:, 0:512], orw[0:1, :], nww[0:1, half * 512:(half + 1) * 512],
            start=True, stop=True), inc=("w9", 1))
        p.wait("ve", "w9", p.cnt["w9"])
        p.op("ve", lambda eng, half=half: eng.tensor_copy(
            normwb[:, half * 512:(half + 1) * 512], pb[1][:, 0:512]), inc=("w10", 1))
        p.wait("te", "w10", p.cnt["w10"])

    sls = p.instream("sls", 2)
    p.rop("ve", lambda eng: eng.memset(Sin[:], 0.0))
    p.op("ve", lambda eng: eng.memset(Sin_bf[:], 0.0), inc=("sz", 1))
    p.wait("ve", "sz", p.cnt["sz"])
    for r in range(4):
        sl = sls.i % 2
        tok = sls.load("sy", [(slb[:, sl * 2048 + h * 128:sl * 2048 + (h + 1) * 128],
                               SLA[r, h]) for h in range(16)])
        sls.wait("ve", tok)
        for h in range(16):
            incs = [("siv", 1), sls.free(tok)] if h == 15 else None
            p.op("ve", lambda eng, h=h, r=r, sl=sl: eng.scalar_tensor_tensor(
                Sin[:, h * 128:(h + 1) * 128],
                slb[:, sl * 2048 + h * 128:sl * 2048 + (h + 1) * 128],
                wb[:, 4 * h + r:4 * h + r + 1],
                Sin[:, h * 128:(h + 1) * 128], ALU.mult, ALU.add), inc=incs)
        p.wait("ve", "siv", p.cnt["siv"])
    p.op("ve", lambda eng: eng.tensor_copy(Sin_bf[:], Sin[:]), inc=("sbv", 1))
    p.wait("te", "sbv", p.cnt["sbv"])

    qst = p.instream("qst", 2)
    ops = p.instream("ops", 2)
    p.wait("te", "w8", p.cnt["w8"])
    p.wait("te", "w10", p.cnt["w10"])
    for m in range(16):
        t0_ = m * 64
        sq = qst.i % 2
        tokq = qst.load("sy", (qsb[:, sq * 1024:(sq + 1) * 1024], QS[m]))
        so = ops.i % 2
        toko = ops.load("sy", (opb[:, so * 2048:(so + 1) * 2048], OPART[t0_:t0_ + 64, :]))
        qst.wait("te", tokq)
        if m >= 1:
            p.wait("te", "oev", p.cnt["oev"])
        for h in range(16):
            dst = pb[h // 4][0:64, 128 * (h % 4):128 * (h % 4) + 128]
            incs = [("oep", 1)] + ([qst.free(tokq)] if True else []) if h == 15 else None
            p.op("te", lambda eng, dst=dst, h=h, sq=sq: eng.matmul(
                dst, qsb[:, sq * 1024 + h * 64:sq * 1024 + (h + 1) * 64],
                Sin_bf[:, h * 128:(h + 1) * 128], start=True, stop=True), inc=incs)
        ops.wait("ve", toko)
        p.wait("ve", "oep", p.cnt["oep"])
        for h in range(16):
            src_ = pb[h // 4][0:64, 128 * (h % 4):128 * (h % 4) + 128]
            incs2 = [("oev", 1)] + ([ops.free(toko)] if True else []) if h == 15 else None
            p.op("ve", lambda eng, src_=src_, h=h, m=m, so=so: eng.scalar_tensor_tensor(
                o_m[:, h * 128:(h + 1) * 128], src_,
                epd[:, (16 * m + h):(16 * m + h) + 1],
                opb[:, so * 2048 + h * 128:so * 2048 + (h + 1) * 128],
                ALU.mult, ALU.add), inc=incs2)
        p.wait("sc", "oev", p.cnt["oev"])
        p.op("sc", lambda eng: eng.activation(sqf32[:], o_m[:], AF.Square),
             inc=("rq", 1))
        p.wait("ve", "rq", p.cnt["rq"])
        p.rop("ve", lambda eng: eng.reduce_sum(rms[:, 0:1], sqf32[:], axis=AX.X))
        p.chain("ve")
        p.op("ve", lambda eng: eng.tensor_scalar(rms[:, 1:2], rms[:, 0:1],
                                                 1.0 / D, None, ALU.mult), inc=("rr", 1))
        p.wait("sc", "rr", p.cnt["rr"])
        p.rop("sc", lambda eng: eng.activation(rms[:, 1:2], rms[:, 1:2], AF.Ln, bias=1e-6))
        p.chain("sc")
        p.op("sc", lambda eng: eng.activation(rms[:, 1:2], rms[:, 1:2], AF.Exp,
                                              scale=-0.5), inc=("rs", 1))
        p.wait("ve", "rs", p.cnt["rs"])
        p.rop("ve", lambda eng: eng.tensor_scalar(o_m[:], o_m[:], rms[:, 1:2],
                                                  None, ALU.mult))
        p.chain("ve")
        p.op("ve", lambda eng: eng.tensor_tensor(on_bf[:], o_m[:], normwb[0:64, :],
                                                 ALU.mult), inc=("onv", 1))
        p.wait("te", "onv", p.cnt["onv"])
        if m >= 1:
            p.wait("te", "otv", p.cnt["otv"])
        for cix in range(16):
            dst = pb[4 + cix // 8][:, 64 * (cix % 8):64 * (cix % 8) + 64]
            p.op("te", lambda eng, dst=dst, cix=cix: eng.matmul(
                dst, on_bf[:, cix * 128:(cix + 1) * 128], idb[0:64, 0:64],
                start=True, stop=True), inc=("otp", 1) if cix == 15 else None)
        p.wait("ve", "otp", p.cnt["otp"])
        for cix in range(16):
            src_ = pb[4 + cix // 8][:, 64 * (cix % 8):64 * (cix % 8) + 64]
            p.op("ve", lambda eng, src_=src_, cix=cix, t0_=t0_: eng.tensor_copy(
                of_full[:, cix * TPC + t0_:cix * TPC + t0_ + 64], src_),
                inc=("otv", 1) if cix == 15 else None)
    p.wait("te", "otv", p.cnt["otv"])
    p.wait("te", "oev", p.cnt["oev"])
    ctx = ProjCtx(p, wt, pb[0:4])
    xst = p.instream("xst3", 2)
    yo = p.outstream("yo3", 2)

    def evac_wo(mm_, half, ps_, pev):
        hs = half * 512
        slx = xst.i % 2
        tokx = xst.load("sy", (xrb[:, slx * 512:(slx + 1) * 512],
                               xT[mm_ * 128:(mm_ + 1) * 128, hs:hs + 512]))
        xst.wait("ve", tokx)
        p.wait("ve", ctx.pe, pev)
        ysl = yo.ready("ve")
        p.op("ve", lambda eng, ps_=ps_, slx=slx, ysl=ysl: eng.tensor_tensor(
            yb[:, ysl * 512:(ysl + 1) * 512], ps_[:],
            xrb[:, slx * 512:(slx + 1) * 512], ALU.add),
            inc=[(ctx.ev, 1), xst.free(tokx)])
        yo.store("sy", xoT[mm_ * 128:(mm_ + 1) * 128, hs:hs + 512],
                 yb[:, ysl * 512:(ysl + 1) * 512], after=(ctx.ev, p.cnt[ctx.ev]))

    emit_proj_fm(p, ctx, WOl, NKD, NKD,
                 lambda k, hs: of_full[:, k * TPC + hs:k * TPC + hs + 512], evac_wo)
    p.finish("gp")
    p.finalize()
    return p


_PROGS = {}


def _prog(name, builder):
    if name not in _PROGS:
        _PROGS[name] = builder()
    return _PROGS[name].nc


def _run(nc, ims):
    return run_bass_kernel_spmd(nc, ims, core_ids=list(range(NCORES))).results


def _rope_tabs(c):
    pp = np.arange(TPC, dtype=np.float64) + (c % 4) * TPC
    inv32 = 1.0 / 10000.0 ** (np.arange(32) / 32.0)
    ang = pp[:, None] * inv32[None, :]
    cs, sn = np.cos(ang).astype(np.float32), np.sin(ang).astype(np.float32)
    inv64 = 1.0 / 10000.0 ** (np.arange(0, 128, 2) / 128.0)
    ang64 = pp[:, None] * inv64[None, :]
    c64, s64 = np.cos(ang64).astype(np.float32), np.sin(ang64).astype(np.float32)
    return cs, sn, c64, s64


def _vis(c):
    v = np.zeros((1, 64), np.float32)
    c4 = c % 4
    for t in range(2):
        v[0, t * 32: t * 32 + 8 * c4 + 4 * t] = 1.0
    return v


def _attn_reshard(r1, kc_key, vt_key, kn_key, qn_key, mla):
    per_core = []
    seqdat = {}
    for sq in range(2):
        cores = range(sq * 4, sq * 4 + 4)
        if mla:
            KCF = np.concatenate([np.asarray(r1[cc][kc_key]) for cc in cores], axis=1)
            KN2F = np.ascontiguousarray(
                np.concatenate([np.asarray(r1[cc][kn_key]) for cc in cores], axis=1))
        else:
            KCF = np.concatenate([np.asarray(r1[cc][kc_key]) for cc in cores], axis=0).T
            KN2F = np.ascontiguousarray(
                np.concatenate([np.asarray(r1[cc][kn_key]) for cc in cores], axis=0).T)
        KCB = np.ascontiguousarray(KCF.reshape(D, 32, 128).transpose(1, 0, 2))
        VTB = np.ascontiguousarray(
            np.concatenate([np.asarray(r1[cc][vt_key]) for cc in cores], axis=0)
            .reshape(32, 128, D))
        seqdat[sq] = (KCB, VTB, KN2F)
    for c in range(NCORES):
        KCB, VTB, KN2F = seqdat[c // 4]
        c4 = c % 4
        d = dict(KCB=KCB, VTB=VTB,
                 KCD=np.ascontiguousarray(KCB[8 * c4:8 * c4 + 8]),
                 VTD=np.ascontiguousarray(VTB[8 * c4:8 * c4 + 8]),
                 KN2F=KN2F, VIS=_vis(c))
        if mla:
            d["QC"] = np.asarray(r1[c]["QC"])
            d["QN2"] = np.asarray(r1[c][qn_key])
        else:
            d["QC"] = np.ascontiguousarray(np.asarray(r1[c]["Q_TM"]).T)
            d["QN2"] = np.ascontiguousarray(np.asarray(r1[c][qn_key]).T)
        per_core.append(d)
    return per_core


def kernel(**inputs):
    nf = lambda k: np.asarray(inputs[k], np.float32)
    bf, f32c, lb = to_bf, f32c_, lineblock

    x = nf("L0_attn_x").reshape(8192, D)
    xTs = [np.ascontiguousarray(x[c * TPC:(c + 1) * TPC].T) for c in range(NCORES)]

    nc_ffn = _prog("ffn", build_ffn_prog)
    nc_mlap = _prog("mlapre", build_mla_pre)
    nc_amla = _prog("amla", lambda: build_attn(True))
    nc_mhap = _prog("mhapre", build_mha_pre)
    nc_amha = _prog("amha", lambda: build_attn(False))
    nc_post = _prog("mhapost", build_mha_post)
    nc_lap = _prog("lapre", build_la_pre)
    nc_lapo = _prog("lapost", build_la_post)

    ffn_sh = host_ffn_inputs(nf("L0_ffn_w_gate"), nf("L0_ffn_w_up"), nf("L0_ffn_w_down"))
    mla_sh = dict(
        WQD=lb(bf(nf("L0_attn_Wq_d"))), BQD=f32c(nf("L0_attn_bq_d"))[:, None],
        WQUC=lb(bf(nf("L0_attn_Wq_uc"))), BQUC=f32c(nf("L0_attn_bq_uc"))[:, None],
        WQUP=bf(nf("L0_attn_Wq_up")), BQUP=bf(nf("L0_attn_bq_up"))[None, :],
        WKVD=lb(bf(nf("L0_attn_Wkv_d"))), BKVD=f32c(nf("L0_attn_bkv_d"))[:, None],
        WKUC=lb(bf(nf("L0_attn_Wk_uc"))), BKUC=f32c(nf("L0_attn_bk_uc"))[:, None],
        WKUP=bf(nf("L0_attn_Wk_up")), BKUP=bf(nf("L0_attn_bk_up"))[None, :],
        WVU=bf(nf("L0_attn_Wv_u")), BVU=bf(nf("L0_attn_bv_u"))[None, :],
    )
    mha_sh = dict(
        WQ=bf(nf("L1_ffn_wq")), WK=bf(nf("L1_ffn_wk")), WV=bf(nf("L1_ffn_wv")),
        BQ=bf(nf("L1_ffn_bq"))[None, :], BK=bf(nf("L1_ffn_bk"))[None, :],
        BV=bf(nf("L1_ffn_bv"))[None, :],
        LN1W=f32c(nf("L1_ffn_ln1_w"))[:, None], LN1B=f32c(nf("L1_ffn_ln1_b"))[:, None],
    )
    post_sh = dict(
        WO=lb(bf(nf("L1_ffn_wo"))), BO=f32c(nf("L1_ffn_bo"))[:, None],
        LN2W=f32c(nf("L1_ffn_ln2_w"))[:, None], LN2B=f32c(nf("L1_ffn_ln2_b"))[:, None],
        WG4=bf(nf("L1_ffn_w_gate")),
        WE1=np.stack([lb(bf(nf("L1_ffn_we1")[e])) for e in range(4)]),
        BE1=f32c(nf("L1_ffn_be1")).reshape(-1, 1),
        WE2=np.stack([lb(bf(nf("L1_ffn_we2")[e])) for e in range(4)]),
        BE2=bf(nf("L1_ffn_be2")),
    )
    la_sh = dict(
        WQl=lb(bf(nf("L1_attn_Wq"))), WKl=lb(bf(nf("L1_attn_Wk"))),
        WVl=lb(bf(nf("L1_attn_Wv"))), WF=bf(nf("L1_attn_Wf")),
    )
    lapo_sh = dict(
        NORMW=f32c(nf("L1_attn_norm_w"))[None, :],
        WOl=lb(bf(nf("L1_attn_Wo"))),
    )

    tabs = [_rope_tabs(c) for c in range(NCORES)]

    def run_ffn(xTs_):
        r = _run(nc_ffn, [dict(xT=xTs_[c], **ffn_sh) for c in range(NCORES)])
        return [np.asarray(r[c]["xoT"]) for c in range(NCORES)]

    def run_s2(xTs_):
        ims = []
        for c in range(NCORES):
            cs, sn, _, _ = tabs[c]
            ims.append(dict(xT=xTs_[c], COS=cs, SIN=sn,
                            COSQ=np.ascontiguousarray(np.tile(cs, (1, 16))),
                            SINQ=np.ascontiguousarray(np.tile(sn, (1, 16))),
                            **mla_sh))
        r1 = _run(nc_mlap, ims)
        parts = _attn_reshard(r1, "KC", "VT", "KN2", "QN2", mla=True)
        ims2 = []
        for c in range(NCORES):
            sq = c // 4
            cores = range(sq * 4, sq * 4 + 4)
            KRFc = np.concatenate([np.asarray(r1[cc]["KR_TM"]) for cc in cores], axis=0).T
            KRFd = np.ascontiguousarray(np.tile(KRFc, (2, 1)).astype(BF))
            KRN2F = np.ascontiguousarray(
                np.concatenate([np.asarray(r1[cc]["KRN2T"]) for cc in cores], axis=0).T)
            c4 = c % 4
            d = parts[c]
            d.update(
                QRF=np.ascontiguousarray(np.asarray(r1[c]["QR_TM"]).T),
                QRN2=np.ascontiguousarray(np.asarray(r1[c]["QRN2T"]).T),
                KRF=KRFd, KRD=np.ascontiguousarray(KRFd[:, c4 * TPC:(c4 + 1) * TPC]),
                KRN2F=KRN2F, xT=xTs_[c],
            )
            ims2.append(d)
        r2 = _run(nc_amla, ims2)
        return [np.asarray(r2[c]["xoT"]) for c in range(NCORES)]

    def run_s4(xTs_):
        ims = []
        for c in range(NCORES):
            _, _, c64, s64 = tabs[c]
            ims.append(dict(xT=xTs_[c],
                            COSQ=np.ascontiguousarray(np.tile(c64, (1, 16))),
                            SINQ=np.ascontiguousarray(np.tile(s64, (1, 16))),
                            **mha_sh))
        r1 = _run(nc_mhap, ims)
        parts = _attn_reshard(r1, "K_TM", "VT", "KN2T", "QN2T", mla=False)
        ims2 = [dict(parts[c]) for c in range(NCORES)]
        r2 = _run(nc_amha, ims2)
        ims3 = []
        for c in range(NCORES):
            ims3.append(dict(
                OF=np.asarray(r2[c]["OF"]), H0=np.asarray(r1[c]["H0"]),
                xT=xTs_[c], **post_sh))
        r3 = _run(nc_post, ims3)
        return [np.asarray(r3[c]["xoT"]) for c in range(NCORES)]

    def run_s5(xTs_):
        r1 = _run(nc_lap, [dict(xT=xTs_[c], **la_sh) for c in range(NCORES)])
        ims2 = []
        for c in range(NCORES):
            sq = c // 4
            cores = list(range(sq * 4, sq * 4 + 4))
            SLA = np.ascontiguousarray(
                np.stack([np.asarray(r1[cc]["SLOC"]) for cc in cores]))
            TOTA = np.ascontiguousarray(
                np.concatenate([np.asarray(r1[cc]["TOTD"]) for cc in cores], axis=0))
            me = c % 4
            mm = np.zeros((4, 4), np.float32)
            g4 = np.zeros((1, 4), np.float32)
            for r_ in range(4):
                if r_ < me:
                    g4[0, r_] = 1
                    for rr in range(4):
                        if r_ < rr < me:
                            mm[r_, rr] = 1
            ims2.append(dict(
                OPART=np.asarray(r1[c]["OPART"]), QS=np.asarray(r1[c]["QS"]),
                PD=np.asarray(r1[c]["PD"]), SLA=SLA, TOTA=TOTA,
                MM=np.ascontiguousarray(mm.T), G4=g4, xT=xTs_[c], **lapo_sh))
        r2 = _run(nc_lapo, ims2)
        return [np.asarray(r2[c]["xoT"]) for c in range(NCORES)]

    xTs = run_ffn(xTs)
    xTs = run_s2(xTs)
    xTs = run_ffn(xTs)
    xTs = run_s4(xTs)
    xTs = run_s5(xTs)
    xTs = run_s4(xTs)

    out = np.concatenate([xTs[c].T for c in range(NCORES)], axis=0)
    return out.reshape(2, 4096, D).astype(np.float32)


f32c_ = f32c
